# version 13
# baseline (speedup 1.0000x reference)
import math
import numpy as np
import ml_dtypes
import concourse.bass as bass
import concourse.mybir as mybir
from concourse.bass_utils import run_bass_kernel_spmd
from concourse.replica_groups import maybe_share_collective_output_space

F32 = mybir.dt.float32
BF16 = mybir.dt.bfloat16
AF = mybir.ActivationFunctionType
ALU = mybir.AluOpType
AXX = mybir.AxisListType.X
BF = ml_dtypes.bfloat16

NCORES = 8
B, T, E = 2, 8192, 2048
H, D = 16, 128
INTER = 5632
R = B * T
RPC = R // NCORES
NIT = INTER // 128
ROPE_BASE = 1e6
ISD = 1.0 / math.sqrt(D)
EPS = 1e-5

ENGINES = ("sync", "scalar", "vector", "tensor", "gpsimd")


class Rec:

    def __init__(self):
        self.streams = {e: [] for e in ENGINES}
        self.cnt = {}
        self.nbar = 0

    def v(self, s):
        return self.cnt.get(s, 0)

    def emit(self, engine, op, *args, wait=(), inc=None, **kw):
        wait = tuple((s, n) for (s, n) in wait if n and n > 0)
        self.streams[engine].append([op, args, kw, wait, inc, None])
        if inc is not None:
            s, n = inc
            self.cnt[s] = self.cnt.get(s, 0) + n
            return self.cnt[s]
        return None

    def dma(self, q, out, in_, wait=(), **kw):
        sem = {"sync": "SQ", "scalar": "SC", "gpsimd": "GQ"}[q]
        pre = self.v(sem)
        v = self.emit(q, "dma_start", wait=wait, inc=(sem, 16),
                      out=out, in_=in_, **kw)
        self.streams[q][-1][5] = (sem, pre)
        return v

    def finalize(self):
        qsems = ("SQ", "SC", "GQ")
        waited = {s: set() for s in qsems}
        for e in ENGINES:
            for it in self.streams[e]:
                for (s, n) in it[3]:
                    if s in qsems:
                        waited[s].add(n)
        wsorted = {s: sorted(waited[s]) for s in qsems}
        import bisect
        for e in ENGINES:
            level = {s: 0 for s in qsems}
            for it in self.streams[e]:
                for (s, n) in it[3]:
                    if s in qsems and n > level[s]:
                        level[s] = n
                if it[5] is not None:
                    s, pre = it[5]
                    ws = wsorted[s]
                    i = bisect.bisect_right(ws, pre)
                    tgt = ws[i - 1] if i > 0 else 0
                    if tgt > level[s]:
                        it[3] = ((s, tgt),) + it[3]
                        level[s] = tgt

    def barrier(self):
        self.nbar += 1
        flush = {"sync": "SQ", "scalar": "SC", "gpsimd": "GQ"}
        own = {"scalar": "ACT", "vector": "DVE", "tensor": "PE"}
        for e in ENGINES:
            w = []
            if e in flush:
                w.append((flush[e], self.v(flush[e])))
            if e in own:
                s = own[e]
                for it in reversed(self.streams[e]):
                    if it[0] is None or it[0] == "nop":
                        continue
                    if it[4] is None:
                        it[4] = (s, 1)
                        self.cnt[s] = self.cnt.get(s, 0) + 1
                    break
                w.append((s, self.v(s)))
            self.emit(e, "nop", wait=tuple(w), inc=("BAR", 1))
        tgt = self.v("BAR")
        assert tgt == 5 * self.nbar
        for e in ENGINES:
            self.emit(e, None, wait=(("BAR", tgt),))


def build_program(debug_names=(), race_detect=True):
    nc = bass.Bass("TRN2", target_bir_lowering=False, debug=False,
                   detect_race_conditions=race_detect)

    def din(name, shape, dt=F32):
        return nc.dram_tensor(name, list(shape), dt, kind="ExternalInput").ap()

    x_own = din("x_own", (RPC, E))
    wq_h = din("wq_h", (128, 4096))
    wk_h = din("wk_h", (128, 4096))
    wv_h = din("wv_h", (128, 4096))
    wo_h = din("wo_h", (128, 4096))
    gate_h = din("gate_h", (NIT, 128, 2048))
    up_h = din("up_h", (NIT, 128, 2048))
    down_h = din("down_h", (16, 128, NIT * 128))
    gate_s2 = din("gate_s2", (128, NIT))
    up_s2 = din("up_s2", (128, NIT))
    down_s2 = din("down_s2", (128, 16))
    cos_t = din("cos_t", (128, T))
    sin_t = din("sin_t", (128, T))
    masks = din("masks", (128, 2048), BF16)
    protT = din("protT", (128, 128), BF16)
    ident = din("ident", (128, 128), BF16)
    ones_col = din("ones_col", (128, 1), BF16)
    ones128 = din("ones128", (128, 128), BF16)
    ones_row_f = din("ones_row_f", (1, 128), F32)

    out_t = nc.dram_tensor("out", [RPC, E], F32, kind="ExternalOutput").ap()

    rg = [list(range(NCORES))]
    ag_space = maybe_share_collective_output_space("AllGather", rg)
    rs_space = maybe_share_collective_output_space("ReduceScatter", rg)
    ag1inA = nc.dram_tensor("ag1inA", [E, RPC // 2], BF16)
    ag1inB = nc.dram_tensor("ag1inB", [E, RPC // 2], BF16)
    ag1outA = nc.dram_tensor("ag1outA", [R, RPC // 2], BF16,
                             addr_space=ag_space)
    ag1outB = nc.dram_tensor("ag1outB", [R, RPC // 2], BF16,
                             addr_space=ag_space)
    qs = nc.dram_tensor("qs", [256, R], BF16)
    ks = nc.dram_tensor("ks", [256, R], BF16)
    vs = nc.dram_tensor("vs", [2, 2, 128, T], BF16)
    spart = nc.dram_tensor("spart", [R, E], BF16)
    s_own = nc.dram_tensor("s_own", [RPC, E], BF16, addr_space=rs_space)
    gsc = nc.dram_tensor("gsc", [RPC, E], BF16)
    gwb = nc.dram_tensor("gwb", [NIT, 128, 2048], BF16)
    uwb = nc.dram_tensor("uwb", [NIT, 128, 2048], BF16)
    dwb = nc.dram_tensor("dwb", [16, 128, NIT * 128], BF16)

    dbg_src = {"ag1in": ag1inA, "ag1out": ag1outA, "qs": qs, "ks": ks,
               "vs": vs, "spart": spart, "s_own": s_own, "gsc": gsc}
    dbg_outs = {}
    for nm in debug_names:
        if nm == "norm0":
            dbg_outs["norm0_sq2"] = nc.dram_tensor(
                "dbg_norm0_sq2", [128, 512], BF16, kind="ExternalOutput").ap()
            dbg_outs["norm0_row"] = nc.dram_tensor(
                "dbg_norm0_row", [1, 512], F32, kind="ExternalOutput").ap()
            dbg_outs["norm0_qrt"] = nc.dram_tensor(
                "dbg_norm0_qrt", [128, 1024], BF16, kind="ExternalOutput").ap()
            continue
        if nm == "bias4":
            dbg_outs["bias4"] = nc.dram_tensor(
                "dbg_bias4", [128, 4], F32, kind="ExternalOutput").ap()
            dbg_outs["runmax"] = nc.dram_tensor(
                "dbg_runmax", [1, 16], F32, kind="ExternalOutput").ap()
            dbg_outs["negm"] = nc.dram_tensor(
                "dbg_negm", [1, 8], F32, kind="ExternalOutput").ap()
            continue
        h = dbg_src[nm]
        dbg_outs[nm] = nc.dram_tensor(
            "dbg_" + nm, list(h.shape), h.dtype, kind="ExternalOutput").ap()

    r = Rec()

    import contextlib
    with contextlib.ExitStack() as ctx:
        e = ctx.enter_context
        PBF = e(nc.sbuf_tensor("PBF", [128, 77824], BF16))
        PF = e(nc.sbuf_tensor("PF", [128, 8192], F32))
        mask_sb = e(nc.sbuf_tensor("mask_sb", [128, 2048], BF16))
        protT_sb = e(nc.sbuf_tensor("protT_sb", [128, 128], BF16))
        ident_sb = e(nc.sbuf_tensor("ident_sb", [128, 128], BF16))
        ones_col_sb = e(nc.sbuf_tensor("ones_col_sb", [128, 1], BF16))
        ones128_sb = e(nc.sbuf_tensor("ones128_sb", [128, 128], BF16))
        ones_row_f_sb = e(nc.sbuf_tensor("ones_row_f_sb", [1, 128], F32))
        gs2_sb = e(nc.sbuf_tensor("gs2_sb", [128, NIT], F32))
        us2_sb = e(nc.sbuf_tensor("us2_sb", [128, NIT], F32))
        ds2_sb = e(nc.sbuf_tensor("ds2_sb", [128, 16], F32))
        wosl_sb = e(nc.sbuf_tensor("wosl_sb", [128, 4096], BF16))
        stats = e(nc.sbuf_tensor("stats", [128, 16], F32))
        runmax2 = e(nc.sbuf_tensor("runmax2", [128, 16], F32))
        stats2m = e(nc.sbuf_tensor("stats2m", [128, 16], F32))
        runmax = e(nc.sbuf_tensor("runmax", [1, 16], F32))
        negm = e(nc.sbuf_tensor("negm", [1, 8], F32))
        bias4 = e(nc.sbuf_tensor("bias4", [128, 4], F32))
        rsum = e(nc.sbuf_tensor("rsum", [1, 512], F32))
        PSF = e(nc.psum_tensor("PSF", [128, 4096], F32))
        PTA = PSF[:, 3072:3584].bitcast(BF16)
        PTB = PSF[:, 3584:4096].bitcast(BF16)
        hSQ = e(nc.semaphore("SQ"))
        hSC = e(nc.semaphore("SC"))
        hGQ = e(nc.semaphore("GQ"))
        hPE = e(nc.semaphore("PE"))
        hACT = e(nc.semaphore("ACT"))
        hDVE = e(nc.semaphore("DVE"))
        hCC1 = e(nc.semaphore("CC1"))
        hCC2 = e(nc.semaphore("CC2"))
        hCC3 = e(nc.semaphore("CC3"))
        hBAR = e(nc.semaphore("BAR"))
        block = e(nc.Block())
        sems = {"SQ": hSQ, "SC": hSC, "GQ": hGQ, "PE": hPE, "ACT": hACT,
                "DVE": hDVE, "CC1": hCC1, "CC2": hCC2, "CC3": hCC3, "BAR": hBAR}

        def bank(i, n=512):
            return PSF[:, 512 * i:512 * i + n]

        def dap(t, off, dims):
            return bass.AP(t, off, [list(d) for d in dims])

        r.dma("sync", mask_sb[:, :], masks)
        r.dma("sync", protT_sb[:, :], protT)
        r.dma("sync", ident_sb[:, :], ident)
        r.dma("sync", ones_col_sb[:, :], ones_col)
        r.dma("sync", ones128_sb[:, :], ones128)
        r.dma("sync", ones_row_f_sb[:, :], ones_row_f)
        r.dma("sync", gs2_sb[:, :], gate_s2)
        r.dma("sync", us2_sb[:, :], up_s2)
        r.dma("sync", ds2_sb[:, :], down_s2)
        p0 = r.v("SQ")
        r.emit("vector", "memset", runmax[0:1, :], 0.0, inc=("DVE", 1))

        ln_done = [0] * 16
        tr_done = [0] * 16
        ev_done = [[0, 0] for _ in range(16)]
        st_done = [0] * 16
        red_done = [0] * 16
        sq_done = [0] * 16
        for i in range(16):
            p = i % 2
            xt = PF[:, p * 2048:(p + 1) * 2048]
            xsq = PF[:, 4096 + p * 2048:4096 + (p + 1) * 2048]
            w = [("SQ", p0)] if i == 0 else []
            if i >= 2:
                w += [("ACT", sq_done[i - 2]), ("DVE", red_done[i - 2])]
            r.dma("sync", xt, x_own[128 * i:128 * (i + 1), :], wait=w)
            ld = r.v("SQ")
            r.emit("vector", "reduce_sum", stats[:, i:i + 1], xt, AXX,
                   wait=(("SQ", ld),), inc=("DVE", 1))
            r.emit("scalar", "activation", xsq, xt, AF.Square,
                   wait=(("SQ", ld),), inc=("ACT", 1))
            sq_done[i] = r.v("ACT")
            r.emit("vector", "reduce_sum", runmax2[:, i:i + 1], xsq, AXX,
                   wait=(("ACT", sq_done[i]),), inc=("DVE", 1))
            red_done[i] = r.v("DVE")
        r.emit("vector", "tensor_scalar_mul", stats[:, 0:16],
               stats[:, 0:16], 1.0 / E, wait=(("DVE", red_done[15]),),
               inc=("DVE", 1))
        c1 = r.v("DVE")
        r.emit("vector", "tensor_scalar_mul", runmax2[:, 0:16],
               runmax2[:, 0:16], 1.0 / E, wait=(("DVE", c1),),
               inc=("DVE", 1))
        c2 = r.v("DVE")
        r.emit("vector", "tensor_tensor", stats2m[:, 0:16],
               stats[:, 0:16], stats[:, 0:16], ALU.mult,
               wait=(("DVE", c2),), inc=("DVE", 1))
        c3 = r.v("DVE")
        r.emit("vector", "tensor_sub", stats2m[:, 0:16],
               runmax2[:, 0:16], stats2m[:, 0:16],
               wait=(("DVE", c3),), inc=("DVE", 1))
        c4 = r.v("DVE")
        r.emit("vector", "tensor_scalar_add", stats2m[:, 0:16],
               stats2m[:, 0:16], EPS, wait=(("DVE", c4),), inc=("DVE", 1))
        c5 = r.v("DVE")
        r.emit("scalar", "activation", stats2m[:, 0:16], stats2m[:, 0:16],
               AF.Ln, wait=(("DVE", c5),))
        r.emit("scalar", "activation", stats2m[:, 0:16], stats2m[:, 0:16],
               AF.Exp, scale=-0.5, inc=("ACT", 1))
        rstd_all = r.v("ACT")
        r.emit("vector", "tensor_scalar_mul", stats[:, 0:16],
               stats[:, 0:16], -1.0, wait=(("DVE", c5),), inc=("DVE", 1))
        negm_all = r.v("DVE")
        for i in range(16):
            p = i % 2
            xt = PF[:, p * 2048:(p + 1) * 2048]
            xnv = PBF[:, 36864 + p * 2048:36864 + (p + 1) * 2048]
            stg = PBF[:, 40960 + p * 2048:40960 + (p + 1) * 2048]
            w = []
            if i < 2:
                w.append(("DVE", negm_all))
            else:
                w.append(("DVE", ln_done[i - 2]))
            r.dma("sync", xt, x_own[128 * i:128 * (i + 1), :], wait=w)
            ld = r.v("SQ")
            w = [("SQ", ld), ("ACT", rstd_all), ("DVE", negm_all)]
            if i >= 2:
                w.append(("PE", tr_done[i - 2]))
            r.emit("vector", "tensor_scalar", xnv, xt,
                   stats[:, i:i + 1], stats2m[:, i:i + 1],
                   ALU.add, ALU.mult, wait=tuple(w), inc=("DVE", 1))
            ln_done[i] = r.v("DVE")
            for half, pt in enumerate((PTA, PTB)):
                w = [("DVE", ln_done[i])]
                if i >= 1:
                    w.append(("DVE", ev_done[i - 1][half]))
                for j in range(8):
                    et = half * 8 + j
                    r.emit("tensor", "transpose",
                           pt[:, 128 * j:128 * (j + 1)],
                           xnv[:, 128 * et:128 * (et + 1)],
                           ident_sb[:, :],
                           wait=tuple(w) if j == 0 else (),
                           inc=(("PE", 1) if j == 7 else None))
                trh = r.v("PE")
                w2 = [("PE", trh)]
                if i >= 2 and half == 0:
                    w2.append(("SC", st_done[i - 2]))
                r.emit("vector", "tensor_copy",
                       stg[:, half * 1024:(half + 1) * 1024], pt[:, :],
                       wait=tuple(w2), inc=("DVE", 1))
                ev_done[i][half] = r.v("DVE")
            tr_done[i] = r.v("PE")
            agt = ag1inA if i < 8 else ag1inB
            dst = dap(agt, 128 * (i % 8),
                      [(RPC // 2, 128), (128 * RPC // 2, 16), (1, 128)])
            r.dma("scalar", dst, stg, wait=(("DVE", ev_done[i][1]),))
            st_done[i] = r.v("SC")

        r.emit("gpsimd", "collective_compute", "AllGather", ALU.bypass,
               replica_groups=[list(range(NCORES))],
               ins=[ag1inA.ap().opt()], outs=[ag1outA.ap().opt()],
               wait=(("SC", st_done[7]),), inc=("CC1", 1))
        r.emit("gpsimd", "collective_compute", "AllGather", ALU.bypass,
               replica_groups=[list(range(NCORES))],
               ins=[ag1inB.ap().opt()], outs=[ag1outB.ap().opt()],
               wait=(("SC", st_done[15]),), inc=("CC3", 1))
        r.dma("gpsimd", PBF[:, 16384:20480], wq_h)
        r.dma("gpsimd", PBF[:, 20480:24576], wk_h)
        r.dma("gpsimd", PBF[:, 24576:28672], wv_h)
        r.dma("gpsimd", wosl_sb[:, :], wo_h)
        gqw = r.v("GQ")

        qraw = PBF[:, 28672:29696]
        kraw = PBF[:, 29696:30720]
        qrt = PBF[:, 30720:31744]
        krt = PBF[:, 31744:32768]
        sq2 = PBF[:, 32768:33280]
        vtile = PBF[:, 33280:34304]
        pe_xnb = [0] * 32
        dve_rope = [0] * 32
        sc_pb = [0] * 32
        prev_v_evac = 0
        prev_ropeq = 0
        prev_ropek = 0
        prev_norm_mm = 0
        prev_norm_red = 0
        tb_order = ([tb for tb in range(32) if tb % 4 < 2]
                    + [tb for tb in range(32) if tb % 4 >= 2])
        for ti, tb in enumerate(tb_order):
            par = ti % 2
            b = tb // 16
            R0 = 512 * tb
            tbase = R0 % T
            blk = tb // 4
            tl0 = (tb % 4) * 512
            half_b = tl0 >= 1024
            xnb = PBF[:, par * 8192:(par + 1) * 8192]
            ct = PF[:, par * 512:(par + 1) * 512]
            st = PF[:, 1024 + par * 512:1024 + (par + 1) * 512]
            tmp1 = PF[:, 2048:2560]
            tmp2 = PF[:, 2560:3072]
            w = []
            if ti == 0:
                w.append(("CC1", 1))
            if ti == 16:
                w.append(("CC3", 1))
            if ti >= 2:
                w.append(("PE", pe_xnb[ti - 2]))
            agsrc = ag1outB if half_b else ag1outA
            src = dap(agsrc, blk * 2048 * (RPC // 2) + (tl0 % 1024),
                      [(RPC // 2, 128), (128 * RPC // 2, 16), (1, 512)])
            r.dma("sync", xnb, src, wait=w)
            w = [("DVE", dve_rope[ti - 2])] if ti >= 2 else [
                ("DVE", ln_done[15])]
            r.dma("sync", ct, cos_t[:, tbase:tbase + 512], wait=w)
            r.dma("sync", st, sin_t[:, tbase:tbase + 512])
            ldb = r.v("SQ")
            w = [("SQ", ldb), ("GQ", gqw), ("DVE", prev_ropek)]
            for fl in range(2):
                for et in range(16):
                    r.emit("tensor", "matmul", bank(fl),
                           PBF[:, 16384 + et * 256 + fl * 128:
                               16384 + et * 256 + fl * 128 + 128],
                           xnb[:, et * 512:(et + 1) * 512],
                           start=(et == 0), stop=(et == 15),
                           wait=tuple(w) if (fl == 0 and et == 0) else (),
                           inc=(("PE", 1) if (fl == 1 and et == 15) else None))
            qmm = r.v("PE")
            w = [("ACT", prev_v_evac)]
            for fl in range(2):
                for et in range(16):
                    r.emit("tensor", "matmul", bank(4 + fl),
                           PBF[:, 20480 + et * 256 + fl * 128:
                               20480 + et * 256 + fl * 128 + 128],
                           xnb[:, et * 512:(et + 1) * 512],
                           start=(et == 0), stop=(et == 15),
                           wait=tuple(w) if (fl == 0 and et == 0) else (),
                           inc=(("PE", 1) if (fl == 1 and et == 15) else None))
            kmm = r.v("PE")
            w = [("PE", qmm), ("DVE", prev_ropeq)]
            r.emit("scalar", "activation", qraw, PSF[:, 0:1024], AF.Copy,
                   scale=ISD, wait=tuple(w), inc=("ACT", 1))
            q_evac = r.v("ACT")
            w = [("ACT", q_evac), ("DVE", prev_norm_red)]
            for fl in range(2):
                r.emit("tensor", "matmul", bank(2 + fl), protT_sb[:, :],
                       qraw[:, fl * 512:(fl + 1) * 512],
                       start=True, stop=True,
                       wait=tuple(w) if fl == 0 else (),
                       inc=(("PE", 1) if fl == 1 else None))
            rotq_mm = r.v("PE")
            w = [("PE", kmm), ("DVE", prev_ropek)]
            r.emit("scalar", "activation", kraw, PSF[:, 2048:3072], AF.Copy,
                   wait=tuple(w), inc=("ACT", 1))
            k_evac = r.v("ACT")
            w = [("PE", rotq_mm), ("SQ", ldb),
                 ("SC", sc_pb[ti - 1] if ti else 0)]
            for fl in range(2):
                r.emit("vector", "tensor_tensor", tmp1,
                       qraw[:, fl * 512:(fl + 1) * 512], ct, ALU.mult,
                       wait=tuple(w) if fl == 0 else ())
                r.emit("vector", "tensor_tensor", tmp2, bank(2 + fl), st,
                       ALU.mult)
                r.emit("vector", "tensor_add",
                       qrt[:, fl * 512:(fl + 1) * 512], tmp1, tmp2,
                       inc=("DVE", 1))
            ropeq = r.v("DVE")
            prev_ropeq = ropeq
            w = [("ACT", k_evac)]
            for fl in range(2):
                r.emit("tensor", "matmul", bank(fl), protT_sb[:, :],
                       kraw[:, fl * 512:(fl + 1) * 512],
                       start=True, stop=True,
                       wait=tuple(w) if fl == 0 else (),
                       inc=(("PE", 1) if fl == 1 else None))
            rotk_mm = r.v("PE")
            w = [("ACT", k_evac)]
            for tt in range(4):
                bk = PSF[:, 2048 + tt * 256:2048 + (tt + 1) * 256]
                for et in range(16):
                    r.emit("tensor", "matmul", bk,
                           xnb[:, et * 512 + tt * 128:
                               et * 512 + tt * 128 + 128],
                           PBF[:, 24576 + et * 256:24576 + (et + 1) * 256],
                           start=(et == 0), stop=(et == 15),
                           wait=tuple(w) if (tt == 0 and et == 0) else (),
                           inc=(("PE", 1) if (tt == 3 and et == 15) else None))
            vmm = r.v("PE")
            pe_xnb[ti] = vmm
            w = [("PE", rotk_mm)]
            for fl in range(2):
                r.emit("vector", "tensor_tensor", tmp1,
                       kraw[:, fl * 512:(fl + 1) * 512], ct, ALU.mult,
                       wait=tuple(w) if fl == 0 else ())
                r.emit("vector", "tensor_tensor", tmp2, bank(fl), st,
                       ALU.mult)
                r.emit("vector", "tensor_add",
                       krt[:, fl * 512:(fl + 1) * 512], tmp1, tmp2,
                       inc=("DVE", 1))
            ropek = r.v("DVE")
            prev_ropek = ropek
            dve_rope[ti] = ropek
            w = [("PE", vmm), ("SC", sc_pb[ti - 1] if ti else 0)]
            r.emit("scalar", "activation", vtile, PSF[:, 2048:3072], AF.Copy,
                   wait=tuple(w), inc=("ACT", 1))
            prev_v_evac = r.v("ACT")
            for pi, (srcv, slot0) in enumerate(((qrt, 0), (krt, 4))):
                for fl in range(2):
                    slot = slot0 + b * 2 + fl
                    w = [("PE", prev_norm_mm)]
                    r.emit("vector", "tensor_tensor", sq2,
                           srcv[:, fl * 512:(fl + 1) * 512],
                           srcv[:, fl * 512:(fl + 1) * 512], ALU.mult,
                           wait=tuple(w), inc=("DVE", 1))
                    sqd2 = r.v("DVE")
                    r.emit("tensor", "matmul", PSF[0:1, 1024:1536],
                           ones_col_sb[:, :], sq2, start=True, stop=True,
                           wait=(("DVE", sqd2),), inc=("PE", 1))
                    prev_norm_mm = r.v("PE")
                    if tb == 0 and pi == 0 and fl == 0 and "norm0" in debug_names:
                        r.emit("vector", "tensor_copy", PBF[:, 73728:74240],
                               sq2)
                        r.emit("vector", "tensor_copy", PBF[:, 74240:75264],
                               qrt)
                        r.emit("vector", "tensor_copy", PF[0:1, 3072:3584],
                               PSF[0:1, 1024:1536],
                               wait=(("PE", prev_norm_mm),), inc=("DVE", 1))
                    r.emit("vector", "reduce_max", stats[0:1, 15:16],
                           PSF[0:1, 1024:1536], AXX,
                           wait=(("PE", prev_norm_mm),
                                 ("DVE", prev_norm_red)),
                           inc=("DVE", 1))
                    rmx = r.v("DVE")
                    r.emit("vector", "tensor_max",
                           runmax[0:1, slot:slot + 1],
                           runmax[0:1, slot:slot + 1], stats[0:1, 15:16],
                           wait=(("DVE", rmx),), inc=("DVE", 1))
                    prev_norm_red = r.v("DVE")
            dstq = dap(qs, R0, [(R, 128), (128 * R, 2), (1, 512)])
            r.dma("scalar", dstq, qrt, wait=(("DVE", ropeq),))
            dstk = dap(ks, R0, [(R, 128), (128 * R, 2), (1, 512)])
            r.dma("scalar", dstk, krt, wait=(("DVE", ropek),))
            ktg0 = (R0 % T) // 128
            for hl in range(2):
                dstv = dap(vs, (b * 2 + hl) * 128 * T + ktg0 * 128,
                           [(T, 128), (128, 4), (1, 128)])
                srcv2 = bass.AP(PBF, 33280 + hl * 128,
                                [[77824, 128], [256, 4], [1, 128]])
                r.dma("scalar", dstv, srcv2,
                      wait=(("ACT", prev_v_evac),) if hl == 0 else ())
            sc_pb[ti] = r.v("SC")

        for u in range(4):
            bu, fl = u // 2, u % 2
            qslot = bu * 2 + fl
            kslot = 4 + bu * 2 + fl
            w = [("DVE", prev_norm_red)] if u == 0 else []
            r.emit("vector", "tensor_tensor", negm[0:1, u:u + 1],
                   runmax[0:1, qslot:qslot + 1],
                   runmax[0:1, kslot:kslot + 1],
                   ALU.mult, wait=tuple(w), inc=("DVE", 1))
            nd = r.v("DVE")
            r.emit("scalar", "activation", negm[0:1, u:u + 1],
                   negm[0:1, u:u + 1], AF.Ln, wait=(("DVE", nd),))
            r.emit("scalar", "activation", negm[0:1, u:u + 1],
                   negm[0:1, u:u + 1], AF.Exp, scale=0.5, inc=("ACT", 1))
            ad = r.v("ACT")
            r.emit("vector", "tensor_scalar", negm[0:1, u:u + 1],
                   negm[0:1, u:u + 1], -1.001, -0.01, ALU.mult, ALU.add,
                   wait=(("ACT", ad),), inc=("DVE", 1))
            nd2 = r.v("DVE")
            r.emit("tensor", "matmul", PSF[:, 1024:1025],
                   ones_row_f_sb[0:1, :], negm[0:1, u:u + 1],
                   start=True, stop=True, wait=(("DVE", nd2),),
                   inc=("PE", 1))
            pm = r.v("PE")
            r.emit("vector", "tensor_copy", bias4[:, u:u + 1],
                   PSF[:, 1024:1025], wait=(("PE", pm),), inc=("DVE", 1))

        if "bias4" in debug_names:
            bd = r.v("DVE")
            r.dma("scalar", dbg_outs["bias4"], bias4[:, :],
                  wait=(("DVE", bd),))
            r.dma("scalar", dbg_outs["runmax"], runmax[0:1, :])
            r.dma("scalar", dbg_outs["negm"], negm[0:1, :])
        if "norm0" in debug_names:
            bd = r.v("DVE")
            r.dma("scalar", dbg_outs["norm0_sq2"], PBF[:, 73728:74240],
                  wait=(("DVE", bd),))
            r.dma("scalar", dbg_outs["norm0_qrt"], PBF[:, 74240:75264])
            r.dma("scalar", dbg_outs["norm0_row"], PF[0:1, 3072:3584])

        r.barrier()

        QOF = (0, 40960)
        KOF = (8192, 24576)
        VOF = (16384, 32768)
        AOF = (49152, 57344)
        PTOF = (65536, 67584)
        units = [(0, 0), (0, 1), (1, 0), (1, 1)]
        ld_unit = [0] * 4
        pe_unit_end = [0] * 4
        pe_op_end = [0, 0]

        def att_loads(u, extra_wait=()):
            b, hl = units[u]
            par = u % 2
            r.dma("sync", PBF[:, QOF[par]:QOF[par] + 8192],
                  qs.ap()[hl * 128:(hl + 1) * 128, b * T:(b + 1) * T],
                  wait=extra_wait)
            r.dma("sync", PBF[:, KOF[par]:KOF[par] + 8192],
                  ks.ap()[hl * 128:(hl + 1) * 128, b * T:(b + 1) * T])
            r.dma("sync", PBF[:, VOF[par]:VOF[par] + 8192],
                  vs.ap()[b, hl, :, :])
            ld_unit[u] = r.v("SQ")

        cast_w = (("SQ", r.v("SQ") + 96),)
        for lo in (0, 22):
            r.dma("gpsimd", gwb.ap()[lo:lo + 22], gate_h[lo:lo + 22],
                  wait=cast_w)
            cast_w = ()
            r.dma("gpsimd", uwb.ap()[lo:lo + 22], up_h[lo:lo + 22])
        for lo in (0, 8):
            r.dma("gpsimd", dwb.ap()[lo:lo + 8], down_h[lo:lo + 8])
        gqcast = r.v("GQ")

        att_loads(0)
        att_loads(1)

        def emit_op_tt(b, tt):
            a0 = PBF[:, AOF[0]:AOF[0] + 8192]
            a1 = PBF[:, AOF[1]:AOF[1] + 8192]
            po = tt % 4
            outst = PBF[:, 69632 + po * 2048:69632 + (po + 1) * 2048]
            stq = "sync" if b == 0 else "scalar"
            stsem = "SQ" if b == 0 else "SC"
            for fb in range(4):
                bk = bank(fb % 2)
                w = [("DVE", dve_out_evac[fb % 2]),
                     ("ACT", act_out_evac[0])]
                if tt == 0 and fb == 0:
                    w.append(("DVE", ao_hist[0]))
                    w.append(("DVE", ao_hist[1]))
                    w.append(("GQ", gqw))
                    w.append(("ACT", exp_hist[0]))
                    w.append(("ACT", exp_hist[1]))
                    w.append(("DVE", set_free[0]))
                    w.append(("DVE", set_free[1]))
                for et, asrc in enumerate((a0, a1)):
                    r.emit("tensor", "matmul", bk,
                           asrc[:, tt * 128:(tt + 1) * 128],
                           wosl_sb[:, et * 2048 + fb * 512:
                                   et * 2048 + fb * 512 + 512],
                           start=(et == 0), stop=(et == 1),
                           wait=tuple(w) if et == 0 else (),
                           inc=(("PE", 1) if et == 1 else None))
                pmm = r.v("PE")
                w2 = [("PE", pmm)]
                if tt >= 4 and fb < 2:
                    w2.append((stsem, sc_sp_hist[b][(tt - 4) // 2]))
                if fb % 2 == 0:
                    r.emit("vector", "tensor_copy",
                           outst[:, fb * 512:(fb + 1) * 512], bk,
                           wait=tuple(w2), inc=("DVE", 1))
                    dve_out_evac[0] = r.v("DVE")
                else:
                    r.emit("scalar", "activation",
                           outst[:, fb * 512:(fb + 1) * 512], bk, AF.Copy,
                           wait=tuple(w2), inc=("ACT", 1))
                    act_out_evac[0] = r.v("ACT")
            if tt % 2 == 1:
                dst = dap(spart, (b * T + (tt - 1) * 128) * E,
                          [(E, 128), (128 * E, 2), (1, E)])
                so = 69632 + (tt - 1) % 4 * 2048
                r.dma(stq, dst, PBF[:, so:so + 4096],
                      wait=(("DVE", dve_out_evac[0]),
                            ("ACT", act_out_evac[0])))
                sc_sp_hist[b][tt // 2] = r.v(stsem)
        exp_hist = [0, 0]
        pvmm_hist = [0, 0]
        ao_hist = [0, 0]
        set_free = [0, 0]
        norm_ln_done = [0]
        sums_acc = PF[:, 2048:2560]
        pv_acc = PF[:, 2560:3072]
        dve_out_evac = [0, 0]
        act_out_evac = [0]
        sc_sp_hist = {0: [0] * 64, 1: [0] * 64}
        pair_idx = 0
        for u in range(4):
            b, hl = units[u]
            par = u % 2
            qu = PBF[:, QOF[par]:QOF[par] + 8192]
            ku = PBF[:, KOF[par]:KOF[par] + 8192]
            vu = PBF[:, VOF[par]:VOF[par] + 8192]
            ao = PBF[:, AOF[hl]:AOF[hl] + 8192]
            first_ao = True
            for qb in range(16):
                nq = qb + 1
                gate_ws = [None] * nq
                qpp = [0] * nq

                def emit_spv(g, pp):
                    S0 = PSF[:, pp * 2048:pp * 2048 + 512]
                    S1 = PSF[:, pp * 2048 + 512:pp * 2048 + 1024]
                    ptg = PBF[:, PTOF[pp]:PTOF[pp] + 2048]
                    w = [gate_ws[g]]
                    for j in range(4):
                        kb = 4 * g + j
                        r.emit("tensor", "matmul", S0,
                               ones128_sb[:, :],
                               ptg[:, j * 512:(j + 1) * 512],
                               start=(j == 0), stop=(j == 3),
                               wait=tuple(w) if j == 0 else ())
                        r.emit("tensor", "matmul", S1,
                               vu[:, kb * 128:(kb + 1) * 128],
                               ptg[:, j * 512:(j + 1) * 512],
                               start=(j == 0), stop=(j == 3),
                               inc=(("PE", 1) if j == 3 else None))
                    spv_pe = r.v("PE")
                    w2 = [("PE", spv_pe)]
                    if g == 0:
                        w2.append(("ACT", norm_ln_done[0]))
                        r.emit("vector", "tensor_copy", sums_acc, S0,
                               wait=tuple(w2))
                        r.emit("vector", "tensor_copy", pv_acc, S1,
                               inc=("DVE", 1))
                    else:
                        r.emit("vector", "tensor_add", sums_acc, S0,
                               sums_acc, wait=tuple(w2))
                        r.emit("vector", "tensor_add", pv_acc, S1,
                               pv_acc, inc=("DVE", 1))
                    set_free[pp] = r.v("DVE")
                    return spv_pe

                for g in range(nq):
                    pp = pair_idx % 2
                    pair_idx += 1
                    qpp[g] = pp
                    pt = PBF[:, PTOF[pp]:PTOF[pp] + 2048]
                    w = [("ACT", exp_hist[pp]), ("DVE", set_free[pp])]
                    if qb == 0 and g == 0:
                        w.append(("SQ", ld_unit[u]))
                        w.append(("DVE", dve_out_evac[0]))
                        w.append(("ACT", act_out_evac[0]))
                    for j in range(4):
                        kb = 4 * g + j
                        r.emit("tensor", "matmul",
                               PSF[:, pp * 2048 + j * 512:
                                   pp * 2048 + (j + 1) * 512],
                               ku[:, kb * 128:(kb + 1) * 128],
                               qu[:, qb * 512:(qb + 1) * 512],
                               start=True, stop=True,
                               wait=tuple(w) if j == 0 else (),
                               inc=(("PE", 1) if j == 3 else None))
                    smm = r.v("PE")
                    w = [("PE", smm), ("PE", pvmm_hist[pp])]
                    r.emit("scalar", "activation", pt,
                           PSF[:, pp * 2048:pp * 2048 + 2048],
                           AF.Exp, bias=bias4[:, u:u + 1], wait=tuple(w),
                           inc=("ACT", 1))
                    exp_hist[pp] = r.v("ACT")
                    gate_ws[g] = ("ACT", exp_hist[pp])
                    if g == qb:
                        r.emit("vector", "tensor_tensor", pt, pt,
                               mask_sb[:, :], ALU.mult,
                               wait=(("ACT", exp_hist[pp]),),
                               inc=("DVE", 1))
                        gate_ws[g] = ("DVE", r.v("DVE"))
                    if g > 0:
                        pvmm_hist[qpp[g - 1]] = emit_spv(g - 1, qpp[g - 1])
                pvmm_hist[qpp[nq - 1]] = emit_spv(nq - 1, qpp[nq - 1])
                qp = qb % 2
                lnt = PF[:, 512 * qp:512 * qp + 512]
                rec = PF[:, 1024 + 512 * qp:1024 + 512 * qp + 512]
                w = [("DVE", set_free[qpp[nq - 1]]), ("DVE", ao_hist[qp])]
                r.emit("scalar", "activation", lnt, sums_acc,
                       AF.Ln, wait=tuple(w), inc=("ACT", 1))
                norm_ln_done[0] = r.v("ACT")
                r.emit("scalar", "activation", rec, lnt, AF.Exp,
                       scale=-1.0, inc=("ACT", 1))
                recd = r.v("ACT")
                w = [("ACT", recd)]
                if first_ao and u >= 2:
                    w.append(("PE", pe_op_end[0]))
                first_ao = False
                r.emit("vector", "tensor_tensor",
                       ao[:, qb * 512:(qb + 1) * 512],
                       pv_acc, rec, ALU.mult,
                       wait=tuple(w), inc=("DVE", 1))
                ao_hist[qp] = r.v("DVE")
            pe_unit_end[u] = r.v("PE")
            if u + 2 < 4:
                att_loads(u + 2, extra_wait=(("PE", pe_unit_end[u]),))
            if hl == 1:
                for _tt in range(64):
                    emit_op_tt(b, _tt)
                pe_op_end[b] = r.v("PE")
                set_free[0] = max(set_free[0], dve_out_evac[0])
                exp_hist[0] = max(exp_hist[0], act_out_evac[0])

        r.emit("gpsimd", "collective_compute", "ReduceScatter", ALU.add,
               replica_groups=[list(range(NCORES))],
               ins=[spart.ap().opt()], outs=[s_own.ap().opt()],
               wait=(("SC", r.v("SC")),), inc=("CC2", 1))

        att_pe_end = r.v("PE")
        att_dve_end = r.v("DVE")

        hdn0 = 16384
        pe_gu = [0] * NIT
        act_silu = 0
        act_upev = 0
        dve_hdn = [0] * NIT
        pe_down_ht = [0] * 16
        pe_tr_g = [0, 0]
        dve_gtev = [0, 0]
        act_gt = [0, 0]
        sc_gp = [0, 0]
        pe_down_end = 0
        pe_gu_end = 0
        for th in range(2):
            w = []
            if th == 1:
                w.append(("PE", pe_down_end))
            else:
                w.append(("PE", pe_unit_end[2]))
            agt = ag1inA if th == 0 else ag1inB
            src = dap(agt, 0,
                      [(RPC // 2, 128), (128 * RPC // 2, 16), (1, 1024)])
            r.dma("sync", PBF[:, 0:16384], src, wait=w)
            xnh_ld = r.v("SQ")
            act_floor = r.v("ACT")
            for it in range(NIT):
                gpar = [2, 1, 0][it % 3]
                gw0 = 61440 + gpar * 4096
                if it >= 3:
                    w = [("PE", pe_gu[it - 3])]
                elif th == 1 and it == 0:
                    w = [("PE", pe_gu[42]), ("PE", pe_tr_g[0]),
                         ("PE", pe_tr_g[1]), ("SC", sc_gp[0]),
                         ("SC", sc_gp[1])]
                elif th == 1 and it == 1:
                    w = [("PE", pe_gu[43])]
                elif th == 1:
                    w = [("PE", pe_gu[41])]
                elif it == 0:
                    w = [("PE", att_pe_end), ("SC", sc_sp_hist[1][30]),
                         ("GQ", gqcast)]
                elif it == 1:
                    w = [("PE", pe_unit_end[3])]
                else:
                    w = [("PE", att_pe_end)]
                r.dma("sync", PBF[:, gw0:gw0 + 2048], gwb.ap()[it], wait=w)
                r.dma("sync", PBF[:, gw0 + 2048:gw0 + 4096], uwb.ap()[it])
                gld = r.v("SQ")
                w = [("SQ", gld), ("SQ", xnh_ld),
                     ("ACT", act_silu if it else act_floor)]
                for tj in range(2):
                    for ht in range(16):
                        r.emit("tensor", "matmul", bank(tj),
                               PBF[:, gw0 + ht * 128:gw0 + ht * 128 + 128],
                               PBF[:, ht * 1024 + tj * 512:
                                   ht * 1024 + tj * 512 + 512],
                               start=(ht == 0), stop=(ht == 15),
                               wait=tuple(w) if (tj == 0 and ht == 0) else (),
                               inc=(("PE", 1) if (tj == 1 and ht == 15)
                                    else None))
                gmm = r.v("PE")
                w = ([("DVE", dve_hdn[it - 1])] if it
                     else [("ACT", act_floor)])
                for tj in range(2):
                    for ht in range(16):
                        r.emit("tensor", "matmul", bank(2 + tj),
                               PBF[:, gw0 + 2048 + ht * 128:
                                   gw0 + 2048 + ht * 128 + 128],
                               PBF[:, ht * 1024 + tj * 512:
                                   ht * 1024 + tj * 512 + 512],
                               start=(ht == 0), stop=(ht == 15),
                               wait=tuple(w) if (tj == 0 and ht == 0) else (),
                               inc=(("PE", 1) if (tj == 1 and ht == 15)
                                    else None))
                umm = r.v("PE")
                pe_gu[it] = umm
                w = [("PE", gmm), ("DVE", dve_hdn[it - 1] if it
                                    else att_dve_end)]
                r.emit("scalar", "activation", PF[:, 0:1024], PSF[:, 0:1024],
                       AF.Silu, scale=gs2_sb[:, it:it + 1], wait=tuple(w),
                       inc=("ACT", 1))
                act_silu = r.v("ACT")
                w = [("PE", umm), ("ACT", act_silu)]
                if it == 0:
                    w.append(("PE", pe_down_end if th else att_pe_end))
                r.emit("vector", "scalar_tensor_tensor",
                       PBF[:, hdn0 + it * 1024:hdn0 + (it + 1) * 1024],
                       PSF[:, 1024:2048], us2_sb[:, it:it + 1],
                       PF[:, 0:1024], ALU.mult, ALU.mult,
                       wait=tuple(w), inc=("DVE", 1))
                dve_hdn[it] = r.v("DVE")
            pe_gu_end = r.v("PE")
            act_floor2 = r.v("ACT")
            for ht in range(16):
                dpar = ht % 2
                dof = dpar * 5632
                w = [("DVE", dve_hdn[NIT - 1])]
                if ht >= 2:
                    w.append(("PE", pe_down_ht[ht - 2]))
                else:
                    w.append(("PE", pe_gu_end))
                r.dma("sync", PBF[:, dof:dof + 5632], dwb.ap()[ht], wait=w)
                dld = r.v("SQ")
                w = [("SQ", dld), ("DVE", dve_hdn[NIT - 1]),
                     ("ACT", act_gt[dpar] if (th > 0 or ht >= 2)
                      else act_floor2)]
                for tj in range(2):
                    for it in range(NIT):
                        r.emit("tensor", "matmul",
                               PSF[:, dpar * 1024 + tj * 512:
                                   dpar * 1024 + tj * 512 + 512],
                               PBF[:, dof + it * 128:dof + it * 128 + 128],
                               PBF[:, hdn0 + it * 1024 + tj * 512:
                                   hdn0 + it * 1024 + tj * 512 + 512],
                               start=(it == 0), stop=(it == NIT - 1),
                               wait=tuple(w) if (tj == 0 and it == 0) else (),
                               inc=(("PE", 1) if (tj == 1 and it == NIT - 1)
                                    else None))
                dmm = r.v("PE")
                pe_down_ht[ht] = dmm
                gt = PBF[:, 69632 + dpar * 1024:69632 + (dpar + 1) * 1024]
                w = [("PE", dmm), ("PE", pe_tr_g[dpar])]
                r.emit("scalar", "activation", gt,
                       PSF[:, dpar * 1024:(dpar + 1) * 1024], AF.Silu,
                       scale=ds2_sb[:, ht:ht + 1], wait=tuple(w),
                       inc=("ACT", 1))
                act_gt[dpar] = r.v("ACT")
                ptb = (PTA, PTB)[dpar]
                w = [("ACT", act_gt[dpar]), ("DVE", dve_gtev[dpar])]
                for j in range(8):
                    r.emit("tensor", "transpose",
                           ptb[:, j * 128:(j + 1) * 128],
                           gt[:, j * 128:(j + 1) * 128], ident_sb[:, :],
                           wait=tuple(w) if j == 0 else (),
                           inc=(("PE", 1) if j == 7 else None))
                pe_tr_g[dpar] = r.v("PE")
                gst = PBF[:, 71680 + dpar * 1024:71680 + (dpar + 1) * 1024]
                w = [("PE", pe_tr_g[dpar]), ("SC", sc_gp[dpar])]
                r.emit("vector", "tensor_copy", gst, ptb[:, :],
                       wait=tuple(w), inc=("DVE", 1))
                dve_gtev[dpar] = r.v("DVE")
                dstg = dap(gsc, th * 1024 * E + ht * 128,
                           [(E, 128), (128 * E, 8), (1, 128)])
                r.dma("scalar", dstg, gst, wait=(("DVE", dve_gtev[dpar]),))
                sc_gp[dpar] = r.v("SC")
            pe_down_end = r.v("PE")
            if th == 0:
                pda_sc = r.v("SC")
                pda_dve = 0
                pda_st = 0
                for tt in range(8):
                    sv = PBF[:, 73728:75776]
                    gv = PBF[:, 75776:77824]
                    xv = PF[:, 2048:4096]
                    tv = PF[:, 4096:6144]
                    w = [("CC2", 1), ("SC", pda_sc),
                         ("SC", sc_sp_hist[1][31])]
                    if tt:
                        w = [("DVE", pda_dve), ("SC", pda_st)]
                    r.dma("sync", sv,
                          s_own.ap()[tt * 128:(tt + 1) * 128, :], wait=w)
                    r.dma("sync", gv, gsc.ap()[tt * 128:(tt + 1) * 128, :])
                    r.dma("sync", xv, x_own[tt * 128:(tt + 1) * 128, :])
                    ld = r.v("SQ")
                    r.emit("vector", "tensor_tensor", tv, sv, gv, ALU.mult,
                           wait=(("SQ", ld),))
                    r.emit("vector", "tensor_add", xv, xv, tv,
                           inc=("DVE", 1))
                    pda_dve = r.v("DVE")
                    r.dma("scalar", out_t[tt * 128:(tt + 1) * 128, :], xv,
                          wait=(("DVE", pda_dve),))
                    pda_st = r.v("SC")

        r.barrier()

        sc_out = [0] * 16
        dve_pd = [0] * 16
        for tt in range(8, 16):
            par = tt % 2
            sv = PBF[:, 4096 + par * 2048:4096 + (par + 1) * 2048]
            gv = PBF[:, par * 2048:(par + 1) * 2048]
            xv = PF[:, par * 2048:(par + 1) * 2048]
            tv = PF[:, 4096 + par * 2048:4096 + (par + 1) * 2048]
            w = [("CC2", 1)] if tt == 8 else []
            if tt >= 10:
                w.append(("DVE", dve_pd[tt - 2]))
                w.append(("SC", sc_out[tt - 2]))
            r.dma("sync", sv, s_own.ap()[tt * 128:(tt + 1) * 128, :], wait=w)
            r.dma("sync", gv, gsc.ap()[tt * 128:(tt + 1) * 128, :])
            r.dma("sync", xv, x_own[tt * 128:(tt + 1) * 128, :])
            ld = r.v("SQ")
            r.emit("vector", "tensor_tensor", tv, sv, gv, ALU.mult,
                   wait=(("SQ", ld),))
            r.emit("vector", "tensor_add", xv, xv, tv, inc=("DVE", 1))
            dve_pd[tt] = r.v("DVE")
            r.dma("scalar", out_t[tt * 128:(tt + 1) * 128, :], xv,
                  wait=(("DVE", dve_pd[tt]),))
            sc_out[tt] = r.v("SC")

        if dbg_outs:
            r.barrier()
            for nm, dap in dbg_outs.items():
                if nm in dbg_src:
                    r.dma("gpsimd", dap, dbg_src[nm].ap())

        r.barrier()

        r.finalize()

        def mk(ename):
            stream = r.streams[ename]

            def play(eng):
                for (op, args, kw, waits, inc, _pre) in stream:
                    for (s, vv) in waits:
                        eng.wait_ge(sems[s], vv)
                    if op is None:
                        continue
                    inst = getattr(eng, op)(*args, **kw)
                    if inc is not None:
                        inst.then_inc(sems[inc[0]], inc[1])
            return play

        block.sync(mk("sync"))
        block.scalar(mk("scalar"))
        block.vector(mk("vector"))
        block.tensor(mk("tensor"))
        block.gpsimd(mk("gpsimd"))

    return nc


def _host_consts():
    k = np.arange(64)
    inv = ROPE_BASE ** (-2.0 * k / D)
    didx = np.concatenate([k, k])
    ang = np.arange(T, dtype=np.float64)[None, :] * inv[didx][:, None]
    cos_t = np.cos(ang).astype(np.float32)
    sin_t = np.sin(ang).astype(np.float32)
    m = np.zeros((128, 2048), dtype=np.float32)
    p = np.arange(128)[:, None]
    c = np.arange(512)[None, :]
    for j in range(4):
        m[:, j * 512:(j + 1) * 512] = (c >= p + 128 * j).astype(np.float32)
    masks = m.astype(BF)
    P = np.zeros((128, 128), dtype=np.float32)
    P[np.arange(64), np.arange(64) + 64] = -1.0
    P[np.arange(64) + 64, np.arange(64)] = 1.0
    protT = np.ascontiguousarray(P.T).astype(BF)
    ident = np.eye(128, dtype=np.float32).astype(BF)
    ones_col = np.ones((128, 1), dtype=np.float32).astype(BF)
    ones128 = np.ones((128, 128), dtype=np.float32).astype(BF)
    ones_row_f = np.ones((1, 128), dtype=np.float32)
    return cos_t, sin_t, masks, protT, ident, ones_col, ones128, ones_row_f


def make_in_maps(L0_attn_x, L0_attn_wq, L0_attn_wk, L0_attn_wv, L0_attn_wo,
                 L0_ffn_gate_w, L0_ffn_up_w, L0_ffn_down_w,
                 L0_ffn_gate_s, L0_ffn_up_s, L0_ffn_down_s):
    f32 = np.float32
    x2 = np.asarray(L0_attn_x, f32).reshape(R, E)
    wq = np.asarray(L0_attn_wq, f32)
    wk = np.asarray(L0_attn_wk, f32)
    wv = np.asarray(L0_attn_wv, f32)
    wo = np.asarray(L0_attn_wo, f32)
    gw = np.asarray(L0_ffn_gate_w, f32)
    uw = np.asarray(L0_ffn_up_w, f32)
    dw = np.asarray(L0_ffn_down_w, f32)
    gs = np.asarray(L0_ffn_gate_s, f32)
    us = np.asarray(L0_ffn_up_s, f32)
    ds = np.asarray(L0_ffn_down_s, f32)

    gate_hh = np.ascontiguousarray(
        gw.reshape(NIT, 128, 16, 128).transpose(0, 3, 2, 1).reshape(
            NIT, 128, 2048))
    up_hh = np.ascontiguousarray(
        uw.reshape(NIT, 128, 16, 128).transpose(0, 3, 2, 1).reshape(
            NIT, 128, 2048))
    down_hh = np.ascontiguousarray(
        dw.reshape(16, 128, NIT, 128).transpose(0, 3, 2, 1).reshape(
            16, 128, NIT * 128))
    gate_s2 = np.ascontiguousarray(gs.reshape(NIT, 128).T)
    up_s2 = np.ascontiguousarray(us.reshape(NIT, 128).T)
    down_s2 = np.ascontiguousarray(ds.reshape(16, 128).T)
    (cos_t, sin_t, masks, protT, ident, ones_col, ones128,
     ones_row_f) = _host_consts()

    in_maps = []
    for r_ in range(NCORES):
        f0 = 256 * r_
        wq_hc = np.ascontiguousarray(
            wq[f0:f0 + 256, :].reshape(256, 16, 128).transpose(2, 1, 0)
            .reshape(128, 4096))
        wk_hc = np.ascontiguousarray(
            wk[f0:f0 + 256, :].reshape(256, 16, 128).transpose(2, 1, 0)
            .reshape(128, 4096))
        wv_hc = np.ascontiguousarray(
            wv[f0:f0 + 256, :].reshape(256, 16, 128).transpose(2, 1, 0)
            .reshape(128, 4096))
        wo_hc = np.ascontiguousarray(
            wo[:, f0:f0 + 256].T.reshape(2, 128, 2048).transpose(1, 0, 2)
            .reshape(128, 4096))
        in_maps.append({
            "x_own": np.ascontiguousarray(x2[RPC * r_:RPC * (r_ + 1), :]),
            "wq_h": wq_hc, "wk_h": wk_hc, "wv_h": wv_hc, "wo_h": wo_hc,
            "gate_h": gate_hh, "up_h": up_hh, "down_h": down_hh,
            "gate_s2": gate_s2, "up_s2": up_s2, "down_s2": down_s2,
            "cos_t": cos_t, "sin_t": sin_t, "masks": masks,
            "protT": protT, "ident": ident, "ones_col": ones_col,
            "ones128": ones128, "ones_row_f": ones_row_f,
        })
    return in_maps


def kernel(**inputs):
    nc = build_program()
    in_maps = make_in_maps(**inputs)
    res = run_bass_kernel_spmd(nc, in_maps, core_ids=list(range(NCORES)))
    out = np.concatenate([res.results[i]["out"] for i in range(NCORES)],
                         axis=0)
    return out.reshape(B, T, E)


# version 14
# speedup vs baseline: 1.0624x; 1.0624x over previous
import math
import numpy as np
import ml_dtypes
import concourse.bass as bass
import concourse.mybir as mybir
from concourse.bass_utils import run_bass_kernel_spmd
from concourse.replica_groups import maybe_share_collective_output_space

F32 = mybir.dt.float32
BF16 = mybir.dt.bfloat16
AF = mybir.ActivationFunctionType
ALU = mybir.AluOpType
AXX = mybir.AxisListType.X
BF = ml_dtypes.bfloat16

NCORES = 8
B, T, E = 2, 8192, 2048
H, D = 16, 128
INTER = 5632
R = B * T
RPC = R // NCORES
NIT = INTER // 128
ROPE_BASE = 1e6
ISD = 1.0 / math.sqrt(D)
EPS = 1e-5

ENGINES = ("sync", "scalar", "vector", "tensor", "gpsimd")


class Rec:

    def __init__(self):
        self.streams = {e: [] for e in ENGINES}
        self.cnt = {}
        self.nbar = 0

    def v(self, s):
        return self.cnt.get(s, 0)

    def emit(self, engine, op, *args, wait=(), inc=None, **kw):
        wait = tuple((s, n) for (s, n) in wait if n and n > 0)
        self.streams[engine].append([op, args, kw, wait, inc, None])
        if inc is not None:
            s, n = inc
            self.cnt[s] = self.cnt.get(s, 0) + n
            return self.cnt[s]
        return None

    def dma(self, q, out, in_, wait=(), **kw):
        sem = {"sync": "SQ", "scalar": "SC", "gpsimd": "GQ"}[q]
        pre = self.v(sem)
        v = self.emit(q, "dma_start", wait=wait, inc=(sem, 16),
                      out=out, in_=in_, **kw)
        self.streams[q][-1][5] = (sem, pre)
        return v

    def finalize(self):
        qsems = ("SQ", "SC", "GQ")
        waited = {s: set() for s in qsems}
        for e in ENGINES:
            for it in self.streams[e]:
                for (s, n) in it[3]:
                    if s in qsems:
                        waited[s].add(n)
        wsorted = {s: sorted(waited[s]) for s in qsems}
        import bisect
        for e in ENGINES:
            level = {s: 0 for s in qsems}
            for it in self.streams[e]:
                for (s, n) in it[3]:
                    if s in qsems and n > level[s]:
                        level[s] = n
                if it[5] is not None:
                    s, pre = it[5]
                    ws = wsorted[s]
                    i = bisect.bisect_right(ws, pre)
                    tgt = ws[i - 1] if i > 0 else 0
                    if tgt > level[s]:
                        it[3] = ((s, tgt),) + it[3]
                        level[s] = tgt

    def barrier(self):
        self.nbar += 1
        flush = {"sync": "SQ", "scalar": "SC", "gpsimd": "GQ"}
        own = {"scalar": "ACT", "vector": "DVE", "tensor": "PE"}
        for e in ENGINES:
            w = []
            if e in flush:
                w.append((flush[e], self.v(flush[e])))
            if e in own:
                s = own[e]
                for it in reversed(self.streams[e]):
                    if it[0] is None or it[0] == "nop":
                        continue
                    if it[4] is None:
                        it[4] = (s, 1)
                        self.cnt[s] = self.cnt.get(s, 0) + 1
                    break
                w.append((s, self.v(s)))
            self.emit(e, "nop", wait=tuple(w), inc=("BAR", 1))
        tgt = self.v("BAR")
        assert tgt == 5 * self.nbar
        for e in ENGINES:
            self.emit(e, None, wait=(("BAR", tgt),))


def build_program(debug_names=(), race_detect=True):
    nc = bass.Bass("TRN2", target_bir_lowering=False, debug=False,
                   detect_race_conditions=race_detect)

    def din(name, shape, dt=F32):
        return nc.dram_tensor(name, list(shape), dt, kind="ExternalInput").ap()

    x_own = din("x_own", (RPC, E))
    wq_h = din("wq_h", (128, 4096))
    wk_h = din("wk_h", (128, 4096))
    wv_h = din("wv_h", (128, 4096))
    wo_h = din("wo_h", (128, 4096))
    gate_h = din("gate_h", (NIT, 128, 2048))
    up_h = din("up_h", (NIT, 128, 2048))
    down_h = din("down_h", (16, 128, NIT * 128))
    gate_s2 = din("gate_s2", (128, NIT))
    up_s2 = din("up_s2", (128, NIT))
    down_s2 = din("down_s2", (128, 16))
    cos_t = din("cos_t", (128, T))
    sin_t = din("sin_t", (128, T))
    masks = din("masks", (128, 2048), BF16)
    protT = din("protT", (128, 128), BF16)
    ident = din("ident", (128, 128), BF16)
    ones_col = din("ones_col", (128, 1), BF16)
    ones128 = din("ones128", (128, 128), BF16)
    ones_row_f = din("ones_row_f", (1, 128), F32)

    out_t = nc.dram_tensor("out", [RPC, E], F32, kind="ExternalOutput").ap()

    rg = [list(range(NCORES))]
    ag_space = maybe_share_collective_output_space("AllGather", rg)
    rs_space = maybe_share_collective_output_space("ReduceScatter", rg)
    ag1inA = nc.dram_tensor("ag1inA", [E, RPC // 2], BF16)
    ag1inB = nc.dram_tensor("ag1inB", [E, RPC // 2], BF16)
    ag1outA = nc.dram_tensor("ag1outA", [R, RPC // 2], BF16,
                             addr_space=ag_space)
    ag1outB = nc.dram_tensor("ag1outB", [R, RPC // 2], BF16,
                             addr_space=ag_space)
    qs = nc.dram_tensor("qs", [256, R], BF16)
    ks = nc.dram_tensor("ks", [256, R], BF16)
    vs = nc.dram_tensor("vs", [2, 2, 128, T], BF16)
    spart = nc.dram_tensor("spart", [R, E], BF16)
    s_own = nc.dram_tensor("s_own", [RPC, E], BF16, addr_space=rs_space)
    gsc = nc.dram_tensor("gsc", [RPC, E], BF16)
    gwb = nc.dram_tensor("gwb", [NIT, 128, 2048], BF16)
    uwb = nc.dram_tensor("uwb", [NIT, 128, 2048], BF16)
    dwb = nc.dram_tensor("dwb", [16, 128, NIT * 128], BF16)

    dbg_src = {"ag1in": ag1inA, "ag1out": ag1outA, "qs": qs, "ks": ks,
               "vs": vs, "spart": spart, "s_own": s_own, "gsc": gsc}
    dbg_outs = {}
    for nm in debug_names:
        if nm == "norm0":
            dbg_outs["norm0_sq2"] = nc.dram_tensor(
                "dbg_norm0_sq2", [128, 512], BF16, kind="ExternalOutput").ap()
            dbg_outs["norm0_row"] = nc.dram_tensor(
                "dbg_norm0_row", [1, 512], F32, kind="ExternalOutput").ap()
            dbg_outs["norm0_qrt"] = nc.dram_tensor(
                "dbg_norm0_qrt", [128, 1024], BF16, kind="ExternalOutput").ap()
            continue
        if nm == "bias4":
            dbg_outs["bias4"] = nc.dram_tensor(
                "dbg_bias4", [128, 4], F32, kind="ExternalOutput").ap()
            dbg_outs["runmax"] = nc.dram_tensor(
                "dbg_runmax", [1, 16], F32, kind="ExternalOutput").ap()
            dbg_outs["negm"] = nc.dram_tensor(
                "dbg_negm", [1, 8], F32, kind="ExternalOutput").ap()
            continue
        h = dbg_src[nm]
        dbg_outs[nm] = nc.dram_tensor(
            "dbg_" + nm, list(h.shape), h.dtype, kind="ExternalOutput").ap()

    r = Rec()

    import contextlib
    with contextlib.ExitStack() as ctx:
        e = ctx.enter_context
        PBF = e(nc.sbuf_tensor("PBF", [128, 77824], BF16))
        PF = e(nc.sbuf_tensor("PF", [128, 8192], F32))
        mask_sb = e(nc.sbuf_tensor("mask_sb", [128, 2048], BF16))
        protT_sb = e(nc.sbuf_tensor("protT_sb", [128, 128], BF16))
        ident_sb = e(nc.sbuf_tensor("ident_sb", [128, 128], BF16))
        ones_col_sb = e(nc.sbuf_tensor("ones_col_sb", [128, 1], BF16))
        ones128_sb = e(nc.sbuf_tensor("ones128_sb", [128, 128], BF16))
        ones_row_f_sb = e(nc.sbuf_tensor("ones_row_f_sb", [1, 128], F32))
        gs2_sb = e(nc.sbuf_tensor("gs2_sb", [128, NIT], F32))
        us2_sb = e(nc.sbuf_tensor("us2_sb", [128, NIT], F32))
        ds2_sb = e(nc.sbuf_tensor("ds2_sb", [128, 16], F32))
        wosl_sb = e(nc.sbuf_tensor("wosl_sb", [128, 4096], BF16))
        stats = e(nc.sbuf_tensor("stats", [128, 16], F32))
        runmax2 = e(nc.sbuf_tensor("runmax2", [128, 16], F32))
        stats2m = e(nc.sbuf_tensor("stats2m", [128, 16], F32))
        runmax = e(nc.sbuf_tensor("runmax", [1, 16], F32))
        negm = e(nc.sbuf_tensor("negm", [1, 8], F32))
        bias4 = e(nc.sbuf_tensor("bias4", [128, 4], F32))
        rsum = e(nc.sbuf_tensor("rsum", [1, 512], F32))
        PSF = e(nc.psum_tensor("PSF", [128, 3072], F32))
        PTA = e(nc.psum_tensor("PTA", [128, 1024], BF16))
        PTB = e(nc.psum_tensor("PTB", [128, 1024], BF16))
        hSQ = e(nc.semaphore("SQ"))
        hSC = e(nc.semaphore("SC"))
        hGQ = e(nc.semaphore("GQ"))
        hPE = e(nc.semaphore("PE"))
        hACT = e(nc.semaphore("ACT"))
        hDVE = e(nc.semaphore("DVE"))
        hCC1 = e(nc.semaphore("CC1"))
        hCC2 = e(nc.semaphore("CC2"))
        hCC3 = e(nc.semaphore("CC3"))
        hPL = e(nc.semaphore("PL"))
        hBAR = e(nc.semaphore("BAR"))
        block = e(nc.Block())
        sems = {"SQ": hSQ, "SC": hSC, "GQ": hGQ, "PE": hPE, "ACT": hACT,
                "DVE": hDVE, "CC1": hCC1, "CC2": hCC2, "CC3": hCC3, "PL": hPL,
                "BAR": hBAR}

        def bank(i, n=512):
            return PSF[:, 512 * i:512 * i + n]

        def dap(t, off, dims):
            return bass.AP(t, off, [list(d) for d in dims])

        r.dma("sync", mask_sb[:, :], masks)
        r.dma("sync", protT_sb[:, :], protT)
        r.dma("sync", ident_sb[:, :], ident)
        r.dma("sync", ones_col_sb[:, :], ones_col)
        r.dma("sync", ones128_sb[:, :], ones128)
        r.dma("sync", ones_row_f_sb[:, :], ones_row_f)
        r.dma("sync", gs2_sb[:, :], gate_s2)
        r.dma("sync", us2_sb[:, :], up_s2)
        r.dma("sync", ds2_sb[:, :], down_s2)
        p0 = r.v("SQ")
        r.emit("vector", "memset", runmax[0:1, :], 0.0, inc=("DVE", 1))

        ln_done = [0] * 16
        tr_done = [0] * 16
        ev_done = [[0, 0] for _ in range(16)]
        st_done = [0] * 16
        red_done = [0] * 16
        sq_done = [0] * 16
        for i in range(16):
            p = i % 2
            xt = PF[:, p * 2048:(p + 1) * 2048]
            xsq = PF[:, 4096 + p * 2048:4096 + (p + 1) * 2048]
            w = [("SQ", p0)] if i == 0 else []
            if i >= 2:
                w += [("ACT", sq_done[i - 2]), ("DVE", red_done[i - 2])]
            r.dma("sync", xt, x_own[128 * i:128 * (i + 1), :], wait=w)
            ld = r.v("SQ")
            r.emit("vector", "reduce_sum", stats[:, i:i + 1], xt, AXX,
                   wait=(("SQ", ld),), inc=("DVE", 1))
            r.emit("scalar", "activation", xsq, xt, AF.Square,
                   wait=(("SQ", ld),), inc=("ACT", 1))
            sq_done[i] = r.v("ACT")
            r.emit("vector", "reduce_sum", runmax2[:, i:i + 1], xsq, AXX,
                   wait=(("ACT", sq_done[i]),), inc=("DVE", 1))
            red_done[i] = r.v("DVE")
        r.emit("vector", "tensor_scalar_mul", stats[:, 0:16],
               stats[:, 0:16], 1.0 / E, wait=(("DVE", red_done[15]),),
               inc=("DVE", 1))
        c1 = r.v("DVE")
        r.emit("vector", "tensor_scalar_mul", runmax2[:, 0:16],
               runmax2[:, 0:16], 1.0 / E, wait=(("DVE", c1),),
               inc=("DVE", 1))
        c2 = r.v("DVE")
        r.emit("vector", "tensor_tensor", stats2m[:, 0:16],
               stats[:, 0:16], stats[:, 0:16], ALU.mult,
               wait=(("DVE", c2),), inc=("DVE", 1))
        c3 = r.v("DVE")
        r.emit("vector", "tensor_sub", stats2m[:, 0:16],
               runmax2[:, 0:16], stats2m[:, 0:16],
               wait=(("DVE", c3),), inc=("DVE", 1))
        c4 = r.v("DVE")
        r.emit("vector", "tensor_scalar_add", stats2m[:, 0:16],
               stats2m[:, 0:16], EPS, wait=(("DVE", c4),), inc=("DVE", 1))
        c5 = r.v("DVE")
        r.emit("scalar", "activation", stats2m[:, 0:16], stats2m[:, 0:16],
               AF.Ln, wait=(("DVE", c5),))
        r.emit("scalar", "activation", stats2m[:, 0:16], stats2m[:, 0:16],
               AF.Exp, scale=-0.5, inc=("ACT", 1))
        rstd_all = r.v("ACT")
        r.emit("vector", "tensor_scalar_mul", stats[:, 0:16],
               stats[:, 0:16], -1.0, wait=(("DVE", c5),), inc=("DVE", 1))
        negm_all = r.v("DVE")
        for i in range(16):
            p = i % 2
            xt = PF[:, p * 2048:(p + 1) * 2048]
            xnv = PBF[:, 36864 + p * 2048:36864 + (p + 1) * 2048]
            stg = PBF[:, 40960 + p * 2048:40960 + (p + 1) * 2048]
            w = []
            if i < 2:
                w.append(("DVE", negm_all))
            else:
                w.append(("DVE", ln_done[i - 2]))
            r.dma("sync", xt, x_own[128 * i:128 * (i + 1), :], wait=w)
            ld = r.v("SQ")
            w = [("SQ", ld), ("ACT", rstd_all), ("DVE", negm_all)]
            if i >= 2:
                w.append(("PE", tr_done[i - 2]))
            r.emit("vector", "tensor_scalar", xnv, xt,
                   stats[:, i:i + 1], stats2m[:, i:i + 1],
                   ALU.add, ALU.mult, wait=tuple(w), inc=("DVE", 1))
            ln_done[i] = r.v("DVE")
            for half, pt in enumerate((PTA, PTB)):
                w = [("DVE", ln_done[i])]
                if i >= 1:
                    w.append(("DVE", ev_done[i - 1][half]))
                for j in range(8):
                    et = half * 8 + j
                    r.emit("tensor", "transpose",
                           pt[:, 128 * j:128 * (j + 1)],
                           xnv[:, 128 * et:128 * (et + 1)],
                           ident_sb[:, :],
                           wait=tuple(w) if j == 0 else (),
                           inc=(("PE", 1) if j == 7 else None))
                trh = r.v("PE")
                w2 = [("PE", trh)]
                if i >= 2 and half == 0:
                    w2.append(("SC", st_done[i - 2]))
                r.emit("vector", "tensor_copy",
                       stg[:, half * 1024:(half + 1) * 1024], pt[:, :],
                       wait=tuple(w2), inc=("DVE", 1))
                ev_done[i][half] = r.v("DVE")
            tr_done[i] = r.v("PE")
            agt = ag1inA if i < 8 else ag1inB
            dst = dap(agt, 128 * (i % 8),
                      [(RPC // 2, 128), (128 * RPC // 2, 16), (1, 128)])
            r.dma("scalar", dst, stg, wait=(("DVE", ev_done[i][1]),))
            st_done[i] = r.v("SC")

        r.emit("gpsimd", "collective_compute", "AllGather", ALU.bypass,
               replica_groups=[list(range(NCORES))],
               ins=[ag1inA.ap().opt()], outs=[ag1outA.ap().opt()],
               wait=(("SC", st_done[7]),), inc=("CC1", 1))
        r.emit("gpsimd", "collective_compute", "AllGather", ALU.bypass,
               replica_groups=[list(range(NCORES))],
               ins=[ag1inB.ap().opt()], outs=[ag1outB.ap().opt()],
               wait=(("SC", st_done[15]),), inc=("CC3", 1))
        r.dma("gpsimd", PBF[:, 16384:20480], wq_h)
        r.dma("gpsimd", PBF[:, 20480:24576], wk_h)
        r.dma("gpsimd", PBF[:, 24576:28672], wv_h)
        r.dma("gpsimd", wosl_sb[:, :], wo_h)
        gqw = r.v("GQ")

        qraw = PBF[:, 28672:29696]
        kraw = PBF[:, 29696:30720]
        qrt = PBF[:, 30720:31744]
        krt = PBF[:, 31744:32768]
        sq2 = PBF[:, 32768:33280]
        vtile = PBF[:, 33280:34304]
        pe_xnb = [0] * 32
        dve_rope = [0] * 32
        sc_pb = [0] * 32
        prev_v_evac = 0
        prev_ropeq = 0
        prev_ropek = 0
        prev_norm_mm = 0
        prev_norm_red = 0
        tb_order = ([tb for tb in range(32) if tb % 4 < 2]
                    + [tb for tb in range(32) if tb % 4 >= 2])
        for ti, tb in enumerate(tb_order):
            par = ti % 2
            b = tb // 16
            R0 = 512 * tb
            tbase = R0 % T
            blk = tb // 4
            tl0 = (tb % 4) * 512
            half_b = tl0 >= 1024
            xnb = PBF[:, par * 8192:(par + 1) * 8192]
            ct = PF[:, par * 512:(par + 1) * 512]
            st = PF[:, 1024 + par * 512:1024 + (par + 1) * 512]
            tmp1 = PF[:, 2048:2560]
            tmp2 = PF[:, 2560:3072]
            w = []
            if ti == 0:
                w.append(("CC1", 1))
            if ti == 16:
                w.append(("CC3", 1))
            if ti >= 2:
                w.append(("PE", pe_xnb[ti - 2]))
            agsrc = ag1outB if half_b else ag1outA
            src = dap(agsrc, blk * 2048 * (RPC // 2) + (tl0 % 1024),
                      [(RPC // 2, 128), (128 * RPC // 2, 16), (1, 512)])
            r.dma("sync", xnb, src, wait=w)
            w = [("DVE", dve_rope[ti - 2])] if ti >= 2 else [
                ("DVE", ln_done[15])]
            r.dma("sync", ct, cos_t[:, tbase:tbase + 512], wait=w)
            r.dma("sync", st, sin_t[:, tbase:tbase + 512])
            ldb = r.v("SQ")
            w = [("SQ", ldb), ("GQ", gqw), ("DVE", prev_ropek)]
            for fl in range(2):
                for et in range(16):
                    r.emit("tensor", "matmul", bank(fl),
                           PBF[:, 16384 + et * 256 + fl * 128:
                               16384 + et * 256 + fl * 128 + 128],
                           xnb[:, et * 512:(et + 1) * 512],
                           start=(et == 0), stop=(et == 15),
                           wait=tuple(w) if (fl == 0 and et == 0) else (),
                           inc=(("PE", 1) if (fl == 1 and et == 15) else None))
            qmm = r.v("PE")
            w = [("ACT", prev_v_evac)]
            for fl in range(2):
                for et in range(16):
                    r.emit("tensor", "matmul", bank(4 + fl),
                           PBF[:, 20480 + et * 256 + fl * 128:
                               20480 + et * 256 + fl * 128 + 128],
                           xnb[:, et * 512:(et + 1) * 512],
                           start=(et == 0), stop=(et == 15),
                           wait=tuple(w) if (fl == 0 and et == 0) else (),
                           inc=(("PE", 1) if (fl == 1 and et == 15) else None))
            kmm = r.v("PE")
            w = [("PE", qmm), ("DVE", prev_ropeq)]
            r.emit("scalar", "activation", qraw, PSF[:, 0:1024], AF.Copy,
                   scale=ISD, wait=tuple(w), inc=("ACT", 1))
            q_evac = r.v("ACT")
            w = [("ACT", q_evac), ("DVE", prev_norm_red)]
            for fl in range(2):
                r.emit("tensor", "matmul", bank(2 + fl), protT_sb[:, :],
                       qraw[:, fl * 512:(fl + 1) * 512],
                       start=True, stop=True,
                       wait=tuple(w) if fl == 0 else (),
                       inc=(("PE", 1) if fl == 1 else None))
            rotq_mm = r.v("PE")
            w = [("PE", kmm), ("DVE", prev_ropek)]
            r.emit("scalar", "activation", kraw, PSF[:, 2048:3072], AF.Copy,
                   wait=tuple(w), inc=("ACT", 1))
            k_evac = r.v("ACT")
            w = [("PE", rotq_mm), ("SQ", ldb),
                 ("SC", sc_pb[ti - 1] if ti else 0)]
            for fl in range(2):
                r.emit("vector", "tensor_tensor", tmp1,
                       qraw[:, fl * 512:(fl + 1) * 512], ct, ALU.mult,
                       wait=tuple(w) if fl == 0 else ())
                r.emit("vector", "tensor_tensor", tmp2, bank(2 + fl), st,
                       ALU.mult)
                r.emit("vector", "tensor_add",
                       qrt[:, fl * 512:(fl + 1) * 512], tmp1, tmp2,
                       inc=("DVE", 1))
            ropeq = r.v("DVE")
            prev_ropeq = ropeq
            w = [("ACT", k_evac)]
            for fl in range(2):
                r.emit("tensor", "matmul", bank(fl), protT_sb[:, :],
                       kraw[:, fl * 512:(fl + 1) * 512],
                       start=True, stop=True,
                       wait=tuple(w) if fl == 0 else (),
                       inc=(("PE", 1) if fl == 1 else None))
            rotk_mm = r.v("PE")
            w = [("ACT", k_evac)]
            for tt in range(4):
                bk = PSF[:, 2048 + tt * 256:2048 + (tt + 1) * 256]
                for et in range(16):
                    r.emit("tensor", "matmul", bk,
                           xnb[:, et * 512 + tt * 128:
                               et * 512 + tt * 128 + 128],
                           PBF[:, 24576 + et * 256:24576 + (et + 1) * 256],
                           start=(et == 0), stop=(et == 15),
                           wait=tuple(w) if (tt == 0 and et == 0) else (),
                           inc=(("PE", 1) if (tt == 3 and et == 15) else None))
            vmm = r.v("PE")
            pe_xnb[ti] = vmm
            w = [("PE", rotk_mm)]
            for fl in range(2):
                r.emit("vector", "tensor_tensor", tmp1,
                       kraw[:, fl * 512:(fl + 1) * 512], ct, ALU.mult,
                       wait=tuple(w) if fl == 0 else ())
                r.emit("vector", "tensor_tensor", tmp2, bank(fl), st,
                       ALU.mult)
                r.emit("vector", "tensor_add",
                       krt[:, fl * 512:(fl + 1) * 512], tmp1, tmp2,
                       inc=("DVE", 1))
            ropek = r.v("DVE")
            prev_ropek = ropek
            dve_rope[ti] = ropek
            w = [("PE", vmm), ("SC", sc_pb[ti - 1] if ti else 0)]
            r.emit("scalar", "activation", vtile, PSF[:, 2048:3072], AF.Copy,
                   wait=tuple(w), inc=("ACT", 1))
            prev_v_evac = r.v("ACT")
            for pi, (srcv, slot0) in enumerate(((qrt, 0), (krt, 4))):
                for fl in range(2):
                    slot = slot0 + b * 2 + fl
                    w = [("PE", prev_norm_mm)]
                    r.emit("vector", "tensor_tensor", sq2,
                           srcv[:, fl * 512:(fl + 1) * 512],
                           srcv[:, fl * 512:(fl + 1) * 512], ALU.mult,
                           wait=tuple(w), inc=("DVE", 1))
                    sqd2 = r.v("DVE")
                    r.emit("tensor", "matmul", PSF[0:1, 1024:1536],
                           ones_col_sb[:, :], sq2, start=True, stop=True,
                           wait=(("DVE", sqd2),), inc=("PE", 1))
                    prev_norm_mm = r.v("PE")
                    if tb == 0 and pi == 0 and fl == 0 and "norm0" in debug_names:
                        r.emit("vector", "tensor_copy", PBF[:, 73728:74240],
                               sq2)
                        r.emit("vector", "tensor_copy", PBF[:, 74240:75264],
                               qrt)
                        r.emit("vector", "tensor_copy", PF[0:1, 3072:3584],
                               PSF[0:1, 1024:1536],
                               wait=(("PE", prev_norm_mm),), inc=("DVE", 1))
                    r.emit("vector", "reduce_max", stats[0:1, 15:16],
                           PSF[0:1, 1024:1536], AXX,
                           wait=(("PE", prev_norm_mm),
                                 ("DVE", prev_norm_red)),
                           inc=("DVE", 1))
                    rmx = r.v("DVE")
                    r.emit("vector", "tensor_max",
                           runmax[0:1, slot:slot + 1],
                           runmax[0:1, slot:slot + 1], stats[0:1, 15:16],
                           wait=(("DVE", rmx),), inc=("DVE", 1))
                    prev_norm_red = r.v("DVE")
            dstq = dap(qs, R0, [(R, 128), (128 * R, 2), (1, 512)])
            r.dma("scalar", dstq, qrt, wait=(("DVE", ropeq),))
            dstk = dap(ks, R0, [(R, 128), (128 * R, 2), (1, 512)])
            r.dma("scalar", dstk, krt, wait=(("DVE", ropek),))
            ktg0 = (R0 % T) // 128
            for hl in range(2):
                dstv = dap(vs, (b * 2 + hl) * 128 * T + ktg0 * 128,
                           [(T, 128), (128, 4), (1, 128)])
                srcv2 = bass.AP(PBF, 33280 + hl * 128,
                                [[77824, 128], [256, 4], [1, 128]])
                r.dma("scalar", dstv, srcv2,
                      wait=(("ACT", prev_v_evac),) if hl == 0 else ())
            sc_pb[ti] = r.v("SC")

        for u in range(4):
            bu, fl = u // 2, u % 2
            qslot = bu * 2 + fl
            kslot = 4 + bu * 2 + fl
            w = [("DVE", prev_norm_red)] if u == 0 else []
            r.emit("vector", "tensor_tensor", negm[0:1, u:u + 1],
                   runmax[0:1, qslot:qslot + 1],
                   runmax[0:1, kslot:kslot + 1],
                   ALU.mult, wait=tuple(w), inc=("DVE", 1))
            nd = r.v("DVE")
            r.emit("scalar", "activation", negm[0:1, u:u + 1],
                   negm[0:1, u:u + 1], AF.Ln, wait=(("DVE", nd),))
            r.emit("scalar", "activation", negm[0:1, u:u + 1],
                   negm[0:1, u:u + 1], AF.Exp, scale=0.5, inc=("ACT", 1))
            ad = r.v("ACT")
            r.emit("vector", "tensor_scalar", negm[0:1, u:u + 1],
                   negm[0:1, u:u + 1], -1.001, -0.01, ALU.mult, ALU.add,
                   wait=(("ACT", ad),), inc=("DVE", 1))
            nd2 = r.v("DVE")
            r.emit("tensor", "matmul", PSF[:, 1024:1025],
                   ones_row_f_sb[0:1, :], negm[0:1, u:u + 1],
                   start=True, stop=True, wait=(("DVE", nd2),),
                   inc=("PE", 1))
            pm = r.v("PE")
            r.emit("vector", "tensor_copy", bias4[:, u:u + 1],
                   PSF[:, 1024:1025], wait=(("PE", pm),), inc=("DVE", 1))

        if "bias4" in debug_names:
            bd = r.v("DVE")
            r.dma("scalar", dbg_outs["bias4"], bias4[:, :],
                  wait=(("DVE", bd),))
            r.dma("scalar", dbg_outs["runmax"], runmax[0:1, :])
            r.dma("scalar", dbg_outs["negm"], negm[0:1, :])
        if "norm0" in debug_names:
            bd = r.v("DVE")
            r.dma("scalar", dbg_outs["norm0_sq2"], PBF[:, 73728:74240],
                  wait=(("DVE", bd),))
            r.dma("scalar", dbg_outs["norm0_qrt"], PBF[:, 74240:75264])
            r.dma("scalar", dbg_outs["norm0_row"], PF[0:1, 3072:3584])

        r.barrier()

        QOF = (0, 40960)
        KOF = (8192, 24576)
        VOF = (16384, 32768)
        AOF = (49152, 57344)
        PTOF = (65536, 67584)
        units = [(0, 0), (0, 1), (1, 0), (1, 1)]
        ld_unit = [0] * 4
        pe_unit_end = [0] * 4
        pe_op_end = [0, 0]

        def att_loads(u, extra_wait=()):
            b, hl = units[u]
            par = u % 2
            r.dma("sync", PBF[:, QOF[par]:QOF[par] + 8192],
                  qs.ap()[hl * 128:(hl + 1) * 128, b * T:(b + 1) * T],
                  wait=extra_wait)
            r.dma("sync", PBF[:, KOF[par]:KOF[par] + 8192],
                  ks.ap()[hl * 128:(hl + 1) * 128, b * T:(b + 1) * T])
            r.dma("sync", PBF[:, VOF[par]:VOF[par] + 8192],
                  vs.ap()[b, hl, :, :])
            ld_unit[u] = r.v("SQ")

        cast_w = (("SQ", r.v("SQ") + 96),)
        for lo in (0, 22):
            r.dma("gpsimd", gwb.ap()[lo:lo + 22], gate_h[lo:lo + 22],
                  wait=cast_w)
            cast_w = ()
            r.dma("gpsimd", uwb.ap()[lo:lo + 22], up_h[lo:lo + 22])
        for lo in (0, 8):
            r.dma("gpsimd", dwb.ap()[lo:lo + 8], down_h[lo:lo + 8])
        gqcast = r.v("GQ")

        att_loads(0)
        att_loads(1)

        def emit_op_tt(b, tt):
            a0 = PBF[:, AOF[0]:AOF[0] + 8192]
            a1 = PBF[:, AOF[1]:AOF[1] + 8192]
            po = tt % 4
            outst = PBF[:, 69632 + po * 2048:69632 + (po + 1) * 2048]
            stq = "sync" if b == 0 else "scalar"
            stsem = "SQ" if b == 0 else "SC"
            for fb in range(4):
                bk = bank(fb % 2)
                w = [("DVE", dve_out_evac[fb % 2]),
                     ("ACT", act_out_evac[0])]
                if tt == 0 and fb == 0:
                    w.append(("DVE", ao_hist[0]))
                    w.append(("DVE", ao_hist[1]))
                    w.append(("GQ", gqw))
                    w.append(("ACT", exp_hist[0]))
                    w.append(("ACT", exp_hist[1]))
                for et, asrc in enumerate((a0, a1)):
                    r.emit("tensor", "matmul", bk,
                           asrc[:, tt * 128:(tt + 1) * 128],
                           wosl_sb[:, et * 2048 + fb * 512:
                                   et * 2048 + fb * 512 + 512],
                           start=(et == 0), stop=(et == 1),
                           wait=tuple(w) if et == 0 else (),
                           inc=(("PE", 1) if et == 1 else None))
                pmm = r.v("PE")
                w2 = [("PE", pmm)]
                if tt >= 4 and fb < 2:
                    w2.append((stsem, sc_sp_hist[b][(tt - 4) // 2]))
                if fb % 2 == 0:
                    r.emit("vector", "tensor_copy",
                           outst[:, fb * 512:(fb + 1) * 512], bk,
                           wait=tuple(w2), inc=("DVE", 1))
                    dve_out_evac[0] = r.v("DVE")
                else:
                    r.emit("scalar", "activation",
                           outst[:, fb * 512:(fb + 1) * 512], bk, AF.Copy,
                           wait=tuple(w2), inc=("ACT", 1))
                    act_out_evac[0] = r.v("ACT")
            if tt % 2 == 1:
                dst = dap(spart, (b * T + (tt - 1) * 128) * E,
                          [(E, 128), (128 * E, 2), (1, E)])
                so = 69632 + (tt - 1) % 4 * 2048
                r.dma(stq, dst, PBF[:, so:so + 4096],
                      wait=(("DVE", dve_out_evac[0]),
                            ("ACT", act_out_evac[0])))
                sc_sp_hist[b][tt // 2] = r.v(stsem)
        exp_hist = [0, 0]
        pvmm_hist = [0, 0]
        ao_hist = [0, 0]
        pv_bank_free = 0
        sum_bank_free = 0
        dve_out_evac = [0, 0]
        act_out_evac = [0]
        sc_sp_hist = {0: [0] * 64, 1: [0] * 64}
        pair_idx = 0
        for u in range(4):
            b, hl = units[u]
            par = u % 2
            qu = PBF[:, QOF[par]:QOF[par] + 8192]
            ku = PBF[:, KOF[par]:KOF[par] + 8192]
            vu = PBF[:, VOF[par]:VOF[par] + 8192]
            ao = PBF[:, AOF[hl]:AOF[hl] + 8192]
            first_ao = True
            for qb in range(16):
                nq = qb + 1
                gate_ws = [None] * nq

                def emit_spv(g, last_pp):
                    ptg = PBF[:, PTOF[last_pp]:PTOF[last_pp] + 2048]
                    w = [gate_ws[g]]
                    if g == 0:
                        w.append(("ACT", sum_bank_free))
                        w.append(("DVE", pv_bank_free))
                    for j in range(4):
                        kb = 4 * g + j
                        r.emit("tensor", "matmul", PSF[:, 2560:3072],
                               ones128_sb[:, :],
                               ptg[:, j * 512:(j + 1) * 512],
                               start=(g == 0 and j == 0),
                               stop=(g == nq - 1 and j == 3),
                               wait=tuple(w) if j == 0 else ())
                        r.emit("tensor", "matmul", PSF[:, 2048:2560],
                               vu[:, kb * 128:(kb + 1) * 128],
                               ptg[:, j * 512:(j + 1) * 512],
                               start=(g == 0 and j == 0),
                               stop=(g == nq - 1 and j == 3),
                               inc=(("PE", 1) if j == 3 else None))
                    return r.v("PE")

                qpp = [0] * nq
                for g in range(nq):
                    pp = pair_idx % 2
                    pair_idx += 1
                    qpp[g] = pp
                    pt = PBF[:, PTOF[pp]:PTOF[pp] + 2048]
                    w = [("ACT", exp_hist[0]), ("ACT", exp_hist[1])]
                    if qb == 0 and g == 0:
                        w.append(("SQ", ld_unit[u]))
                        w.append(("DVE", dve_out_evac[0]))
                        w.append(("DVE", dve_out_evac[1]))
                        w.append(("ACT", act_out_evac[0]))
                    for j in range(4):
                        kb = 4 * g + j
                        r.emit("tensor", "matmul",
                               PSF[:, j * 512:(j + 1) * 512],
                               ku[:, kb * 128:(kb + 1) * 128],
                               qu[:, qb * 512:(qb + 1) * 512],
                               start=True, stop=True,
                               wait=tuple(w) if j == 0 else (),
                               inc=(("PE", 1) if j == 3 else None))
                    smm = r.v("PE")
                    w = [("PE", smm), ("PE", pvmm_hist[pp])]
                    r.emit("scalar", "activation", pt, PSF[:, 0:2048],
                           AF.Exp, bias=bias4[:, u:u + 1], wait=tuple(w),
                           inc=("ACT", 1))
                    exp_hist[pp] = r.v("ACT")
                    gate_ws[g] = ("ACT", exp_hist[pp])
                    if g == qb:
                        r.emit("gpsimd", "tensor_tensor", pt, pt,
                               mask_sb[:, :], ALU.mult,
                               wait=(("ACT", exp_hist[pp]),),
                               inc=("PL", 1))
                        gate_ws[g] = ("PL", r.v("PL"))
                    if g > 0:
                        pvmm_hist[qpp[g - 1]] = emit_spv(g - 1, qpp[g - 1])
                pvmm_hist[qpp[nq - 1]] = emit_spv(nq - 1, qpp[nq - 1])
                pvd = pvmm_hist[qpp[nq - 1]]
                qp = qb % 2
                lnt = PF[:, 512 * qp:512 * qp + 512]
                rec = PF[:, 1024 + 512 * qp:1024 + 512 * qp + 512]
                w = [("PE", pvd), ("DVE", ao_hist[qp])]
                r.emit("scalar", "activation", lnt, PSF[:, 2560:3072],
                       AF.Ln, wait=tuple(w), inc=("ACT", 1))
                sum_bank_free = r.v("ACT")
                r.emit("scalar", "activation", rec, lnt, AF.Exp,
                       scale=-1.0, inc=("ACT", 1))
                recd = r.v("ACT")
                w = [("ACT", recd), ("PE", pvd)]
                if first_ao and u >= 2:
                    w.append(("PE", pe_op_end[0]))
                first_ao = False
                r.emit("vector", "tensor_tensor",
                       ao[:, qb * 512:(qb + 1) * 512],
                       PSF[:, 2048:2560], rec, ALU.mult,
                       wait=tuple(w), inc=("DVE", 1))
                ao_hist[qp] = r.v("DVE")
                pv_bank_free = ao_hist[qp]
            pe_unit_end[u] = r.v("PE")
            if u + 2 < 4:
                att_loads(u + 2, extra_wait=(("PE", pe_unit_end[u]),))
            if hl == 1:
                for _tt in range(64):
                    emit_op_tt(b, _tt)
                pe_op_end[b] = r.v("PE")

        r.emit("gpsimd", "collective_compute", "ReduceScatter", ALU.add,
               replica_groups=[list(range(NCORES))],
               ins=[spart.ap().opt()], outs=[s_own.ap().opt()],
               wait=(("SC", r.v("SC")),), inc=("CC2", 1))

        att_pe_end = r.v("PE")
        att_dve_end = r.v("DVE")

        hdn0 = 16384
        pe_gu = [0] * NIT
        act_silu = 0
        act_upev = 0
        dve_hdn = [0] * NIT
        pe_down_ht = [0] * 16
        pe_tr_g = [0, 0]
        dve_gtev = [0, 0]
        act_gt = [0, 0]
        sc_gp = [0, 0]
        pe_down_end = 0
        pe_gu_end = 0
        for th in range(2):
            w = []
            if th == 1:
                w.append(("PE", pe_down_end))
            else:
                w.append(("PE", pe_unit_end[2]))
            agt = ag1inA if th == 0 else ag1inB
            src = dap(agt, 0,
                      [(RPC // 2, 128), (128 * RPC // 2, 16), (1, 1024)])
            r.dma("sync", PBF[:, 0:16384], src, wait=w)
            xnh_ld = r.v("SQ")
            act_floor = r.v("ACT")
            for it in range(NIT):
                gpar = [2, 1, 0][it % 3]
                gw0 = 61440 + gpar * 4096
                if it >= 3:
                    w = [("PE", pe_gu[it - 3])]
                elif th == 1 and it == 0:
                    w = [("PE", pe_gu[42]), ("PE", pe_tr_g[0]),
                         ("PE", pe_tr_g[1]), ("SC", sc_gp[0]),
                         ("SC", sc_gp[1])]
                elif th == 1 and it == 1:
                    w = [("PE", pe_gu[43])]
                elif th == 1:
                    w = [("PE", pe_gu[41])]
                elif it == 0:
                    w = [("PE", att_pe_end), ("SC", sc_sp_hist[1][30]),
                         ("GQ", gqcast)]
                elif it == 1:
                    w = [("PE", pe_unit_end[3])]
                else:
                    w = [("PE", att_pe_end)]
                r.dma("sync", PBF[:, gw0:gw0 + 2048], gwb.ap()[it], wait=w)
                r.dma("sync", PBF[:, gw0 + 2048:gw0 + 4096], uwb.ap()[it])
                gld = r.v("SQ")
                w = [("SQ", gld), ("SQ", xnh_ld),
                     ("ACT", act_silu if it else act_floor)]
                for tj in range(2):
                    for ht in range(16):
                        r.emit("tensor", "matmul", bank(tj),
                               PBF[:, gw0 + ht * 128:gw0 + ht * 128 + 128],
                               PBF[:, ht * 1024 + tj * 512:
                                   ht * 1024 + tj * 512 + 512],
                               start=(ht == 0), stop=(ht == 15),
                               wait=tuple(w) if (tj == 0 and ht == 0) else (),
                               inc=(("PE", 1) if (tj == 1 and ht == 15)
                                    else None))
                gmm = r.v("PE")
                w = ([("DVE", dve_hdn[it - 1])] if it
                     else [("ACT", act_floor)])
                for tj in range(2):
                    for ht in range(16):
                        r.emit("tensor", "matmul", bank(2 + tj),
                               PBF[:, gw0 + 2048 + ht * 128:
                                   gw0 + 2048 + ht * 128 + 128],
                               PBF[:, ht * 1024 + tj * 512:
                                   ht * 1024 + tj * 512 + 512],
                               start=(ht == 0), stop=(ht == 15),
                               wait=tuple(w) if (tj == 0 and ht == 0) else (),
                               inc=(("PE", 1) if (tj == 1 and ht == 15)
                                    else None))
                umm = r.v("PE")
                pe_gu[it] = umm
                w = [("PE", gmm), ("DVE", dve_hdn[it - 1] if it
                                    else att_dve_end)]
                r.emit("scalar", "activation", PF[:, 0:1024], PSF[:, 0:1024],
                       AF.Silu, scale=gs2_sb[:, it:it + 1], wait=tuple(w),
                       inc=("ACT", 1))
                act_silu = r.v("ACT")
                w = [("PE", umm), ("ACT", act_silu)]
                if it == 0:
                    w.append(("PE", pe_down_end if th else att_pe_end))
                r.emit("vector", "scalar_tensor_tensor",
                       PBF[:, hdn0 + it * 1024:hdn0 + (it + 1) * 1024],
                       PSF[:, 1024:2048], us2_sb[:, it:it + 1],
                       PF[:, 0:1024], ALU.mult, ALU.mult,
                       wait=tuple(w), inc=("DVE", 1))
                dve_hdn[it] = r.v("DVE")
            pe_gu_end = r.v("PE")
            act_floor2 = r.v("ACT")
            for ht in range(16):
                dpar = ht % 2
                dof = dpar * 5632
                w = [("DVE", dve_hdn[NIT - 1])]
                if ht >= 2:
                    w.append(("PE", pe_down_ht[ht - 2]))
                else:
                    w.append(("PE", pe_gu_end))
                r.dma("sync", PBF[:, dof:dof + 5632], dwb.ap()[ht], wait=w)
                dld = r.v("SQ")
                w = [("SQ", dld), ("DVE", dve_hdn[NIT - 1]),
                     ("ACT", act_gt[dpar] if (th > 0 or ht >= 2)
                      else act_floor2)]
                for tj in range(2):
                    for it in range(NIT):
                        r.emit("tensor", "matmul",
                               PSF[:, dpar * 1024 + tj * 512:
                                   dpar * 1024 + tj * 512 + 512],
                               PBF[:, dof + it * 128:dof + it * 128 + 128],
                               PBF[:, hdn0 + it * 1024 + tj * 512:
                                   hdn0 + it * 1024 + tj * 512 + 512],
                               start=(it == 0), stop=(it == NIT - 1),
                               wait=tuple(w) if (tj == 0 and it == 0) else (),
                               inc=(("PE", 1) if (tj == 1 and it == NIT - 1)
                                    else None))
                dmm = r.v("PE")
                pe_down_ht[ht] = dmm
                gt = PBF[:, 69632 + dpar * 1024:69632 + (dpar + 1) * 1024]
                w = [("PE", dmm), ("PE", pe_tr_g[dpar])]
                r.emit("scalar", "activation", gt,
                       PSF[:, dpar * 1024:(dpar + 1) * 1024], AF.Silu,
                       scale=ds2_sb[:, ht:ht + 1], wait=tuple(w),
                       inc=("ACT", 1))
                act_gt[dpar] = r.v("ACT")
                ptb = (PTA, PTB)[dpar]
                w = [("ACT", act_gt[dpar]), ("DVE", dve_gtev[dpar])]
                for j in range(8):
                    r.emit("tensor", "transpose",
                           ptb[:, j * 128:(j + 1) * 128],
                           gt[:, j * 128:(j + 1) * 128], ident_sb[:, :],
                           wait=tuple(w) if j == 0 else (),
                           inc=(("PE", 1) if j == 7 else None))
                pe_tr_g[dpar] = r.v("PE")
                gst = PBF[:, 71680 + dpar * 1024:71680 + (dpar + 1) * 1024]
                w = [("PE", pe_tr_g[dpar]), ("SC", sc_gp[dpar])]
                r.emit("vector", "tensor_copy", gst, ptb[:, :],
                       wait=tuple(w), inc=("DVE", 1))
                dve_gtev[dpar] = r.v("DVE")
                dstg = dap(gsc, th * 1024 * E + ht * 128,
                           [(E, 128), (128 * E, 8), (1, 128)])
                r.dma("scalar", dstg, gst, wait=(("DVE", dve_gtev[dpar]),))
                sc_gp[dpar] = r.v("SC")
            pe_down_end = r.v("PE")
            if th == 0:
                pda_sc = r.v("SC")
                pda_dve = 0
                pda_st = 0
                for tt in range(8):
                    sv = PBF[:, 73728:75776]
                    gv = PBF[:, 75776:77824]
                    xv = PF[:, 2048:4096]
                    tv = PF[:, 4096:6144]
                    w = [("CC2", 1), ("SC", pda_sc),
                         ("SC", sc_sp_hist[1][31])]
                    if tt:
                        w = [("DVE", pda_dve), ("SC", pda_st)]
                    r.dma("sync", sv,
                          s_own.ap()[tt * 128:(tt + 1) * 128, :], wait=w)
                    r.dma("sync", gv, gsc.ap()[tt * 128:(tt + 1) * 128, :])
                    r.dma("sync", xv, x_own[tt * 128:(tt + 1) * 128, :])
                    ld = r.v("SQ")
                    r.emit("vector", "tensor_tensor", tv, sv, gv, ALU.mult,
                           wait=(("SQ", ld),))
                    r.emit("vector", "tensor_add", xv, xv, tv,
                           inc=("DVE", 1))
                    pda_dve = r.v("DVE")
                    r.dma("scalar", out_t[tt * 128:(tt + 1) * 128, :], xv,
                          wait=(("DVE", pda_dve),))
                    pda_st = r.v("SC")

        r.barrier()

        sc_out = [0] * 16
        dve_pd = [0] * 16
        for tt in range(8, 16):
            par = tt % 2
            sv = PBF[:, 4096 + par * 2048:4096 + (par + 1) * 2048]
            gv = PBF[:, par * 2048:(par + 1) * 2048]
            xv = PF[:, par * 2048:(par + 1) * 2048]
            tv = PF[:, 4096 + par * 2048:4096 + (par + 1) * 2048]
            w = [("CC2", 1)] if tt == 8 else []
            if tt >= 10:
                w.append(("DVE", dve_pd[tt - 2]))
                w.append(("SC", sc_out[tt - 2]))
            r.dma("sync", sv, s_own.ap()[tt * 128:(tt + 1) * 128, :], wait=w)
            r.dma("sync", gv, gsc.ap()[tt * 128:(tt + 1) * 128, :])
            r.dma("sync", xv, x_own[tt * 128:(tt + 1) * 128, :])
            ld = r.v("SQ")
            r.emit("vector", "tensor_tensor", tv, sv, gv, ALU.mult,
                   wait=(("SQ", ld),))
            r.emit("vector", "tensor_add", xv, xv, tv, inc=("DVE", 1))
            dve_pd[tt] = r.v("DVE")
            r.dma("scalar", out_t[tt * 128:(tt + 1) * 128, :], xv,
                  wait=(("DVE", dve_pd[tt]),))
            sc_out[tt] = r.v("SC")

        if dbg_outs:
            r.barrier()
            for nm, dap in dbg_outs.items():
                if nm in dbg_src:
                    r.dma("gpsimd", dap, dbg_src[nm].ap())

        r.barrier()

        r.finalize()

        def mk(ename):
            stream = r.streams[ename]

            def play(eng):
                for (op, args, kw, waits, inc, _pre) in stream:
                    for (s, vv) in waits:
                        eng.wait_ge(sems[s], vv)
                    if op is None:
                        continue
                    inst = getattr(eng, op)(*args, **kw)
                    if inc is not None:
                        inst.then_inc(sems[inc[0]], inc[1])
            return play

        block.sync(mk("sync"))
        block.scalar(mk("scalar"))
        block.vector(mk("vector"))
        block.tensor(mk("tensor"))
        block.gpsimd(mk("gpsimd"))

    return nc


def _host_consts():
    k = np.arange(64)
    inv = ROPE_BASE ** (-2.0 * k / D)
    didx = np.concatenate([k, k])
    ang = np.arange(T, dtype=np.float64)[None, :] * inv[didx][:, None]
    cos_t = np.cos(ang).astype(np.float32)
    sin_t = np.sin(ang).astype(np.float32)
    m = np.zeros((128, 2048), dtype=np.float32)
    p = np.arange(128)[:, None]
    c = np.arange(512)[None, :]
    for j in range(4):
        m[:, j * 512:(j + 1) * 512] = (c >= p + 128 * j).astype(np.float32)
    masks = m.astype(BF)
    P = np.zeros((128, 128), dtype=np.float32)
    P[np.arange(64), np.arange(64) + 64] = -1.0
    P[np.arange(64) + 64, np.arange(64)] = 1.0
    protT = np.ascontiguousarray(P.T).astype(BF)
    ident = np.eye(128, dtype=np.float32).astype(BF)
    ones_col = np.ones((128, 1), dtype=np.float32).astype(BF)
    ones128 = np.ones((128, 128), dtype=np.float32).astype(BF)
    ones_row_f = np.ones((1, 128), dtype=np.float32)
    return cos_t, sin_t, masks, protT, ident, ones_col, ones128, ones_row_f


def make_in_maps(L0_attn_x, L0_attn_wq, L0_attn_wk, L0_attn_wv, L0_attn_wo,
                 L0_ffn_gate_w, L0_ffn_up_w, L0_ffn_down_w,
                 L0_ffn_gate_s, L0_ffn_up_s, L0_ffn_down_s):
    f32 = np.float32
    x2 = np.asarray(L0_attn_x, f32).reshape(R, E)
    wq = np.asarray(L0_attn_wq, f32)
    wk = np.asarray(L0_attn_wk, f32)
    wv = np.asarray(L0_attn_wv, f32)
    wo = np.asarray(L0_attn_wo, f32)
    gw = np.asarray(L0_ffn_gate_w, f32)
    uw = np.asarray(L0_ffn_up_w, f32)
    dw = np.asarray(L0_ffn_down_w, f32)
    gs = np.asarray(L0_ffn_gate_s, f32)
    us = np.asarray(L0_ffn_up_s, f32)
    ds = np.asarray(L0_ffn_down_s, f32)

    gate_hh = np.ascontiguousarray(
        gw.reshape(NIT, 128, 16, 128).transpose(0, 3, 2, 1).reshape(
            NIT, 128, 2048))
    up_hh = np.ascontiguousarray(
        uw.reshape(NIT, 128, 16, 128).transpose(0, 3, 2, 1).reshape(
            NIT, 128, 2048))
    down_hh = np.ascontiguousarray(
        dw.reshape(16, 128, NIT, 128).transpose(0, 3, 2, 1).reshape(
            16, 128, NIT * 128))
    gate_s2 = np.ascontiguousarray(gs.reshape(NIT, 128).T)
    up_s2 = np.ascontiguousarray(us.reshape(NIT, 128).T)
    down_s2 = np.ascontiguousarray(ds.reshape(16, 128).T)
    (cos_t, sin_t, masks, protT, ident, ones_col, ones128,
     ones_row_f) = _host_consts()

    in_maps = []
    for r_ in range(NCORES):
        f0 = 256 * r_
        wq_hc = np.ascontiguousarray(
            wq[f0:f0 + 256, :].reshape(256, 16, 128).transpose(2, 1, 0)
            .reshape(128, 4096))
        wk_hc = np.ascontiguousarray(
            wk[f0:f0 + 256, :].reshape(256, 16, 128).transpose(2, 1, 0)
            .reshape(128, 4096))
        wv_hc = np.ascontiguousarray(
            wv[f0:f0 + 256, :].reshape(256, 16, 128).transpose(2, 1, 0)
            .reshape(128, 4096))
        wo_hc = np.ascontiguousarray(
            wo[:, f0:f0 + 256].T.reshape(2, 128, 2048).transpose(1, 0, 2)
            .reshape(128, 4096))
        in_maps.append({
            "x_own": np.ascontiguousarray(x2[RPC * r_:RPC * (r_ + 1), :]),
            "wq_h": wq_hc, "wk_h": wk_hc, "wv_h": wv_hc, "wo_h": wo_hc,
            "gate_h": gate_hh, "up_h": up_hh, "down_h": down_hh,
            "gate_s2": gate_s2, "up_s2": up_s2, "down_s2": down_s2,
            "cos_t": cos_t, "sin_t": sin_t, "masks": masks,
            "protT": protT, "ident": ident, "ones_col": ones_col,
            "ones128": ones128, "ones_row_f": ones_row_f,
        })
    return in_maps


def kernel(**inputs):
    nc = build_program()
    in_maps = make_in_maps(**inputs)
    res = run_bass_kernel_spmd(nc, in_maps, core_ids=list(range(NCORES)))
    out = np.concatenate([res.results[i]["out"] for i in range(NCORES)],
                         axis=0)
    return out.reshape(B, T, E)


# version 15
# speedup vs baseline: 1.1315x; 1.0650x over previous
import math
import numpy as np
import ml_dtypes
import concourse.bass as bass
import concourse.mybir as mybir
from concourse.bass_utils import run_bass_kernel_spmd
from concourse.replica_groups import maybe_share_collective_output_space

F32 = mybir.dt.float32
BF16 = mybir.dt.bfloat16
AF = mybir.ActivationFunctionType
ALU = mybir.AluOpType
AXX = mybir.AxisListType.X
BF = ml_dtypes.bfloat16

NCORES = 8
B, T, E = 2, 8192, 2048
H, D = 16, 128
INTER = 5632
R = B * T
RPC = R // NCORES
NIT = INTER // 128
ROPE_BASE = 1e6
ISD = 1.0 / math.sqrt(D)
EPS = 1e-5

ENGINES = ("sync", "scalar", "vector", "tensor", "gpsimd")


class Rec:

    def __init__(self):
        self.streams = {e: [] for e in ENGINES}
        self.cnt = {}
        self.nbar = 0

    def v(self, s):
        return self.cnt.get(s, 0)

    def emit(self, engine, op, *args, wait=(), inc=None, **kw):
        wait = tuple((s, n) for (s, n) in wait if n and n > 0)
        self.streams[engine].append([op, args, kw, wait, inc, None])
        if inc is not None:
            s, n = inc
            self.cnt[s] = self.cnt.get(s, 0) + n
            return self.cnt[s]
        return None

    def dma(self, q, out, in_, wait=(), **kw):
        sem = {"sync": "SQ", "scalar": "SC", "gpsimd": "GQ"}[q]
        pre = self.v(sem)
        v = self.emit(q, "dma_start", wait=wait, inc=(sem, 16),
                      out=out, in_=in_, **kw)
        self.streams[q][-1][5] = (sem, pre)
        return v

    def finalize(self):
        qsems = ("SQ", "SC", "GQ")
        waited = {s: set() for s in qsems}
        for e in ENGINES:
            for it in self.streams[e]:
                for (s, n) in it[3]:
                    if s in qsems:
                        waited[s].add(n)
        wsorted = {s: sorted(waited[s]) for s in qsems}
        import bisect
        for e in ENGINES:
            level = {s: 0 for s in qsems}
            for it in self.streams[e]:
                for (s, n) in it[3]:
                    if s in qsems and n > level[s]:
                        level[s] = n
                if it[5] is not None:
                    s, pre = it[5]
                    ws = wsorted[s]
                    i = bisect.bisect_right(ws, pre)
                    tgt = ws[i - 1] if i > 0 else 0
                    if tgt > level[s]:
                        it[3] = ((s, tgt),) + it[3]
                        level[s] = tgt

    def barrier(self):
        self.nbar += 1
        flush = {"sync": "SQ", "scalar": "SC", "gpsimd": "GQ"}
        own = {"scalar": "ACT", "vector": "DVE", "tensor": "PE"}
        for e in ENGINES:
            w = []
            if e in flush:
                w.append((flush[e], self.v(flush[e])))
            if e in own:
                s = own[e]
                for it in reversed(self.streams[e]):
                    if it[0] is None or it[0] == "nop":
                        continue
                    if it[4] is None:
                        it[4] = (s, 1)
                        self.cnt[s] = self.cnt.get(s, 0) + 1
                    break
                w.append((s, self.v(s)))
            self.emit(e, "nop", wait=tuple(w), inc=("BAR", 1))
        tgt = self.v("BAR")
        assert tgt == 5 * self.nbar
        for e in ENGINES:
            self.emit(e, None, wait=(("BAR", tgt),))


def build_program(debug_names=(), race_detect=True):
    nc = bass.Bass("TRN2", target_bir_lowering=False, debug=False,
                   detect_race_conditions=race_detect)

    def din(name, shape, dt=F32):
        return nc.dram_tensor(name, list(shape), dt, kind="ExternalInput").ap()

    x_own = din("x_own", (RPC, E))
    wq_h = din("wq_h", (128, 4096))
    wk_h = din("wk_h", (128, 4096))
    wv_h = din("wv_h", (128, 4096))
    wo_h = din("wo_h", (128, 4096))
    gate_h = din("gate_h", (NIT, 128, 2048))
    up_h = din("up_h", (NIT, 128, 2048))
    down_h = din("down_h", (16, 128, NIT * 128))
    gate_s2 = din("gate_s2", (128, NIT))
    up_s2 = din("up_s2", (128, NIT))
    down_s2 = din("down_s2", (128, 16))
    cos_t = din("cos_t", (128, T))
    sin_t = din("sin_t", (128, T))
    masks = din("masks", (128, 2048), BF16)
    protT = din("protT", (128, 128), BF16)
    ident = din("ident", (128, 128), BF16)
    ones_col = din("ones_col", (128, 1), BF16)
    ones128 = din("ones128", (128, 128), BF16)
    ones_row_f = din("ones_row_f", (1, 128), F32)

    out_t = nc.dram_tensor("out", [RPC, E], F32, kind="ExternalOutput").ap()

    rg = [list(range(NCORES))]
    ag_space = maybe_share_collective_output_space("AllGather", rg)
    rs_space = maybe_share_collective_output_space("ReduceScatter", rg)
    ag1inA = nc.dram_tensor("ag1inA", [E, RPC // 2], BF16)
    ag1inB = nc.dram_tensor("ag1inB", [E, RPC // 2], BF16)
    ag1outA = nc.dram_tensor("ag1outA", [R, RPC // 2], BF16,
                             addr_space=ag_space)
    ag1outB = nc.dram_tensor("ag1outB", [R, RPC // 2], BF16,
                             addr_space=ag_space)
    qs = nc.dram_tensor("qs", [256, R], BF16)
    ks = nc.dram_tensor("ks", [256, R], BF16)
    vs = nc.dram_tensor("vs", [2, 2, 128, T], BF16)
    spart = nc.dram_tensor("spart", [R, E], BF16)
    s_own = nc.dram_tensor("s_own", [RPC, E], BF16, addr_space=rs_space)
    gsc = nc.dram_tensor("gsc", [RPC, E], BF16)
    gwb = nc.dram_tensor("gwb", [NIT, 128, 2048], BF16)
    uwb = nc.dram_tensor("uwb", [NIT, 128, 2048], BF16)
    dwb = nc.dram_tensor("dwb", [16, 128, NIT * 128], BF16)

    dbg_src = {"ag1in": ag1inA, "ag1out": ag1outA, "qs": qs, "ks": ks,
               "vs": vs, "spart": spart, "s_own": s_own, "gsc": gsc}
    dbg_outs = {}
    for nm in debug_names:
        if nm == "norm0":
            dbg_outs["norm0_sq2"] = nc.dram_tensor(
                "dbg_norm0_sq2", [128, 512], BF16, kind="ExternalOutput").ap()
            dbg_outs["norm0_row"] = nc.dram_tensor(
                "dbg_norm0_row", [1, 512], F32, kind="ExternalOutput").ap()
            dbg_outs["norm0_qrt"] = nc.dram_tensor(
                "dbg_norm0_qrt", [128, 1024], BF16, kind="ExternalOutput").ap()
            continue
        if nm == "bias4":
            dbg_outs["bias4"] = nc.dram_tensor(
                "dbg_bias4", [128, 4], F32, kind="ExternalOutput").ap()
            dbg_outs["runmax"] = nc.dram_tensor(
                "dbg_runmax", [1, 16], F32, kind="ExternalOutput").ap()
            dbg_outs["negm"] = nc.dram_tensor(
                "dbg_negm", [1, 8], F32, kind="ExternalOutput").ap()
            continue
        h = dbg_src[nm]
        dbg_outs[nm] = nc.dram_tensor(
            "dbg_" + nm, list(h.shape), h.dtype, kind="ExternalOutput").ap()

    r = Rec()

    import contextlib
    with contextlib.ExitStack() as ctx:
        e = ctx.enter_context
        PBF = e(nc.sbuf_tensor("PBF", [128, 77824], BF16))
        PF = e(nc.sbuf_tensor("PF", [128, 8192], F32))
        mask_sb = e(nc.sbuf_tensor("mask_sb", [128, 2048], BF16))
        protT_sb = e(nc.sbuf_tensor("protT_sb", [128, 128], BF16))
        ident_sb = e(nc.sbuf_tensor("ident_sb", [128, 128], BF16))
        ones_col_sb = e(nc.sbuf_tensor("ones_col_sb", [128, 1], BF16))
        ones128_sb = e(nc.sbuf_tensor("ones128_sb", [128, 128], BF16))
        ones_row_f_sb = e(nc.sbuf_tensor("ones_row_f_sb", [1, 128], F32))
        gs2_sb = e(nc.sbuf_tensor("gs2_sb", [128, NIT], F32))
        us2_sb = e(nc.sbuf_tensor("us2_sb", [128, NIT], F32))
        ds2_sb = e(nc.sbuf_tensor("ds2_sb", [128, 16], F32))
        wosl_sb = e(nc.sbuf_tensor("wosl_sb", [128, 4096], BF16))
        stats = e(nc.sbuf_tensor("stats", [128, 16], F32))
        runmax2 = e(nc.sbuf_tensor("runmax2", [128, 16], F32))
        stats2m = e(nc.sbuf_tensor("stats2m", [128, 16], F32))
        runmax = e(nc.sbuf_tensor("runmax", [1, 16], F32))
        negm = e(nc.sbuf_tensor("negm", [1, 8], F32))
        bias4 = e(nc.sbuf_tensor("bias4", [128, 4], F32))
        rsum = e(nc.sbuf_tensor("rsum", [1, 512], F32))
        PSF = e(nc.psum_tensor("PSF", [128, 3072], F32))
        PTA = e(nc.psum_tensor("PTA", [128, 1024], BF16))
        PTB = e(nc.psum_tensor("PTB", [128, 1024], BF16))
        hSQ = e(nc.semaphore("SQ"))
        hSC = e(nc.semaphore("SC"))
        hGQ = e(nc.semaphore("GQ"))
        hPE = e(nc.semaphore("PE"))
        hACT = e(nc.semaphore("ACT"))
        hDVE = e(nc.semaphore("DVE"))
        hCC1 = e(nc.semaphore("CC1"))
        hCC2 = e(nc.semaphore("CC2"))
        hCC3 = e(nc.semaphore("CC3"))
        hBAR = e(nc.semaphore("BAR"))
        block = e(nc.Block())
        sems = {"SQ": hSQ, "SC": hSC, "GQ": hGQ, "PE": hPE, "ACT": hACT,
                "DVE": hDVE, "CC1": hCC1, "CC2": hCC2, "CC3": hCC3, "BAR": hBAR}

        def bank(i, n=512):
            return PSF[:, 512 * i:512 * i + n]

        def dap(t, off, dims):
            return bass.AP(t, off, [list(d) for d in dims])

        r.dma("sync", mask_sb[:, :], masks)
        r.dma("sync", protT_sb[:, :], protT)
        r.dma("sync", ident_sb[:, :], ident)
        r.dma("sync", ones_col_sb[:, :], ones_col)
        r.dma("sync", ones128_sb[:, :], ones128)
        r.dma("sync", ones_row_f_sb[:, :], ones_row_f)
        r.dma("sync", gs2_sb[:, :], gate_s2)
        r.dma("sync", us2_sb[:, :], up_s2)
        r.dma("sync", ds2_sb[:, :], down_s2)
        p0 = r.v("SQ")
        r.emit("vector", "memset", runmax[0:1, :], 0.0, inc=("DVE", 1))

        ln_done = [0] * 16
        tr_done = [0] * 16
        ev_done = [[0, 0] for _ in range(16)]
        st_done = [0] * 16
        red_done = [0] * 16
        sq_done = [0] * 16
        for i in range(16):
            p = i % 2
            xt = PF[:, p * 2048:(p + 1) * 2048]
            xsq = PF[:, 4096 + p * 2048:4096 + (p + 1) * 2048]
            w = [("SQ", p0)] if i == 0 else []
            if i >= 2:
                w += [("ACT", sq_done[i - 2]), ("DVE", red_done[i - 2])]
            r.dma("sync", xt, x_own[128 * i:128 * (i + 1), :], wait=w)
            ld = r.v("SQ")
            r.emit("vector", "reduce_sum", stats[:, i:i + 1], xt, AXX,
                   wait=(("SQ", ld),), inc=("DVE", 1))
            r.emit("scalar", "activation", xsq, xt, AF.Square,
                   wait=(("SQ", ld),), inc=("ACT", 1))
            sq_done[i] = r.v("ACT")
            r.emit("vector", "reduce_sum", runmax2[:, i:i + 1], xsq, AXX,
                   wait=(("ACT", sq_done[i]),), inc=("DVE", 1))
            red_done[i] = r.v("DVE")
        r.emit("vector", "tensor_scalar_mul", stats[:, 0:16],
               stats[:, 0:16], 1.0 / E, wait=(("DVE", red_done[15]),),
               inc=("DVE", 1))
        c1 = r.v("DVE")
        r.emit("vector", "tensor_scalar_mul", runmax2[:, 0:16],
               runmax2[:, 0:16], 1.0 / E, wait=(("DVE", c1),),
               inc=("DVE", 1))
        c2 = r.v("DVE")
        r.emit("vector", "tensor_tensor", stats2m[:, 0:16],
               stats[:, 0:16], stats[:, 0:16], ALU.mult,
               wait=(("DVE", c2),), inc=("DVE", 1))
        c3 = r.v("DVE")
        r.emit("vector", "tensor_sub", stats2m[:, 0:16],
               runmax2[:, 0:16], stats2m[:, 0:16],
               wait=(("DVE", c3),), inc=("DVE", 1))
        c4 = r.v("DVE")
        r.emit("vector", "tensor_scalar_add", stats2m[:, 0:16],
               stats2m[:, 0:16], EPS, wait=(("DVE", c4),), inc=("DVE", 1))
        c5 = r.v("DVE")
        r.emit("scalar", "activation", stats2m[:, 0:16], stats2m[:, 0:16],
               AF.Ln, wait=(("DVE", c5),))
        r.emit("scalar", "activation", stats2m[:, 0:16], stats2m[:, 0:16],
               AF.Exp, scale=-0.5, inc=("ACT", 1))
        rstd_all = r.v("ACT")
        r.emit("vector", "tensor_scalar_mul", stats[:, 0:16],
               stats[:, 0:16], -1.0, wait=(("DVE", c5),), inc=("DVE", 1))
        negm_all = r.v("DVE")
        for i in range(16):
            p = i % 2
            xt = PF[:, p * 2048:(p + 1) * 2048]
            xnv = PBF[:, 36864 + p * 2048:36864 + (p + 1) * 2048]
            stg = PBF[:, 40960 + p * 2048:40960 + (p + 1) * 2048]
            w = []
            if i < 2:
                w.append(("DVE", negm_all))
            else:
                w.append(("DVE", ln_done[i - 2]))
            r.dma("sync", xt, x_own[128 * i:128 * (i + 1), :], wait=w)
            ld = r.v("SQ")
            w = [("SQ", ld), ("ACT", rstd_all), ("DVE", negm_all)]
            if i >= 2:
                w.append(("PE", tr_done[i - 2]))
            r.emit("vector", "tensor_scalar", xnv, xt,
                   stats[:, i:i + 1], stats2m[:, i:i + 1],
                   ALU.add, ALU.mult, wait=tuple(w), inc=("DVE", 1))
            ln_done[i] = r.v("DVE")
            for half, pt in enumerate((PTA, PTB)):
                w = [("DVE", ln_done[i])]
                if i >= 1:
                    w.append(("DVE", ev_done[i - 1][half]))
                for j in range(8):
                    et = half * 8 + j
                    r.emit("tensor", "transpose",
                           pt[:, 128 * j:128 * (j + 1)],
                           xnv[:, 128 * et:128 * (et + 1)],
                           ident_sb[:, :],
                           wait=tuple(w) if j == 0 else (),
                           inc=(("PE", 1) if j == 7 else None))
                trh = r.v("PE")
                w2 = [("PE", trh)]
                if i >= 2 and half == 0:
                    w2.append(("SC", st_done[i - 2]))
                r.emit("vector", "tensor_copy",
                       stg[:, half * 1024:(half + 1) * 1024], pt[:, :],
                       wait=tuple(w2), inc=("DVE", 1))
                ev_done[i][half] = r.v("DVE")
            tr_done[i] = r.v("PE")
            agt = ag1inA if i < 8 else ag1inB
            dst = dap(agt, 128 * (i % 8),
                      [(RPC // 2, 128), (128 * RPC // 2, 16), (1, 128)])
            r.dma("scalar", dst, stg, wait=(("DVE", ev_done[i][1]),))
            st_done[i] = r.v("SC")

        r.emit("gpsimd", "collective_compute", "AllGather", ALU.bypass,
               replica_groups=[list(range(NCORES))],
               ins=[ag1inA.ap().opt()], outs=[ag1outA.ap().opt()],
               wait=(("SC", st_done[7]),), inc=("CC1", 1))
        r.emit("gpsimd", "collective_compute", "AllGather", ALU.bypass,
               replica_groups=[list(range(NCORES))],
               ins=[ag1inB.ap().opt()], outs=[ag1outB.ap().opt()],
               wait=(("SC", st_done[15]),), inc=("CC3", 1))
        r.dma("gpsimd", PBF[:, 16384:20480], wq_h)
        r.dma("gpsimd", PBF[:, 20480:24576], wk_h)
        r.dma("gpsimd", PBF[:, 24576:28672], wv_h)
        r.dma("gpsimd", wosl_sb[:, :], wo_h)
        gqw = r.v("GQ")

        qraw = PBF[:, 28672:29696]
        kraw = PBF[:, 29696:30720]
        qrt = PBF[:, 30720:31744]
        krt = PBF[:, 31744:32768]
        sq2 = PBF[:, 32768:33280]
        vtile = PBF[:, 33280:34304]
        pe_xnb = [0] * 32
        dve_rope = [0] * 32
        sc_pb = [0] * 32
        prev_v_evac = 0
        prev_ropeq = 0
        prev_ropek = 0
        prev_norm_mm = 0
        prev_norm_red = 0
        tb_order = ([tb for tb in range(32) if tb % 4 < 2]
                    + [tb for tb in range(32) if tb % 4 >= 2])
        for ti, tb in enumerate(tb_order):
            par = ti % 2
            b = tb // 16
            R0 = 512 * tb
            tbase = R0 % T
            blk = tb // 4
            tl0 = (tb % 4) * 512
            half_b = tl0 >= 1024
            xnb = PBF[:, par * 8192:(par + 1) * 8192]
            ct = PF[:, par * 512:(par + 1) * 512]
            st = PF[:, 1024 + par * 512:1024 + (par + 1) * 512]
            tmp1 = PF[:, 2048:2560]
            tmp2 = PF[:, 2560:3072]
            w = []
            if ti == 0:
                w.append(("CC1", 1))
            if ti == 16:
                w.append(("CC3", 1))
            if ti >= 2:
                w.append(("PE", pe_xnb[ti - 2]))
            agsrc = ag1outB if half_b else ag1outA
            src = dap(agsrc, blk * 2048 * (RPC // 2) + (tl0 % 1024),
                      [(RPC // 2, 128), (128 * RPC // 2, 16), (1, 512)])
            r.dma("sync", xnb, src, wait=w)
            w = [("DVE", dve_rope[ti - 2])] if ti >= 2 else [
                ("DVE", ln_done[15])]
            r.dma("sync", ct, cos_t[:, tbase:tbase + 512], wait=w)
            r.dma("sync", st, sin_t[:, tbase:tbase + 512])
            ldb = r.v("SQ")
            w = [("SQ", ldb), ("GQ", gqw), ("DVE", prev_ropek)]
            for fl in range(2):
                for et in range(16):
                    r.emit("tensor", "matmul", bank(fl),
                           PBF[:, 16384 + et * 256 + fl * 128:
                               16384 + et * 256 + fl * 128 + 128],
                           xnb[:, et * 512:(et + 1) * 512],
                           start=(et == 0), stop=(et == 15),
                           wait=tuple(w) if (fl == 0 and et == 0) else (),
                           inc=(("PE", 1) if (fl == 1 and et == 15) else None))
            qmm = r.v("PE")
            w = [("ACT", prev_v_evac)]
            for fl in range(2):
                for et in range(16):
                    r.emit("tensor", "matmul", bank(4 + fl),
                           PBF[:, 20480 + et * 256 + fl * 128:
                               20480 + et * 256 + fl * 128 + 128],
                           xnb[:, et * 512:(et + 1) * 512],
                           start=(et == 0), stop=(et == 15),
                           wait=tuple(w) if (fl == 0 and et == 0) else (),
                           inc=(("PE", 1) if (fl == 1 and et == 15) else None))
            kmm = r.v("PE")
            w = [("PE", qmm), ("DVE", prev_ropeq)]
            r.emit("scalar", "activation", qraw, PSF[:, 0:1024], AF.Copy,
                   scale=ISD, wait=tuple(w), inc=("ACT", 1))
            q_evac = r.v("ACT")
            w = [("ACT", q_evac), ("DVE", prev_norm_red)]
            for fl in range(2):
                r.emit("tensor", "matmul", bank(2 + fl), protT_sb[:, :],
                       qraw[:, fl * 512:(fl + 1) * 512],
                       start=True, stop=True,
                       wait=tuple(w) if fl == 0 else (),
                       inc=(("PE", 1) if fl == 1 else None))
            rotq_mm = r.v("PE")
            w = [("PE", kmm), ("DVE", prev_ropek)]
            r.emit("scalar", "activation", kraw, PSF[:, 2048:3072], AF.Copy,
                   wait=tuple(w), inc=("ACT", 1))
            k_evac = r.v("ACT")
            w = [("PE", rotq_mm), ("SQ", ldb),
                 ("SC", sc_pb[ti - 1] if ti else 0)]
            for fl in range(2):
                r.emit("vector", "tensor_tensor", tmp1,
                       qraw[:, fl * 512:(fl + 1) * 512], ct, ALU.mult,
                       wait=tuple(w) if fl == 0 else ())
                r.emit("vector", "tensor_tensor", tmp2, bank(2 + fl), st,
                       ALU.mult)
                r.emit("vector", "tensor_add",
                       qrt[:, fl * 512:(fl + 1) * 512], tmp1, tmp2,
                       inc=("DVE", 1))
            ropeq = r.v("DVE")
            prev_ropeq = ropeq
            w = [("ACT", k_evac)]
            for fl in range(2):
                r.emit("tensor", "matmul", bank(fl), protT_sb[:, :],
                       kraw[:, fl * 512:(fl + 1) * 512],
                       start=True, stop=True,
                       wait=tuple(w) if fl == 0 else (),
                       inc=(("PE", 1) if fl == 1 else None))
            rotk_mm = r.v("PE")
            w = [("ACT", k_evac)]
            for tt in range(4):
                bk = PSF[:, 2048 + tt * 256:2048 + (tt + 1) * 256]
                for et in range(16):
                    r.emit("tensor", "matmul", bk,
                           xnb[:, et * 512 + tt * 128:
                               et * 512 + tt * 128 + 128],
                           PBF[:, 24576 + et * 256:24576 + (et + 1) * 256],
                           start=(et == 0), stop=(et == 15),
                           wait=tuple(w) if (tt == 0 and et == 0) else (),
                           inc=(("PE", 1) if (tt == 3 and et == 15) else None))
            vmm = r.v("PE")
            pe_xnb[ti] = vmm
            w = [("PE", rotk_mm)]
            for fl in range(2):
                r.emit("vector", "tensor_tensor", tmp1,
                       kraw[:, fl * 512:(fl + 1) * 512], ct, ALU.mult,
                       wait=tuple(w) if fl == 0 else ())
                r.emit("vector", "tensor_tensor", tmp2, bank(fl), st,
                       ALU.mult)
                r.emit("vector", "tensor_add",
                       krt[:, fl * 512:(fl + 1) * 512], tmp1, tmp2,
                       inc=("DVE", 1))
            ropek = r.v("DVE")
            prev_ropek = ropek
            dve_rope[ti] = ropek
            w = [("PE", vmm), ("SC", sc_pb[ti - 1] if ti else 0)]
            r.emit("scalar", "activation", vtile, PSF[:, 2048:3072], AF.Copy,
                   wait=tuple(w), inc=("ACT", 1))
            prev_v_evac = r.v("ACT")
            for pi, (srcv, slot0) in enumerate(((qrt, 0), (krt, 4))):
                for fl in range(2):
                    slot = slot0 + b * 2 + fl
                    w = [("PE", prev_norm_mm)]
                    r.emit("vector", "tensor_tensor", sq2,
                           srcv[:, fl * 512:(fl + 1) * 512],
                           srcv[:, fl * 512:(fl + 1) * 512], ALU.mult,
                           wait=tuple(w), inc=("DVE", 1))
                    sqd2 = r.v("DVE")
                    r.emit("tensor", "matmul", PSF[0:1, 1024:1536],
                           ones_col_sb[:, :], sq2, start=True, stop=True,
                           wait=(("DVE", sqd2),), inc=("PE", 1))
                    prev_norm_mm = r.v("PE")
                    if tb == 0 and pi == 0 and fl == 0 and "norm0" in debug_names:
                        r.emit("vector", "tensor_copy", PBF[:, 73728:74240],
                               sq2)
                        r.emit("vector", "tensor_copy", PBF[:, 74240:75264],
                               qrt)
                        r.emit("vector", "tensor_copy", PF[0:1, 3072:3584],
                               PSF[0:1, 1024:1536],
                               wait=(("PE", prev_norm_mm),), inc=("DVE", 1))
                    r.emit("vector", "reduce_max", stats[0:1, 15:16],
                           PSF[0:1, 1024:1536], AXX,
                           wait=(("PE", prev_norm_mm),
                                 ("DVE", prev_norm_red)),
                           inc=("DVE", 1))
                    rmx = r.v("DVE")
                    r.emit("vector", "tensor_max",
                           runmax[0:1, slot:slot + 1],
                           runmax[0:1, slot:slot + 1], stats[0:1, 15:16],
                           wait=(("DVE", rmx),), inc=("DVE", 1))
                    prev_norm_red = r.v("DVE")
            dstq = dap(qs, R0, [(R, 128), (128 * R, 2), (1, 512)])
            r.dma("scalar", dstq, qrt, wait=(("DVE", ropeq),))
            dstk = dap(ks, R0, [(R, 128), (128 * R, 2), (1, 512)])
            r.dma("scalar", dstk, krt, wait=(("DVE", ropek),))
            ktg0 = (R0 % T) // 128
            for hl in range(2):
                dstv = dap(vs, (b * 2 + hl) * 128 * T + ktg0 * 128,
                           [(T, 128), (128, 4), (1, 128)])
                srcv2 = bass.AP(PBF, 33280 + hl * 128,
                                [[77824, 128], [256, 4], [1, 128]])
                r.dma("scalar", dstv, srcv2,
                      wait=(("ACT", prev_v_evac),) if hl == 0 else ())
            sc_pb[ti] = r.v("SC")

        for u in range(4):
            bu, fl = u // 2, u % 2
            qslot = bu * 2 + fl
            kslot = 4 + bu * 2 + fl
            w = [("DVE", prev_norm_red)] if u == 0 else []
            r.emit("vector", "tensor_tensor", negm[0:1, u:u + 1],
                   runmax[0:1, qslot:qslot + 1],
                   runmax[0:1, kslot:kslot + 1],
                   ALU.mult, wait=tuple(w), inc=("DVE", 1))
            nd = r.v("DVE")
            r.emit("scalar", "activation", negm[0:1, u:u + 1],
                   negm[0:1, u:u + 1], AF.Ln, wait=(("DVE", nd),))
            r.emit("scalar", "activation", negm[0:1, u:u + 1],
                   negm[0:1, u:u + 1], AF.Exp, scale=0.5, inc=("ACT", 1))
            ad = r.v("ACT")
            r.emit("vector", "tensor_scalar", negm[0:1, u:u + 1],
                   negm[0:1, u:u + 1], -1.001, -0.01, ALU.mult, ALU.add,
                   wait=(("ACT", ad),), inc=("DVE", 1))
            nd2 = r.v("DVE")
            r.emit("tensor", "matmul", PSF[:, 1024:1025],
                   ones_row_f_sb[0:1, :], negm[0:1, u:u + 1],
                   start=True, stop=True, wait=(("DVE", nd2),),
                   inc=("PE", 1))
            pm = r.v("PE")
            r.emit("vector", "tensor_copy", bias4[:, u:u + 1],
                   PSF[:, 1024:1025], wait=(("PE", pm),), inc=("DVE", 1))

        if "bias4" in debug_names:
            bd = r.v("DVE")
            r.dma("scalar", dbg_outs["bias4"], bias4[:, :],
                  wait=(("DVE", bd),))
            r.dma("scalar", dbg_outs["runmax"], runmax[0:1, :])
            r.dma("scalar", dbg_outs["negm"], negm[0:1, :])
        if "norm0" in debug_names:
            bd = r.v("DVE")
            r.dma("scalar", dbg_outs["norm0_sq2"], PBF[:, 73728:74240],
                  wait=(("DVE", bd),))
            r.dma("scalar", dbg_outs["norm0_qrt"], PBF[:, 74240:75264])
            r.dma("scalar", dbg_outs["norm0_row"], PF[0:1, 3072:3584])

        r.barrier()

        QOF = (0, 40960)
        KOF = (8192, 24576)
        VOF = (16384, 32768)
        AOF = (49152, 57344)
        PTOF = (65536, 67584)
        units = [(0, 0), (0, 1), (1, 0), (1, 1)]
        ld_unit = [0] * 4
        pe_unit_end = [0] * 4
        pe_op_end = [0, 0]

        def att_loads(u, extra_wait=()):
            b, hl = units[u]
            par = u % 2
            r.dma("sync", PBF[:, QOF[par]:QOF[par] + 8192],
                  qs.ap()[hl * 128:(hl + 1) * 128, b * T:(b + 1) * T],
                  wait=extra_wait)
            r.dma("sync", PBF[:, KOF[par]:KOF[par] + 8192],
                  ks.ap()[hl * 128:(hl + 1) * 128, b * T:(b + 1) * T])
            r.dma("sync", PBF[:, VOF[par]:VOF[par] + 8192],
                  vs.ap()[b, hl, :, :])
            ld_unit[u] = r.v("SQ")

        cast_w = (("SQ", r.v("SQ") + 96),)
        for lo in (0, 22):
            r.dma("gpsimd", gwb.ap()[lo:lo + 22], gate_h[lo:lo + 22],
                  wait=cast_w)
            cast_w = ()
            r.dma("gpsimd", uwb.ap()[lo:lo + 22], up_h[lo:lo + 22])
        for lo in (0, 8):
            r.dma("gpsimd", dwb.ap()[lo:lo + 8], down_h[lo:lo + 8])
        gqcast = r.v("GQ")

        att_loads(0)
        att_loads(1)

        def emit_op_tt(b, tt):
            a0 = PBF[:, AOF[0]:AOF[0] + 8192]
            a1 = PBF[:, AOF[1]:AOF[1] + 8192]
            po = tt % 4
            outst = PBF[:, 69632 + po * 2048:69632 + (po + 1) * 2048]
            stq = "sync" if b == 0 else "scalar"
            stsem = "SQ" if b == 0 else "SC"
            for fb in range(4):
                bk = bank(fb % 2)
                w = [("DVE", dve_out_evac[fb % 2]),
                     ("ACT", act_out_evac[0])]
                if tt == 0 and fb == 0:
                    w.append(("DVE", ao_hist[0]))
                    w.append(("DVE", ao_hist[1]))
                    w.append(("GQ", gqw))
                    w.append(("ACT", exp_hist[0]))
                    w.append(("ACT", exp_hist[1]))
                for et, asrc in enumerate((a0, a1)):
                    r.emit("tensor", "matmul", bk,
                           asrc[:, tt * 128:(tt + 1) * 128],
                           wosl_sb[:, et * 2048 + fb * 512:
                                   et * 2048 + fb * 512 + 512],
                           start=(et == 0), stop=(et == 1),
                           wait=tuple(w) if et == 0 else (),
                           inc=(("PE", 1) if et == 1 else None))
                pmm = r.v("PE")
                w2 = [("PE", pmm)]
                if tt >= 4 and fb < 2:
                    w2.append((stsem, sc_sp_hist[b][(tt - 4) // 2]))
                if fb % 2 == 0:
                    r.emit("vector", "tensor_copy",
                           outst[:, fb * 512:(fb + 1) * 512], bk,
                           wait=tuple(w2), inc=("DVE", 1))
                    dve_out_evac[0] = r.v("DVE")
                else:
                    r.emit("scalar", "activation",
                           outst[:, fb * 512:(fb + 1) * 512], bk, AF.Copy,
                           wait=tuple(w2), inc=("ACT", 1))
                    act_out_evac[0] = r.v("ACT")
            if tt % 2 == 1:
                dst = dap(spart, (b * T + (tt - 1) * 128) * E,
                          [(E, 128), (128 * E, 2), (1, E)])
                so = 69632 + (tt - 1) % 4 * 2048
                r.dma(stq, dst, PBF[:, so:so + 4096],
                      wait=(("DVE", dve_out_evac[0]),
                            ("ACT", act_out_evac[0])))
                sc_sp_hist[b][tt // 2] = r.v(stsem)
        exp_hist = [0, 0]
        exp_h0 = 0
        pvmm_hist = [0, 0]
        ao_hist = [0, 0]
        pv_bank_free = 0
        sum_bank_free = 0
        dve_out_evac = [0, 0]
        act_out_evac = [0]
        sc_sp_hist = {0: [0] * 64, 1: [0] * 64}
        pair_idx = 0
        for u in range(4):
            b, hl = units[u]
            par = u % 2
            qu = PBF[:, QOF[par]:QOF[par] + 8192]
            ku = PBF[:, KOF[par]:KOF[par] + 8192]
            vu = PBF[:, VOF[par]:VOF[par] + 8192]
            ao = PBF[:, AOF[hl]:AOF[hl] + 8192]
            first_ao = True
            for qb in range(16):
                nq = qb + 1
                gate_ws = [None] * nq

                def emit_spv(g, last_pp):
                    ptg = PBF[:, PTOF[last_pp]:PTOF[last_pp] + 2048]
                    w = [gate_ws[g]]
                    if g == 0:
                        w.append(("ACT", sum_bank_free))
                        w.append(("DVE", pv_bank_free))
                    for j in range(4):
                        kb = 4 * g + j
                        r.emit("tensor", "matmul", PSF[:, 2560:3072],
                               ones128_sb[:, :],
                               ptg[:, j * 512:(j + 1) * 512],
                               start=(g == 0 and j == 0),
                               stop=(g == nq - 1 and j == 3),
                               wait=tuple(w) if j == 0 else ())
                        r.emit("tensor", "matmul", PSF[:, 2048:2560],
                               vu[:, kb * 128:(kb + 1) * 128],
                               ptg[:, j * 512:(j + 1) * 512],
                               start=(g == 0 and j == 0),
                               stop=(g == nq - 1 and j == 3),
                               inc=(("PE", 1) if j == 3 else None))
                    return r.v("PE")

                qpp = [0] * nq
                for g in range(nq):
                    pp = pair_idx % 2
                    pair_idx += 1
                    qpp[g] = pp
                    pt = PBF[:, PTOF[pp]:PTOF[pp] + 2048]
                    w = [("ACT", exp_h0)]
                    if qb == 0 and g == 0:
                        w.append(("SQ", ld_unit[u]))
                        w.append(("DVE", dve_out_evac[0]))
                        w.append(("DVE", dve_out_evac[1]))
                        w.append(("ACT", act_out_evac[0]))
                    w2 = [("ACT", exp_hist[0]), ("ACT", exp_hist[1])]
                    smm_h0 = 0
                    for j in range(4):
                        kb = 4 * g + j
                        ww = ()
                        if j == 0:
                            ww = tuple(w)
                        elif j == 2:
                            ww = tuple(w2)
                        r.emit("tensor", "matmul",
                               PSF[:, j * 512:(j + 1) * 512],
                               ku[:, kb * 128:(kb + 1) * 128],
                               qu[:, qb * 512:(qb + 1) * 512],
                               start=True, stop=True,
                               wait=ww,
                               inc=(("PE", 1) if j in (1, 3) else None))
                        if j == 1:
                            smm_h0 = r.v("PE")
                    smm = r.v("PE")
                    r.emit("scalar", "activation", pt[:, 0:1024],
                           PSF[:, 0:1024],
                           AF.Exp, bias=bias4[:, u:u + 1],
                           wait=(("PE", smm_h0), ("PE", pvmm_hist[pp])),
                           inc=("ACT", 1))
                    exp_h0 = r.v("ACT")
                    r.emit("scalar", "activation", pt[:, 1024:2048],
                           PSF[:, 1024:2048],
                           AF.Exp, bias=bias4[:, u:u + 1],
                           wait=(("PE", smm),), inc=("ACT", 1))
                    exp_hist[pp] = r.v("ACT")
                    gate_ws[g] = ("ACT", exp_hist[pp])
                    if g == qb:
                        r.emit("vector", "tensor_tensor", pt, pt,
                               mask_sb[:, :], ALU.mult,
                               wait=(("ACT", exp_hist[pp]),),
                               inc=("DVE", 1))
                        gate_ws[g] = ("DVE", r.v("DVE"))
                    if g > 0:
                        pvmm_hist[qpp[g - 1]] = emit_spv(g - 1, qpp[g - 1])
                pvmm_hist[qpp[nq - 1]] = emit_spv(nq - 1, qpp[nq - 1])
                pvd = pvmm_hist[qpp[nq - 1]]
                qp = qb % 2
                lnt = PF[:, 512 * qp:512 * qp + 512]
                rec = PF[:, 1024 + 512 * qp:1024 + 512 * qp + 512]
                w = [("PE", pvd), ("DVE", ao_hist[qp])]
                r.emit("scalar", "activation", lnt, PSF[:, 2560:3072],
                       AF.Ln, wait=tuple(w), inc=("ACT", 1))
                sum_bank_free = r.v("ACT")
                r.emit("scalar", "activation", rec, lnt, AF.Exp,
                       scale=-1.0, inc=("ACT", 1))
                recd = r.v("ACT")
                w = [("ACT", recd), ("PE", pvd)]
                if first_ao and u >= 2:
                    w.append(("PE", pe_op_end[0]))
                first_ao = False
                r.emit("vector", "tensor_tensor",
                       ao[:, qb * 512:(qb + 1) * 512],
                       PSF[:, 2048:2560], rec, ALU.mult,
                       wait=tuple(w), inc=("DVE", 1))
                ao_hist[qp] = r.v("DVE")
                pv_bank_free = ao_hist[qp]
            pe_unit_end[u] = r.v("PE")
            if u + 2 < 4:
                att_loads(u + 2, extra_wait=(("PE", pe_unit_end[u]),))
            if hl == 1:
                for _tt in range(64):
                    emit_op_tt(b, _tt)
                pe_op_end[b] = r.v("PE")

        r.emit("gpsimd", "collective_compute", "ReduceScatter", ALU.add,
               replica_groups=[list(range(NCORES))],
               ins=[spart.ap().opt()], outs=[s_own.ap().opt()],
               wait=(("SC", r.v("SC")),), inc=("CC2", 1))

        att_pe_end = r.v("PE")
        att_dve_end = r.v("DVE")

        hdn0 = 16384
        pe_gu = [0] * NIT
        act_silu = 0
        act_upev = 0
        dve_hdn = [0] * NIT
        pe_down_ht = [0] * 16
        pe_tr_g = [0, 0]
        dve_gtev = [0, 0]
        act_gt = [0, 0]
        sc_gp = [0, 0]
        pe_down_end = 0
        pe_gu_end = 0
        for th in range(2):
            w = []
            if th == 1:
                w.append(("PE", pe_down_end))
            else:
                w.append(("PE", pe_unit_end[2]))
            agt = ag1inA if th == 0 else ag1inB
            src = dap(agt, 0,
                      [(RPC // 2, 128), (128 * RPC // 2, 16), (1, 1024)])
            r.dma("sync", PBF[:, 0:16384], src, wait=w)
            xnh_ld = r.v("SQ")
            act_floor = r.v("ACT")
            for it in range(NIT):
                gpar = [2, 1, 0][it % 3]
                gw0 = 61440 + gpar * 4096
                if it >= 3:
                    w = [("PE", pe_gu[it - 3])]
                elif th == 1 and it == 0:
                    w = [("PE", pe_gu[42]), ("PE", pe_tr_g[0]),
                         ("PE", pe_tr_g[1]), ("SC", sc_gp[0]),
                         ("SC", sc_gp[1])]
                elif th == 1 and it == 1:
                    w = [("PE", pe_gu[43])]
                elif th == 1:
                    w = [("PE", pe_gu[41])]
                elif it == 0:
                    w = [("PE", att_pe_end), ("SC", sc_sp_hist[1][30]),
                         ("GQ", gqcast)]
                elif it == 1:
                    w = [("PE", pe_unit_end[3])]
                else:
                    w = [("PE", att_pe_end)]
                r.dma("sync", PBF[:, gw0:gw0 + 2048], gwb.ap()[it], wait=w)
                r.dma("sync", PBF[:, gw0 + 2048:gw0 + 4096], uwb.ap()[it])
                gld = r.v("SQ")
                w = [("SQ", gld), ("SQ", xnh_ld),
                     ("ACT", act_silu if it else act_floor)]
                for tj in range(2):
                    for ht in range(16):
                        r.emit("tensor", "matmul", bank(tj),
                               PBF[:, gw0 + ht * 128:gw0 + ht * 128 + 128],
                               PBF[:, ht * 1024 + tj * 512:
                                   ht * 1024 + tj * 512 + 512],
                               start=(ht == 0), stop=(ht == 15),
                               wait=tuple(w) if (tj == 0 and ht == 0) else (),
                               inc=(("PE", 1) if (tj == 1 and ht == 15)
                                    else None))
                gmm = r.v("PE")
                w = ([("DVE", dve_hdn[it - 1])] if it
                     else [("ACT", act_floor)])
                for tj in range(2):
                    for ht in range(16):
                        r.emit("tensor", "matmul", bank(2 + tj),
                               PBF[:, gw0 + 2048 + ht * 128:
                                   gw0 + 2048 + ht * 128 + 128],
                               PBF[:, ht * 1024 + tj * 512:
                                   ht * 1024 + tj * 512 + 512],
                               start=(ht == 0), stop=(ht == 15),
                               wait=tuple(w) if (tj == 0 and ht == 0) else (),
                               inc=(("PE", 1) if (tj == 1 and ht == 15)
                                    else None))
                umm = r.v("PE")
                pe_gu[it] = umm
                w = [("PE", gmm), ("DVE", dve_hdn[it - 1] if it
                                    else att_dve_end)]
                r.emit("scalar", "activation", PF[:, 0:1024], PSF[:, 0:1024],
                       AF.Silu, scale=gs2_sb[:, it:it + 1], wait=tuple(w),
                       inc=("ACT", 1))
                act_silu = r.v("ACT")
                w = [("PE", umm), ("ACT", act_silu)]
                if it == 0:
                    w.append(("PE", pe_down_end if th else att_pe_end))
                r.emit("vector", "scalar_tensor_tensor",
                       PBF[:, hdn0 + it * 1024:hdn0 + (it + 1) * 1024],
                       PSF[:, 1024:2048], us2_sb[:, it:it + 1],
                       PF[:, 0:1024], ALU.mult, ALU.mult,
                       wait=tuple(w), inc=("DVE", 1))
                dve_hdn[it] = r.v("DVE")
            pe_gu_end = r.v("PE")
            act_floor2 = r.v("ACT")
            for ht in range(16):
                dpar = ht % 2
                dof = dpar * 5632
                w = [("DVE", dve_hdn[NIT - 1])]
                if ht >= 2:
                    w.append(("PE", pe_down_ht[ht - 2]))
                else:
                    w.append(("PE", pe_gu_end))
                r.dma("sync", PBF[:, dof:dof + 5632], dwb.ap()[ht], wait=w)
                dld = r.v("SQ")
                w = [("SQ", dld), ("DVE", dve_hdn[NIT - 1]),
                     ("ACT", act_gt[dpar] if (th > 0 or ht >= 2)
                      else act_floor2)]
                for tj in range(2):
                    for it in range(NIT):
                        r.emit("tensor", "matmul",
                               PSF[:, dpar * 1024 + tj * 512:
                                   dpar * 1024 + tj * 512 + 512],
                               PBF[:, dof + it * 128:dof + it * 128 + 128],
                               PBF[:, hdn0 + it * 1024 + tj * 512:
                                   hdn0 + it * 1024 + tj * 512 + 512],
                               start=(it == 0), stop=(it == NIT - 1),
                               wait=tuple(w) if (tj == 0 and it == 0) else (),
                               inc=(("PE", 1) if (tj == 1 and it == NIT - 1)
                                    else None))
                dmm = r.v("PE")
                pe_down_ht[ht] = dmm
                gt = PBF[:, 69632 + dpar * 1024:69632 + (dpar + 1) * 1024]
                w = [("PE", dmm), ("PE", pe_tr_g[dpar])]
                r.emit("scalar", "activation", gt,
                       PSF[:, dpar * 1024:(dpar + 1) * 1024], AF.Silu,
                       scale=ds2_sb[:, ht:ht + 1], wait=tuple(w),
                       inc=("ACT", 1))
                act_gt[dpar] = r.v("ACT")
                ptb = (PTA, PTB)[dpar]
                w = [("ACT", act_gt[dpar]), ("DVE", dve_gtev[dpar])]
                for j in range(8):
                    r.emit("tensor", "transpose",
                           ptb[:, j * 128:(j + 1) * 128],
                           gt[:, j * 128:(j + 1) * 128], ident_sb[:, :],
                           wait=tuple(w) if j == 0 else (),
                           inc=(("PE", 1) if j == 7 else None))
                pe_tr_g[dpar] = r.v("PE")
                gst = PBF[:, 71680 + dpar * 1024:71680 + (dpar + 1) * 1024]
                w = [("PE", pe_tr_g[dpar]), ("SC", sc_gp[dpar])]
                r.emit("vector", "tensor_copy", gst, ptb[:, :],
                       wait=tuple(w), inc=("DVE", 1))
                dve_gtev[dpar] = r.v("DVE")
                dstg = dap(gsc, th * 1024 * E + ht * 128,
                           [(E, 128), (128 * E, 8), (1, 128)])
                r.dma("scalar", dstg, gst, wait=(("DVE", dve_gtev[dpar]),))
                sc_gp[dpar] = r.v("SC")
            pe_down_end = r.v("PE")
            if th == 0:
                pda_sc = r.v("SC")
                pda_dve = 0
                pda_st = 0
                for tt in range(8):
                    sv = PBF[:, 73728:75776]
                    gv = PBF[:, 75776:77824]
                    xv = PF[:, 2048:4096]
                    tv = PF[:, 4096:6144]
                    w = [("CC2", 1), ("SC", pda_sc),
                         ("SC", sc_sp_hist[1][31])]
                    if tt:
                        w = [("DVE", pda_dve), ("SC", pda_st)]
                    r.dma("sync", sv,
                          s_own.ap()[tt * 128:(tt + 1) * 128, :], wait=w)
                    r.dma("sync", gv, gsc.ap()[tt * 128:(tt + 1) * 128, :])
                    r.dma("sync", xv, x_own[tt * 128:(tt + 1) * 128, :])
                    ld = r.v("SQ")
                    r.emit("vector", "tensor_tensor", tv, sv, gv, ALU.mult,
                           wait=(("SQ", ld),))
                    r.emit("vector", "tensor_add", xv, xv, tv,
                           inc=("DVE", 1))
                    pda_dve = r.v("DVE")
                    r.dma("scalar", out_t[tt * 128:(tt + 1) * 128, :], xv,
                          wait=(("DVE", pda_dve),))
                    pda_st = r.v("SC")

        r.barrier()

        sc_out = [0] * 16
        dve_pd = [0] * 16
        for tt in range(8, 16):
            par = tt % 2
            sv = PBF[:, 4096 + par * 2048:4096 + (par + 1) * 2048]
            gv = PBF[:, par * 2048:(par + 1) * 2048]
            xv = PF[:, par * 2048:(par + 1) * 2048]
            tv = PF[:, 4096 + par * 2048:4096 + (par + 1) * 2048]
            w = [("CC2", 1)] if tt == 8 else []
            if tt >= 10:
                w.append(("DVE", dve_pd[tt - 2]))
                w.append(("SC", sc_out[tt - 2]))
            r.dma("sync", sv, s_own.ap()[tt * 128:(tt + 1) * 128, :], wait=w)
            r.dma("sync", gv, gsc.ap()[tt * 128:(tt + 1) * 128, :])
            r.dma("sync", xv, x_own[tt * 128:(tt + 1) * 128, :])
            ld = r.v("SQ")
            r.emit("vector", "tensor_tensor", tv, sv, gv, ALU.mult,
                   wait=(("SQ", ld),))
            r.emit("vector", "tensor_add", xv, xv, tv, inc=("DVE", 1))
            dve_pd[tt] = r.v("DVE")
            r.dma("scalar", out_t[tt * 128:(tt + 1) * 128, :], xv,
                  wait=(("DVE", dve_pd[tt]),))
            sc_out[tt] = r.v("SC")

        if dbg_outs:
            r.barrier()
            for nm, dap in dbg_outs.items():
                if nm in dbg_src:
                    r.dma("gpsimd", dap, dbg_src[nm].ap())

        r.barrier()

        r.finalize()

        def mk(ename):
            stream = r.streams[ename]

            def play(eng):
                for (op, args, kw, waits, inc, _pre) in stream:
                    for (s, vv) in waits:
                        eng.wait_ge(sems[s], vv)
                    if op is None:
                        continue
                    inst = getattr(eng, op)(*args, **kw)
                    if inc is not None:
                        inst.then_inc(sems[inc[0]], inc[1])
            return play

        block.sync(mk("sync"))
        block.scalar(mk("scalar"))
        block.vector(mk("vector"))
        block.tensor(mk("tensor"))
        block.gpsimd(mk("gpsimd"))

    return nc


def _host_consts():
    k = np.arange(64)
    inv = ROPE_BASE ** (-2.0 * k / D)
    didx = np.concatenate([k, k])
    ang = np.arange(T, dtype=np.float64)[None, :] * inv[didx][:, None]
    cos_t = np.cos(ang).astype(np.float32)
    sin_t = np.sin(ang).astype(np.float32)
    m = np.zeros((128, 2048), dtype=np.float32)
    p = np.arange(128)[:, None]
    c = np.arange(512)[None, :]
    for j in range(4):
        m[:, j * 512:(j + 1) * 512] = (c >= p + 128 * j).astype(np.float32)
    masks = m.astype(BF)
    P = np.zeros((128, 128), dtype=np.float32)
    P[np.arange(64), np.arange(64) + 64] = -1.0
    P[np.arange(64) + 64, np.arange(64)] = 1.0
    protT = np.ascontiguousarray(P.T).astype(BF)
    ident = np.eye(128, dtype=np.float32).astype(BF)
    ones_col = np.ones((128, 1), dtype=np.float32).astype(BF)
    ones128 = np.ones((128, 128), dtype=np.float32).astype(BF)
    ones_row_f = np.ones((1, 128), dtype=np.float32)
    return cos_t, sin_t, masks, protT, ident, ones_col, ones128, ones_row_f


def make_in_maps(L0_attn_x, L0_attn_wq, L0_attn_wk, L0_attn_wv, L0_attn_wo,
                 L0_ffn_gate_w, L0_ffn_up_w, L0_ffn_down_w,
                 L0_ffn_gate_s, L0_ffn_up_s, L0_ffn_down_s):
    f32 = np.float32
    x2 = np.asarray(L0_attn_x, f32).reshape(R, E)
    wq = np.asarray(L0_attn_wq, f32)
    wk = np.asarray(L0_attn_wk, f32)
    wv = np.asarray(L0_attn_wv, f32)
    wo = np.asarray(L0_attn_wo, f32)
    gw = np.asarray(L0_ffn_gate_w, f32)
    uw = np.asarray(L0_ffn_up_w, f32)
    dw = np.asarray(L0_ffn_down_w, f32)
    gs = np.asarray(L0_ffn_gate_s, f32)
    us = np.asarray(L0_ffn_up_s, f32)
    ds = np.asarray(L0_ffn_down_s, f32)

    gate_hh = np.ascontiguousarray(
        gw.reshape(NIT, 128, 16, 128).transpose(0, 3, 2, 1).reshape(
            NIT, 128, 2048))
    up_hh = np.ascontiguousarray(
        uw.reshape(NIT, 128, 16, 128).transpose(0, 3, 2, 1).reshape(
            NIT, 128, 2048))
    down_hh = np.ascontiguousarray(
        dw.reshape(16, 128, NIT, 128).transpose(0, 3, 2, 1).reshape(
            16, 128, NIT * 128))
    gate_s2 = np.ascontiguousarray(gs.reshape(NIT, 128).T)
    up_s2 = np.ascontiguousarray(us.reshape(NIT, 128).T)
    down_s2 = np.ascontiguousarray(ds.reshape(16, 128).T)
    (cos_t, sin_t, masks, protT, ident, ones_col, ones128,
     ones_row_f) = _host_consts()

    in_maps = []
    for r_ in range(NCORES):
        f0 = 256 * r_
        wq_hc = np.ascontiguousarray(
            wq[f0:f0 + 256, :].reshape(256, 16, 128).transpose(2, 1, 0)
            .reshape(128, 4096))
        wk_hc = np.ascontiguousarray(
            wk[f0:f0 + 256, :].reshape(256, 16, 128).transpose(2, 1, 0)
            .reshape(128, 4096))
        wv_hc = np.ascontiguousarray(
            wv[f0:f0 + 256, :].reshape(256, 16, 128).transpose(2, 1, 0)
            .reshape(128, 4096))
        wo_hc = np.ascontiguousarray(
            wo[:, f0:f0 + 256].T.reshape(2, 128, 2048).transpose(1, 0, 2)
            .reshape(128, 4096))
        in_maps.append({
            "x_own": np.ascontiguousarray(x2[RPC * r_:RPC * (r_ + 1), :]),
            "wq_h": wq_hc, "wk_h": wk_hc, "wv_h": wv_hc, "wo_h": wo_hc,
            "gate_h": gate_hh, "up_h": up_hh, "down_h": down_hh,
            "gate_s2": gate_s2, "up_s2": up_s2, "down_s2": down_s2,
            "cos_t": cos_t, "sin_t": sin_t, "masks": masks,
            "protT": protT, "ident": ident, "ones_col": ones_col,
            "ones128": ones128, "ones_row_f": ones_row_f,
        })
    return in_maps


def kernel(**inputs):
    nc = build_program()
    in_maps = make_in_maps(**inputs)
    res = run_bass_kernel_spmd(nc, in_maps, core_ids=list(range(NCORES)))
    out = np.concatenate([res.results[i]["out"] for i in range(NCORES)],
                         axis=0)
    return out.reshape(B, T, E)


# version 16 (final)
# speedup vs baseline: 1.1416x; 1.0089x over previous
import math
import numpy as np
import ml_dtypes
import concourse.bass as bass
import concourse.mybir as mybir
from concourse.bass_utils import run_bass_kernel_spmd
from concourse.replica_groups import maybe_share_collective_output_space

F32 = mybir.dt.float32
BF16 = mybir.dt.bfloat16
AF = mybir.ActivationFunctionType
ALU = mybir.AluOpType
AXX = mybir.AxisListType.X
BF = ml_dtypes.bfloat16

NCORES = 8
B, T, E = 2, 8192, 2048
H, D = 16, 128
INTER = 5632
R = B * T
RPC = R // NCORES
NIT = INTER // 128
ROPE_BASE = 1e6
ISD = 1.0 / math.sqrt(D)
EPS = 1e-5

ENGINES = ("sync", "scalar", "vector", "tensor", "gpsimd")


class Rec:

    def __init__(self):
        self.streams = {e: [] for e in ENGINES}
        self.cnt = {}
        self.nbar = 0

    def v(self, s):
        return self.cnt.get(s, 0)

    def emit(self, engine, op, *args, wait=(), inc=None, **kw):
        wait = tuple((s, n) for (s, n) in wait if n and n > 0)
        self.streams[engine].append([op, args, kw, wait, inc, None])
        if inc is not None:
            s, n = inc
            self.cnt[s] = self.cnt.get(s, 0) + n
            return self.cnt[s]
        return None

    def dma(self, q, out, in_, wait=(), **kw):
        sem = {"sync": "SQ", "scalar": "SC", "gpsimd": "GQ"}[q]
        pre = self.v(sem)
        v = self.emit(q, "dma_start", wait=wait, inc=(sem, 16),
                      out=out, in_=in_, **kw)
        self.streams[q][-1][5] = (sem, pre)
        return v

    def finalize(self):
        qsems = ("SQ", "SC", "GQ")
        waited = {s: set() for s in qsems}
        for e in ENGINES:
            for it in self.streams[e]:
                for (s, n) in it[3]:
                    if s in qsems:
                        waited[s].add(n)
        wsorted = {s: sorted(waited[s]) for s in qsems}
        import bisect
        for e in ENGINES:
            level = {s: 0 for s in qsems}
            for it in self.streams[e]:
                for (s, n) in it[3]:
                    if s in qsems and n > level[s]:
                        level[s] = n
                if it[5] is not None:
                    s, pre = it[5]
                    ws = wsorted[s]
                    i = bisect.bisect_right(ws, pre)
                    tgt = ws[i - 1] if i > 0 else 0
                    if tgt > level[s]:
                        it[3] = ((s, tgt),) + it[3]
                        level[s] = tgt

    def barrier(self):
        self.nbar += 1
        flush = {"sync": "SQ", "scalar": "SC", "gpsimd": "GQ"}
        own = {"scalar": "ACT", "vector": "DVE", "tensor": "PE"}
        for e in ENGINES:
            w = []
            if e in flush:
                w.append((flush[e], self.v(flush[e])))
            if e in own:
                s = own[e]
                for it in reversed(self.streams[e]):
                    if it[0] is None or it[0] == "nop":
                        continue
                    if it[4] is None:
                        it[4] = (s, 1)
                        self.cnt[s] = self.cnt.get(s, 0) + 1
                    break
                w.append((s, self.v(s)))
            self.emit(e, "nop", wait=tuple(w), inc=("BAR", 1))
        tgt = self.v("BAR")
        assert tgt == 5 * self.nbar
        for e in ENGINES:
            self.emit(e, None, wait=(("BAR", tgt),))


def build_program(debug_names=(), race_detect=True):
    nc = bass.Bass("TRN2", target_bir_lowering=False, debug=False,
                   detect_race_conditions=race_detect)

    def din(name, shape, dt=F32):
        return nc.dram_tensor(name, list(shape), dt, kind="ExternalInput").ap()

    x_own = din("x_own", (RPC, E))
    wq_h = din("wq_h", (128, 4096))
    wk_h = din("wk_h", (128, 4096))
    wv_h = din("wv_h", (128, 4096))
    wo_h = din("wo_h", (128, 4096))
    gate_h = din("gate_h", (NIT, 128, 2048))
    up_h = din("up_h", (NIT, 128, 2048))
    down_h = din("down_h", (16, 128, NIT * 128))
    gate_s2 = din("gate_s2", (128, NIT))
    up_s2 = din("up_s2", (128, NIT))
    down_s2 = din("down_s2", (128, 16))
    cos_t = din("cos_t", (128, T))
    sin_t = din("sin_t", (128, T))
    masks = din("masks", (128, 2048), BF16)
    protT = din("protT", (128, 128), BF16)
    ident = din("ident", (128, 128), BF16)
    ones_col = din("ones_col", (128, 1), BF16)
    ones128 = din("ones128", (128, 128), BF16)
    ones_row_f = din("ones_row_f", (1, 128), F32)

    out_t = nc.dram_tensor("out", [RPC, E], F32, kind="ExternalOutput").ap()

    rg = [list(range(NCORES))]
    ag_space = maybe_share_collective_output_space("AllGather", rg)
    rs_space = maybe_share_collective_output_space("ReduceScatter", rg)
    ag1inA = nc.dram_tensor("ag1inA", [E, RPC // 2], BF16)
    ag1inB = nc.dram_tensor("ag1inB", [E, RPC // 2], BF16)
    ag1outA = nc.dram_tensor("ag1outA", [R, RPC // 2], BF16,
                             addr_space=ag_space)
    ag1outB = nc.dram_tensor("ag1outB", [R, RPC // 2], BF16,
                             addr_space=ag_space)
    qs = nc.dram_tensor("qs", [256, R], BF16)
    ks = nc.dram_tensor("ks", [256, R], BF16)
    vs = nc.dram_tensor("vs", [2, 2, 128, T], BF16)
    spart = nc.dram_tensor("spart", [R, E], BF16)
    s_own = nc.dram_tensor("s_own", [RPC, E], BF16, addr_space=rs_space)
    gsc = nc.dram_tensor("gsc", [RPC, E], BF16)
    gwb = nc.dram_tensor("gwb", [NIT, 128, 2048], BF16)
    uwb = nc.dram_tensor("uwb", [NIT, 128, 2048], BF16)
    dwb = nc.dram_tensor("dwb", [16, 128, NIT * 128], BF16)

    dbg_src = {"ag1in": ag1inA, "ag1out": ag1outA, "qs": qs, "ks": ks,
               "vs": vs, "spart": spart, "s_own": s_own, "gsc": gsc}
    dbg_outs = {}
    for nm in debug_names:
        if nm == "norm0":
            dbg_outs["norm0_sq2"] = nc.dram_tensor(
                "dbg_norm0_sq2", [128, 512], BF16, kind="ExternalOutput").ap()
            dbg_outs["norm0_row"] = nc.dram_tensor(
                "dbg_norm0_row", [1, 512], F32, kind="ExternalOutput").ap()
            dbg_outs["norm0_qrt"] = nc.dram_tensor(
                "dbg_norm0_qrt", [128, 1024], BF16, kind="ExternalOutput").ap()
            continue
        if nm == "bias4":
            dbg_outs["bias4"] = nc.dram_tensor(
                "dbg_bias4", [128, 4], F32, kind="ExternalOutput").ap()
            dbg_outs["runmax"] = nc.dram_tensor(
                "dbg_runmax", [1, 16], F32, kind="ExternalOutput").ap()
            dbg_outs["negm"] = nc.dram_tensor(
                "dbg_negm", [1, 8], F32, kind="ExternalOutput").ap()
            continue
        h = dbg_src[nm]
        dbg_outs[nm] = nc.dram_tensor(
            "dbg_" + nm, list(h.shape), h.dtype, kind="ExternalOutput").ap()

    r = Rec()

    import contextlib
    with contextlib.ExitStack() as ctx:
        e = ctx.enter_context
        PBF = e(nc.sbuf_tensor("PBF", [128, 77824], BF16))
        PF = e(nc.sbuf_tensor("PF", [128, 8192], F32))
        mask_sb = e(nc.sbuf_tensor("mask_sb", [128, 2048], BF16))
        protT_sb = e(nc.sbuf_tensor("protT_sb", [128, 128], BF16))
        ident_sb = e(nc.sbuf_tensor("ident_sb", [128, 128], BF16))
        ones_col_sb = e(nc.sbuf_tensor("ones_col_sb", [128, 1], BF16))
        ones128_sb = e(nc.sbuf_tensor("ones128_sb", [128, 128], BF16))
        ones_row_f_sb = e(nc.sbuf_tensor("ones_row_f_sb", [1, 128], F32))
        gs2_sb = e(nc.sbuf_tensor("gs2_sb", [128, NIT], F32))
        us2_sb = e(nc.sbuf_tensor("us2_sb", [128, NIT], F32))
        ds2_sb = e(nc.sbuf_tensor("ds2_sb", [128, 16], F32))
        wosl_sb = e(nc.sbuf_tensor("wosl_sb", [128, 4096], BF16))
        stats = e(nc.sbuf_tensor("stats", [128, 16], F32))
        runmax2 = e(nc.sbuf_tensor("runmax2", [128, 16], F32))
        stats2m = e(nc.sbuf_tensor("stats2m", [128, 16], F32))
        runmax = e(nc.sbuf_tensor("runmax", [1, 16], F32))
        negm = e(nc.sbuf_tensor("negm", [1, 8], F32))
        bias4 = e(nc.sbuf_tensor("bias4", [128, 4], F32))
        rsum = e(nc.sbuf_tensor("rsum", [1, 512], F32))
        PSF = e(nc.psum_tensor("PSF", [128, 3072], F32))
        PTA = e(nc.psum_tensor("PTA", [128, 1024], BF16))
        PTB = e(nc.psum_tensor("PTB", [128, 1024], BF16))
        hSQ = e(nc.semaphore("SQ"))
        hSC = e(nc.semaphore("SC"))
        hGQ = e(nc.semaphore("GQ"))
        hPE = e(nc.semaphore("PE"))
        hACT = e(nc.semaphore("ACT"))
        hDVE = e(nc.semaphore("DVE"))
        hCC1 = e(nc.semaphore("CC1"))
        hCC2 = e(nc.semaphore("CC2"))
        hCC3 = e(nc.semaphore("CC3"))
        hBAR = e(nc.semaphore("BAR"))
        block = e(nc.Block())
        sems = {"SQ": hSQ, "SC": hSC, "GQ": hGQ, "PE": hPE, "ACT": hACT,
                "DVE": hDVE, "CC1": hCC1, "CC2": hCC2, "CC3": hCC3, "BAR": hBAR}

        def bank(i, n=512):
            return PSF[:, 512 * i:512 * i + n]

        def dap(t, off, dims):
            return bass.AP(t, off, [list(d) for d in dims])

        r.dma("sync", mask_sb[:, :], masks)
        r.dma("sync", protT_sb[:, :], protT)
        r.dma("sync", ident_sb[:, :], ident)
        r.dma("sync", ones_col_sb[:, :], ones_col)
        r.dma("sync", ones128_sb[:, :], ones128)
        r.dma("sync", ones_row_f_sb[:, :], ones_row_f)
        r.dma("sync", gs2_sb[:, :], gate_s2)
        r.dma("sync", us2_sb[:, :], up_s2)
        r.dma("sync", ds2_sb[:, :], down_s2)
        p0 = r.v("SQ")
        r.emit("vector", "memset", runmax[0:1, :], 0.0, inc=("DVE", 1))

        ln_done = [0] * 16
        tr_done = [0] * 16
        ev_done = [[0, 0] for _ in range(16)]
        st_done = [0] * 16
        red_done = [0] * 16
        sq_done = [0] * 16
        for i in range(16):
            p = i % 2
            xt = PF[:, p * 2048:(p + 1) * 2048]
            xsq = PF[:, 4096 + p * 2048:4096 + (p + 1) * 2048]
            w = [("SQ", p0)] if i == 0 else []
            if i >= 2:
                w += [("ACT", sq_done[i - 2]), ("DVE", red_done[i - 2])]
            r.dma("sync", xt, x_own[128 * i:128 * (i + 1), :], wait=w)
            ld = r.v("SQ")
            r.emit("vector", "reduce_sum", stats[:, i:i + 1], xt, AXX,
                   wait=(("SQ", ld),), inc=("DVE", 1))
            r.emit("scalar", "activation", xsq, xt, AF.Square,
                   wait=(("SQ", ld),), inc=("ACT", 1))
            sq_done[i] = r.v("ACT")
            r.emit("vector", "reduce_sum", runmax2[:, i:i + 1], xsq, AXX,
                   wait=(("ACT", sq_done[i]),), inc=("DVE", 1))
            red_done[i] = r.v("DVE")
        r.emit("vector", "tensor_scalar_mul", stats[:, 0:16],
               stats[:, 0:16], 1.0 / E, wait=(("DVE", red_done[15]),),
               inc=("DVE", 1))
        c1 = r.v("DVE")
        r.emit("vector", "tensor_scalar_mul", runmax2[:, 0:16],
               runmax2[:, 0:16], 1.0 / E, wait=(("DVE", c1),),
               inc=("DVE", 1))
        c2 = r.v("DVE")
        r.emit("vector", "tensor_tensor", stats2m[:, 0:16],
               stats[:, 0:16], stats[:, 0:16], ALU.mult,
               wait=(("DVE", c2),), inc=("DVE", 1))
        c3 = r.v("DVE")
        r.emit("vector", "tensor_sub", stats2m[:, 0:16],
               runmax2[:, 0:16], stats2m[:, 0:16],
               wait=(("DVE", c3),), inc=("DVE", 1))
        c4 = r.v("DVE")
        r.emit("vector", "tensor_scalar_add", stats2m[:, 0:16],
               stats2m[:, 0:16], EPS, wait=(("DVE", c4),), inc=("DVE", 1))
        c5 = r.v("DVE")
        r.emit("scalar", "activation", stats2m[:, 0:16], stats2m[:, 0:16],
               AF.Ln, wait=(("DVE", c5),))
        r.emit("scalar", "activation", stats2m[:, 0:16], stats2m[:, 0:16],
               AF.Exp, scale=-0.5, inc=("ACT", 1))
        rstd_all = r.v("ACT")
        r.emit("vector", "tensor_scalar_mul", stats[:, 0:16],
               stats[:, 0:16], -1.0, wait=(("DVE", c5),), inc=("DVE", 1))
        negm_all = r.v("DVE")
        for i in range(16):
            p = i % 2
            xt = PF[:, p * 2048:(p + 1) * 2048]
            xnv = PBF[:, 36864 + p * 2048:36864 + (p + 1) * 2048]
            stg = PBF[:, 40960 + p * 2048:40960 + (p + 1) * 2048]
            w = []
            if i < 2:
                w.append(("DVE", negm_all))
            else:
                w.append(("DVE", ln_done[i - 2]))
            r.dma("sync", xt, x_own[128 * i:128 * (i + 1), :], wait=w)
            ld = r.v("SQ")
            w = [("SQ", ld), ("ACT", rstd_all), ("DVE", negm_all)]
            if i >= 2:
                w.append(("PE", tr_done[i - 2]))
            r.emit("vector", "tensor_scalar", xnv, xt,
                   stats[:, i:i + 1], stats2m[:, i:i + 1],
                   ALU.add, ALU.mult, wait=tuple(w), inc=("DVE", 1))
            ln_done[i] = r.v("DVE")
            for half, pt in enumerate((PTA, PTB)):
                w = [("DVE", ln_done[i])]
                if i >= 1:
                    w.append(("DVE", ev_done[i - 1][half]))
                for j in range(8):
                    et = half * 8 + j
                    r.emit("tensor", "transpose",
                           pt[:, 128 * j:128 * (j + 1)],
                           xnv[:, 128 * et:128 * (et + 1)],
                           ident_sb[:, :],
                           wait=tuple(w) if j == 0 else (),
                           inc=(("PE", 1) if j == 7 else None))
                trh = r.v("PE")
                w2 = [("PE", trh)]
                if i >= 2 and half == 0:
                    w2.append(("SC", st_done[i - 2]))
                r.emit("vector", "tensor_copy",
                       stg[:, half * 1024:(half + 1) * 1024], pt[:, :],
                       wait=tuple(w2), inc=("DVE", 1))
                ev_done[i][half] = r.v("DVE")
            tr_done[i] = r.v("PE")
            agt = ag1inA if i < 8 else ag1inB
            dst = dap(agt, 128 * (i % 8),
                      [(RPC // 2, 128), (128 * RPC // 2, 16), (1, 128)])
            r.dma("scalar", dst, stg, wait=(("DVE", ev_done[i][1]),))
            st_done[i] = r.v("SC")

        r.emit("gpsimd", "collective_compute", "AllGather", ALU.bypass,
               replica_groups=[list(range(NCORES))],
               ins=[ag1inA.ap().opt()], outs=[ag1outA.ap().opt()],
               wait=(("SC", st_done[7]),), inc=("CC1", 1))
        r.emit("gpsimd", "collective_compute", "AllGather", ALU.bypass,
               replica_groups=[list(range(NCORES))],
               ins=[ag1inB.ap().opt()], outs=[ag1outB.ap().opt()],
               wait=(("SC", st_done[15]),), inc=("CC3", 1))
        r.dma("gpsimd", PBF[:, 16384:20480], wq_h)
        r.dma("gpsimd", PBF[:, 20480:24576], wk_h)
        r.dma("gpsimd", PBF[:, 24576:28672], wv_h)
        r.dma("gpsimd", wosl_sb[:, :], wo_h)
        gqw = r.v("GQ")

        qraw = PBF[:, 28672:29696]
        kraw = PBF[:, 29696:30720]
        qrt = PBF[:, 30720:31744]
        krt = PBF[:, 31744:32768]
        sq2 = PBF[:, 32768:33280]
        vtile = PBF[:, 33280:34304]
        pe_xnb = [0] * 32
        dve_rope = [0] * 32
        sc_pb = [0] * 32
        prev_v_evac = 0
        prev_ropeq = 0
        prev_ropek = 0
        prev_norm_mm = 0
        prev_norm_red = 0
        tb_order = ([tb for tb in range(32) if tb % 4 < 2]
                    + [tb for tb in range(32) if tb % 4 >= 2])
        for ti, tb in enumerate(tb_order):
            par = ti % 2
            b = tb // 16
            R0 = 512 * tb
            tbase = R0 % T
            blk = tb // 4
            tl0 = (tb % 4) * 512
            half_b = tl0 >= 1024
            xnb = PBF[:, par * 8192:(par + 1) * 8192]
            ct = PF[:, par * 512:(par + 1) * 512]
            st = PF[:, 1024 + par * 512:1024 + (par + 1) * 512]
            tmp1 = PF[:, 2048:2560]
            tmp2 = PF[:, 2560:3072]
            w = []
            if ti == 0:
                w.append(("CC1", 1))
            if ti == 16:
                w.append(("CC3", 1))
            if ti >= 2:
                w.append(("PE", pe_xnb[ti - 2]))
            agsrc = ag1outB if half_b else ag1outA
            src = dap(agsrc, blk * 2048 * (RPC // 2) + (tl0 % 1024),
                      [(RPC // 2, 128), (128 * RPC // 2, 16), (1, 512)])
            r.dma("sync", xnb, src, wait=w)
            w = [("DVE", dve_rope[ti - 2])] if ti >= 2 else [
                ("DVE", ln_done[15])]
            r.dma("sync", ct, cos_t[:, tbase:tbase + 512], wait=w)
            r.dma("sync", st, sin_t[:, tbase:tbase + 512])
            ldb = r.v("SQ")
            w = [("SQ", ldb), ("GQ", gqw), ("DVE", prev_ropek)]
            for fl in range(2):
                for et in range(16):
                    r.emit("tensor", "matmul", bank(fl),
                           PBF[:, 16384 + et * 256 + fl * 128:
                               16384 + et * 256 + fl * 128 + 128],
                           xnb[:, et * 512:(et + 1) * 512],
                           start=(et == 0), stop=(et == 15),
                           wait=tuple(w) if (fl == 0 and et == 0) else (),
                           inc=(("PE", 1) if (fl == 1 and et == 15) else None))
            qmm = r.v("PE")
            w = [("ACT", prev_v_evac)]
            for fl in range(2):
                for et in range(16):
                    r.emit("tensor", "matmul", bank(4 + fl),
                           PBF[:, 20480 + et * 256 + fl * 128:
                               20480 + et * 256 + fl * 128 + 128],
                           xnb[:, et * 512:(et + 1) * 512],
                           start=(et == 0), stop=(et == 15),
                           wait=tuple(w) if (fl == 0 and et == 0) else (),
                           inc=(("PE", 1) if (fl == 1 and et == 15) else None))
            kmm = r.v("PE")
            w = [("PE", qmm), ("DVE", prev_ropeq)]
            r.emit("scalar", "activation", qraw, PSF[:, 0:1024], AF.Copy,
                   scale=ISD, wait=tuple(w), inc=("ACT", 1))
            q_evac = r.v("ACT")
            w = [("ACT", q_evac), ("DVE", prev_norm_red)]
            for fl in range(2):
                r.emit("tensor", "matmul", bank(2 + fl), protT_sb[:, :],
                       qraw[:, fl * 512:(fl + 1) * 512],
                       start=True, stop=True,
                       wait=tuple(w) if fl == 0 else (),
                       inc=(("PE", 1) if fl == 1 else None))
            rotq_mm = r.v("PE")
            w = [("PE", kmm), ("DVE", prev_ropek)]
            r.emit("scalar", "activation", kraw, PSF[:, 2048:3072], AF.Copy,
                   wait=tuple(w), inc=("ACT", 1))
            k_evac = r.v("ACT")
            w = [("PE", rotq_mm), ("SQ", ldb),
                 ("SC", sc_pb[ti - 1] if ti else 0)]
            for fl in range(2):
                r.emit("vector", "tensor_tensor", tmp1,
                       qraw[:, fl * 512:(fl + 1) * 512], ct, ALU.mult,
                       wait=tuple(w) if fl == 0 else ())
                r.emit("vector", "tensor_tensor", tmp2, bank(2 + fl), st,
                       ALU.mult)
                r.emit("vector", "tensor_add",
                       qrt[:, fl * 512:(fl + 1) * 512], tmp1, tmp2,
                       inc=("DVE", 1))
            ropeq = r.v("DVE")
            prev_ropeq = ropeq
            w = [("ACT", k_evac)]
            for fl in range(2):
                r.emit("tensor", "matmul", bank(fl), protT_sb[:, :],
                       kraw[:, fl * 512:(fl + 1) * 512],
                       start=True, stop=True,
                       wait=tuple(w) if fl == 0 else (),
                       inc=(("PE", 1) if fl == 1 else None))
            rotk_mm = r.v("PE")
            w = [("ACT", k_evac)]
            for tt in range(4):
                bk = PSF[:, 2048 + tt * 256:2048 + (tt + 1) * 256]
                for et in range(16):
                    r.emit("tensor", "matmul", bk,
                           xnb[:, et * 512 + tt * 128:
                               et * 512 + tt * 128 + 128],
                           PBF[:, 24576 + et * 256:24576 + (et + 1) * 256],
                           start=(et == 0), stop=(et == 15),
                           wait=tuple(w) if (tt == 0 and et == 0) else (),
                           inc=(("PE", 1) if (tt == 3 and et == 15) else None))
            vmm = r.v("PE")
            pe_xnb[ti] = vmm
            w = [("PE", rotk_mm)]
            for fl in range(2):
                r.emit("vector", "tensor_tensor", tmp1,
                       kraw[:, fl * 512:(fl + 1) * 512], ct, ALU.mult,
                       wait=tuple(w) if fl == 0 else ())
                r.emit("vector", "tensor_tensor", tmp2, bank(fl), st,
                       ALU.mult)
                r.emit("vector", "tensor_add",
                       krt[:, fl * 512:(fl + 1) * 512], tmp1, tmp2,
                       inc=("DVE", 1))
            ropek = r.v("DVE")
            prev_ropek = ropek
            dve_rope[ti] = ropek
            w = [("PE", vmm), ("SC", sc_pb[ti - 1] if ti else 0)]
            r.emit("scalar", "activation", vtile, PSF[:, 2048:3072], AF.Copy,
                   wait=tuple(w), inc=("ACT", 1))
            prev_v_evac = r.v("ACT")
            for pi, (srcv, slot0) in enumerate(((qrt, 0), (krt, 4))):
                for fl in range(2):
                    slot = slot0 + b * 2 + fl
                    w = [("PE", prev_norm_mm)]
                    r.emit("vector", "tensor_tensor", sq2,
                           srcv[:, fl * 512:(fl + 1) * 512],
                           srcv[:, fl * 512:(fl + 1) * 512], ALU.mult,
                           wait=tuple(w), inc=("DVE", 1))
                    sqd2 = r.v("DVE")
                    r.emit("tensor", "matmul", PSF[0:1, 1024:1536],
                           ones_col_sb[:, :], sq2, start=True, stop=True,
                           wait=(("DVE", sqd2),), inc=("PE", 1))
                    prev_norm_mm = r.v("PE")
                    if tb == 0 and pi == 0 and fl == 0 and "norm0" in debug_names:
                        r.emit("vector", "tensor_copy", PBF[:, 73728:74240],
                               sq2)
                        r.emit("vector", "tensor_copy", PBF[:, 74240:75264],
                               qrt)
                        r.emit("vector", "tensor_copy", PF[0:1, 3072:3584],
                               PSF[0:1, 1024:1536],
                               wait=(("PE", prev_norm_mm),), inc=("DVE", 1))
                    r.emit("vector", "reduce_max", stats[0:1, 15:16],
                           PSF[0:1, 1024:1536], AXX,
                           wait=(("PE", prev_norm_mm),
                                 ("DVE", prev_norm_red)),
                           inc=("DVE", 1))
                    rmx = r.v("DVE")
                    r.emit("vector", "tensor_max",
                           runmax[0:1, slot:slot + 1],
                           runmax[0:1, slot:slot + 1], stats[0:1, 15:16],
                           wait=(("DVE", rmx),), inc=("DVE", 1))
                    prev_norm_red = r.v("DVE")
            dstq = dap(qs, R0, [(R, 128), (128 * R, 2), (1, 512)])
            r.dma("scalar", dstq, qrt, wait=(("DVE", ropeq),))
            dstk = dap(ks, R0, [(R, 128), (128 * R, 2), (1, 512)])
            r.dma("scalar", dstk, krt, wait=(("DVE", ropek),))
            ktg0 = (R0 % T) // 128
            for hl in range(2):
                dstv = dap(vs, (b * 2 + hl) * 128 * T + ktg0 * 128,
                           [(T, 128), (128, 4), (1, 128)])
                srcv2 = bass.AP(PBF, 33280 + hl * 128,
                                [[77824, 128], [256, 4], [1, 128]])
                r.dma("scalar", dstv, srcv2,
                      wait=(("ACT", prev_v_evac),) if hl == 0 else ())
            sc_pb[ti] = r.v("SC")

        for u in range(4):
            bu, fl = u // 2, u % 2
            qslot = bu * 2 + fl
            kslot = 4 + bu * 2 + fl
            w = [("DVE", prev_norm_red)] if u == 0 else []
            r.emit("vector", "tensor_tensor", negm[0:1, u:u + 1],
                   runmax[0:1, qslot:qslot + 1],
                   runmax[0:1, kslot:kslot + 1],
                   ALU.mult, wait=tuple(w), inc=("DVE", 1))
            nd = r.v("DVE")
            r.emit("scalar", "activation", negm[0:1, u:u + 1],
                   negm[0:1, u:u + 1], AF.Ln, wait=(("DVE", nd),))
            r.emit("scalar", "activation", negm[0:1, u:u + 1],
                   negm[0:1, u:u + 1], AF.Exp, scale=0.5, inc=("ACT", 1))
            ad = r.v("ACT")
            r.emit("vector", "tensor_scalar", negm[0:1, u:u + 1],
                   negm[0:1, u:u + 1], -1.001, -0.01, ALU.mult, ALU.add,
                   wait=(("ACT", ad),), inc=("DVE", 1))
            nd2 = r.v("DVE")
            r.emit("tensor", "matmul", PSF[:, 1024:1025],
                   ones_row_f_sb[0:1, :], negm[0:1, u:u + 1],
                   start=True, stop=True, wait=(("DVE", nd2),),
                   inc=("PE", 1))
            pm = r.v("PE")
            r.emit("vector", "tensor_copy", bias4[:, u:u + 1],
                   PSF[:, 1024:1025], wait=(("PE", pm),), inc=("DVE", 1))

        if "bias4" in debug_names:
            bd = r.v("DVE")
            r.dma("scalar", dbg_outs["bias4"], bias4[:, :],
                  wait=(("DVE", bd),))
            r.dma("scalar", dbg_outs["runmax"], runmax[0:1, :])
            r.dma("scalar", dbg_outs["negm"], negm[0:1, :])
        if "norm0" in debug_names:
            bd = r.v("DVE")
            r.dma("scalar", dbg_outs["norm0_sq2"], PBF[:, 73728:74240],
                  wait=(("DVE", bd),))
            r.dma("scalar", dbg_outs["norm0_qrt"], PBF[:, 74240:75264])
            r.dma("scalar", dbg_outs["norm0_row"], PF[0:1, 3072:3584])

        r.barrier()

        QOF = (0, 40960)
        KOF = (8192, 24576)
        VOF = (16384, 32768)
        AOF = (49152, 57344)
        PTOF = (65536, 67584)
        units = [(0, 0), (0, 1), (1, 0), (1, 1)]
        ld_unit = [0] * 4
        pe_unit_end = [0] * 4
        pe_op_end = [0, 0]

        def att_loads(u, extra_wait=()):
            b, hl = units[u]
            par = u % 2
            r.dma("sync", PBF[:, QOF[par]:QOF[par] + 8192],
                  qs.ap()[hl * 128:(hl + 1) * 128, b * T:(b + 1) * T],
                  wait=extra_wait)
            r.dma("sync", PBF[:, KOF[par]:KOF[par] + 8192],
                  ks.ap()[hl * 128:(hl + 1) * 128, b * T:(b + 1) * T])
            r.dma("sync", PBF[:, VOF[par]:VOF[par] + 8192],
                  vs.ap()[b, hl, :, :])
            ld_unit[u] = r.v("SQ")

        cast_w = (("SQ", r.v("SQ") + 96),)
        for lo in (0, 22):
            r.dma("gpsimd", gwb.ap()[lo:lo + 22], gate_h[lo:lo + 22],
                  wait=cast_w)
            cast_w = ()
            r.dma("gpsimd", uwb.ap()[lo:lo + 22], up_h[lo:lo + 22])
        for lo in (0, 8):
            r.dma("gpsimd", dwb.ap()[lo:lo + 8], down_h[lo:lo + 8])
        gqcast = r.v("GQ")

        att_loads(0)
        att_loads(1)

        def emit_op_tt(b, tt):
            a0 = PBF[:, AOF[0]:AOF[0] + 8192]
            a1 = PBF[:, AOF[1]:AOF[1] + 8192]
            po = tt % 4
            outst = PBF[:, 69632 + po * 2048:69632 + (po + 1) * 2048]
            stq = "sync" if b == 0 else "scalar"
            stsem = "SQ" if b == 0 else "SC"
            for fb in range(4):
                bk = bank(fb % 2)
                w = [("DVE", dve_out_evac[fb % 2]),
                     ("ACT", act_out_evac[0])]
                if tt == 0 and fb == 0:
                    w.append(("DVE", ao_hist[0]))
                    w.append(("DVE", ao_hist[1]))
                    w.append(("GQ", gqw))
                    w.append(("ACT", exp_hist[0]))
                    w.append(("ACT", exp_hist[1]))
                for et, asrc in enumerate((a0, a1)):
                    r.emit("tensor", "matmul", bk,
                           asrc[:, tt * 128:(tt + 1) * 128],
                           wosl_sb[:, et * 2048 + fb * 512:
                                   et * 2048 + fb * 512 + 512],
                           start=(et == 0), stop=(et == 1),
                           wait=tuple(w) if et == 0 else (),
                           inc=(("PE", 1) if et == 1 else None))
                pmm = r.v("PE")
                w2 = [("PE", pmm)]
                if tt >= 4 and fb < 2:
                    w2.append((stsem, sc_sp_hist[b][(tt - 4) // 2]))
                if fb % 2 == 0:
                    r.emit("vector", "tensor_copy",
                           outst[:, fb * 512:(fb + 1) * 512], bk,
                           wait=tuple(w2), inc=("DVE", 1))
                    dve_out_evac[0] = r.v("DVE")
                else:
                    r.emit("scalar", "activation",
                           outst[:, fb * 512:(fb + 1) * 512], bk, AF.Copy,
                           wait=tuple(w2), inc=("ACT", 1))
                    act_out_evac[0] = r.v("ACT")
            if tt % 2 == 1:
                dst = dap(spart, (b * T + (tt - 1) * 128) * E,
                          [(E, 128), (128 * E, 2), (1, E)])
                so = 69632 + (tt - 1) % 4 * 2048
                r.dma(stq, dst, PBF[:, so:so + 4096],
                      wait=(("DVE", dve_out_evac[0]),
                            ("ACT", act_out_evac[0])))
                sc_sp_hist[b][tt // 2] = r.v(stsem)
        exp_hist = [0, 0]
        exp_h0 = 0
        pvmm_hist = [0, 0]
        ao_hist = [0, 0]
        pv_bank_free = 0
        sum_bank_free = 0
        dve_out_evac = [0, 0]
        act_out_evac = [0]
        sc_sp_hist = {0: [0] * 64, 1: [0] * 64}
        pair_idx = 0
        for u in range(4):
            b, hl = units[u]
            par = u % 2
            qu = PBF[:, QOF[par]:QOF[par] + 8192]
            ku = PBF[:, KOF[par]:KOF[par] + 8192]
            vu = PBF[:, VOF[par]:VOF[par] + 8192]
            ao = PBF[:, AOF[hl]:AOF[hl] + 8192]
            first_ao = True
            for qb in range(16):
                nq = qb + 1
                gate_ws = [None] * nq

                def emit_spv(g, last_pp):
                    ptg = PBF[:, PTOF[last_pp]:PTOF[last_pp] + 2048]
                    w = [gate_ws[g]]
                    if g == 0:
                        w.append(("ACT", sum_bank_free))
                        w.append(("DVE", pv_bank_free))
                    for j in range(4):
                        kb = 4 * g + j
                        r.emit("tensor", "matmul", PSF[:, 2560:3072],
                               ones128_sb[:, :],
                               ptg[:, j * 512:(j + 1) * 512],
                               start=(g == 0 and j == 0),
                               stop=(g == nq - 1 and j == 3),
                               wait=tuple(w) if j == 0 else ())
                        r.emit("tensor", "matmul", PSF[:, 2048:2560],
                               vu[:, kb * 128:(kb + 1) * 128],
                               ptg[:, j * 512:(j + 1) * 512],
                               start=(g == 0 and j == 0),
                               stop=(g == nq - 1 and j == 3),
                               inc=(("PE", 1) if j == 3 else None))
                    return r.v("PE")

                qpp = [0] * nq
                for g in range(nq):
                    pp = pair_idx % 2
                    pair_idx += 1
                    qpp[g] = pp
                    pt = PBF[:, PTOF[pp]:PTOF[pp] + 2048]
                    w = [("ACT", exp_h0)]
                    if qb == 0 and g == 0:
                        w.append(("SQ", ld_unit[u]))
                        w.append(("DVE", dve_out_evac[0]))
                        w.append(("DVE", dve_out_evac[1]))
                        w.append(("ACT", act_out_evac[0]))
                    w2 = [("ACT", exp_hist[0]), ("ACT", exp_hist[1])]
                    smm_h0 = 0
                    for j in range(4):
                        kb = 4 * g + j
                        ww = ()
                        if j == 0:
                            ww = tuple(w)
                        elif j == 2:
                            ww = tuple(w2)
                        r.emit("tensor", "matmul",
                               PSF[:, j * 512:(j + 1) * 512],
                               ku[:, kb * 128:(kb + 1) * 128],
                               qu[:, qb * 512:(qb + 1) * 512],
                               start=True, stop=True,
                               wait=ww,
                               inc=(("PE", 1) if j in (1, 3) else None))
                        if j == 1:
                            smm_h0 = r.v("PE")
                    smm = r.v("PE")
                    r.emit("scalar", "activation", pt[:, 0:1024],
                           PSF[:, 0:1024],
                           AF.Exp, bias=bias4[:, u:u + 1],
                           wait=(("PE", smm_h0), ("PE", pvmm_hist[pp])),
                           inc=("ACT", 1))
                    exp_h0 = r.v("ACT")
                    r.emit("scalar", "activation", pt[:, 1024:2048],
                           PSF[:, 1024:2048],
                           AF.Exp, bias=bias4[:, u:u + 1],
                           wait=(("PE", smm),), inc=("ACT", 1))
                    exp_hist[pp] = r.v("ACT")
                    gate_ws[g] = ("ACT", exp_hist[pp])
                    if g == qb:
                        r.emit("vector", "tensor_tensor", pt, pt,
                               mask_sb[:, :], ALU.mult,
                               wait=(("ACT", exp_hist[pp]),),
                               inc=("DVE", 1))
                        gate_ws[g] = ("DVE", r.v("DVE"))
                    if g > 0:
                        pvmm_hist[qpp[g - 1]] = emit_spv(g - 1, qpp[g - 1])
                pvmm_hist[qpp[nq - 1]] = emit_spv(nq - 1, qpp[nq - 1])
                pvd = pvmm_hist[qpp[nq - 1]]
                qp = qb % 2
                lnt = PF[:, 512 * qp:512 * qp + 512]
                rec = PF[:, 1024 + 512 * qp:1024 + 512 * qp + 512]
                w = [("PE", pvd), ("DVE", ao_hist[qp])]
                r.emit("scalar", "activation", lnt, PSF[:, 2560:3072],
                       AF.Ln, wait=tuple(w), inc=("ACT", 1))
                sum_bank_free = r.v("ACT")
                r.emit("scalar", "activation", rec, lnt, AF.Exp,
                       scale=-1.0, inc=("ACT", 1))
                recd = r.v("ACT")
                w = [("ACT", recd), ("PE", pvd)]
                if first_ao and u >= 2:
                    w.append(("PE", pe_op_end[0]))
                first_ao = False
                r.emit("vector", "tensor_tensor",
                       ao[:, qb * 512:(qb + 1) * 512],
                       PSF[:, 2048:2560], rec, ALU.mult,
                       wait=tuple(w), inc=("DVE", 1))
                ao_hist[qp] = r.v("DVE")
                pv_bank_free = ao_hist[qp]
            pe_unit_end[u] = r.v("PE")
            if u + 2 < 4:
                att_loads(u + 2, extra_wait=(("PE", pe_unit_end[u]),))
            if hl == 1:
                for _tt in range(64):
                    emit_op_tt(b, _tt)
                pe_op_end[b] = r.v("PE")

        r.emit("gpsimd", "collective_compute", "ReduceScatter", ALU.add,
               replica_groups=[list(range(NCORES))],
               ins=[spart.ap().opt()], outs=[s_own.ap().opt()],
               wait=(("SC", r.v("SC")),), inc=("CC2", 1))

        att_pe_end = r.v("PE")
        att_dve_end = r.v("DVE")

        hdn0 = 16384
        pe_gu = [0] * NIT
        act_silu = 0
        act_upev = 0
        dve_hdn = [0] * NIT
        pe_down_ht = [0] * 16
        pe_tr_g = [0, 0]
        dve_gtev = [0, 0]
        act_gt = [0, 0]
        sc_gp = [0, 0]
        pe_down_end = 0
        pe_gu_end = 0
        for th in range(2):
            w = []
            if th == 1:
                w.append(("PE", pe_down_end))
            else:
                w.append(("PE", pe_unit_end[2]))
            agt = ag1inA if th == 0 else ag1inB
            if th == 0:
                srcA = dap(agt, 0,
                           [(RPC // 2, 128), (128 * RPC // 2, 8), (1, 1024)])
                srcB = dap(agt, 128 * RPC // 2 * 8,
                           [(RPC // 2, 128), (128 * RPC // 2, 8), (1, 1024)])
                r.dma("sync", PBF[:, 0:8192], srcA, wait=list(w))
                xnh_ld = r.v("SQ")
                r.dma("scalar", PBF[:, 8192:16384], srcB, wait=list(w))
                xnh_ld2 = ("SC", r.v("SC"))
            else:
                src = dap(agt, 0,
                          [(RPC // 2, 128), (128 * RPC // 2, 16), (1, 1024)])
                r.dma("sync", PBF[:, 0:16384], src, wait=w)
                xnh_ld = r.v("SQ")
                xnh_ld2 = None
            act_floor = r.v("ACT")
            for it in range(NIT):
                gpar = [2, 1, 0][it % 3]
                gw0 = 61440 + gpar * 4096
                if it >= 3:
                    w = [("PE", pe_gu[it - 3])]
                elif th == 1 and it == 0:
                    w = [("PE", pe_gu[42]), ("PE", pe_tr_g[0]),
                         ("PE", pe_tr_g[1]), ("SC", sc_gp[0]),
                         ("SC", sc_gp[1])]
                elif th == 1 and it == 1:
                    w = [("PE", pe_gu[43])]
                elif th == 1:
                    w = [("PE", pe_gu[41])]
                elif it == 0:
                    w = [("PE", att_pe_end), ("SC", sc_sp_hist[1][30]),
                         ("GQ", gqcast)]
                elif it == 1:
                    w = [("PE", pe_unit_end[3])]
                else:
                    w = [("PE", att_pe_end)]
                r.dma("sync", PBF[:, gw0:gw0 + 2048], gwb.ap()[it],
                      wait=list(w))
                if th == 0:
                    r.dma("scalar", PBF[:, gw0 + 2048:gw0 + 4096],
                          uwb.ap()[it], wait=list(w))
                    uld = ("SC", r.v("SC"))
                else:
                    r.dma("sync", PBF[:, gw0 + 2048:gw0 + 4096], uwb.ap()[it])
                    uld = None
                gld = r.v("SQ")
                w = [("SQ", gld), ("SQ", xnh_ld),
                     ("ACT", act_silu if it else act_floor)]
                if xnh_ld2 is not None:
                    w.append(xnh_ld2)
                for tj in range(2):
                    for ht in range(16):
                        r.emit("tensor", "matmul", bank(tj),
                               PBF[:, gw0 + ht * 128:gw0 + ht * 128 + 128],
                               PBF[:, ht * 1024 + tj * 512:
                                   ht * 1024 + tj * 512 + 512],
                               start=(ht == 0), stop=(ht == 15),
                               wait=tuple(w) if (tj == 0 and ht == 0) else (),
                               inc=(("PE", 1) if (tj == 1 and ht == 15)
                                    else None))
                gmm = r.v("PE")
                w = ([("DVE", dve_hdn[it - 1])] if it
                     else [("ACT", act_floor)])
                if uld is not None:
                    w.append(uld)
                for tj in range(2):
                    for ht in range(16):
                        r.emit("tensor", "matmul", bank(2 + tj),
                               PBF[:, gw0 + 2048 + ht * 128:
                                   gw0 + 2048 + ht * 128 + 128],
                               PBF[:, ht * 1024 + tj * 512:
                                   ht * 1024 + tj * 512 + 512],
                               start=(ht == 0), stop=(ht == 15),
                               wait=tuple(w) if (tj == 0 and ht == 0) else (),
                               inc=(("PE", 1) if (tj == 1 and ht == 15)
                                    else None))
                umm = r.v("PE")
                pe_gu[it] = umm
                w = [("PE", gmm), ("DVE", dve_hdn[it - 1] if it
                                    else att_dve_end)]
                r.emit("scalar", "activation", PF[:, 0:1024], PSF[:, 0:1024],
                       AF.Silu, scale=gs2_sb[:, it:it + 1], wait=tuple(w),
                       inc=("ACT", 1))
                act_silu = r.v("ACT")
                w = [("PE", umm), ("ACT", act_silu)]
                if it == 0:
                    w.append(("PE", pe_down_end if th else att_pe_end))
                r.emit("vector", "scalar_tensor_tensor",
                       PBF[:, hdn0 + it * 1024:hdn0 + (it + 1) * 1024],
                       PSF[:, 1024:2048], us2_sb[:, it:it + 1],
                       PF[:, 0:1024], ALU.mult, ALU.mult,
                       wait=tuple(w), inc=("DVE", 1))
                dve_hdn[it] = r.v("DVE")
            pe_gu_end = r.v("PE")
            act_floor2 = r.v("ACT")
            for ht in range(16):
                dpar = ht % 2
                dof = dpar * 5632
                w = [("DVE", dve_hdn[NIT - 1])]
                if ht >= 2:
                    w.append(("PE", pe_down_ht[ht - 2]))
                else:
                    w.append(("PE", pe_gu_end))
                r.dma("sync", PBF[:, dof:dof + 5632], dwb.ap()[ht], wait=w)
                dld = r.v("SQ")
                w = [("SQ", dld), ("DVE", dve_hdn[NIT - 1]),
                     ("ACT", act_gt[dpar] if (th > 0 or ht >= 2)
                      else act_floor2)]
                for tj in range(2):
                    for it in range(NIT):
                        r.emit("tensor", "matmul",
                               PSF[:, dpar * 1024 + tj * 512:
                                   dpar * 1024 + tj * 512 + 512],
                               PBF[:, dof + it * 128:dof + it * 128 + 128],
                               PBF[:, hdn0 + it * 1024 + tj * 512:
                                   hdn0 + it * 1024 + tj * 512 + 512],
                               start=(it == 0), stop=(it == NIT - 1),
                               wait=tuple(w) if (tj == 0 and it == 0) else (),
                               inc=(("PE", 1) if (tj == 1 and it == NIT - 1)
                                    else None))
                dmm = r.v("PE")
                pe_down_ht[ht] = dmm
                gt = PBF[:, 69632 + dpar * 1024:69632 + (dpar + 1) * 1024]
                w = [("PE", dmm), ("PE", pe_tr_g[dpar])]
                r.emit("scalar", "activation", gt,
                       PSF[:, dpar * 1024:(dpar + 1) * 1024], AF.Silu,
                       scale=ds2_sb[:, ht:ht + 1], wait=tuple(w),
                       inc=("ACT", 1))
                act_gt[dpar] = r.v("ACT")
                ptb = (PTA, PTB)[dpar]
                w = [("ACT", act_gt[dpar]), ("DVE", dve_gtev[dpar])]
                for j in range(8):
                    r.emit("tensor", "transpose",
                           ptb[:, j * 128:(j + 1) * 128],
                           gt[:, j * 128:(j + 1) * 128], ident_sb[:, :],
                           wait=tuple(w) if j == 0 else (),
                           inc=(("PE", 1) if j == 7 else None))
                pe_tr_g[dpar] = r.v("PE")
                gst = PBF[:, 71680 + dpar * 1024:71680 + (dpar + 1) * 1024]
                w = [("PE", pe_tr_g[dpar]), ("SC", sc_gp[dpar])]
                r.emit("vector", "tensor_copy", gst, ptb[:, :],
                       wait=tuple(w), inc=("DVE", 1))
                dve_gtev[dpar] = r.v("DVE")
                dstg = dap(gsc, th * 1024 * E + ht * 128,
                           [(E, 128), (128 * E, 8), (1, 128)])
                r.dma("scalar", dstg, gst, wait=(("DVE", dve_gtev[dpar]),))
                sc_gp[dpar] = r.v("SC")
            pe_down_end = r.v("PE")
            if th == 0:
                pda_sc = r.v("SC")
                pda_dve = 0
                pda_st = 0
                for tt in range(8):
                    sv = PBF[:, 73728:75776]
                    gv = PBF[:, 75776:77824]
                    xv = PF[:, 2048:4096]
                    tv = PF[:, 4096:6144]
                    w = [("CC2", 1), ("SC", pda_sc),
                         ("SC", sc_sp_hist[1][31])]
                    if tt:
                        w = [("DVE", pda_dve), ("SC", pda_st)]
                    r.dma("sync", sv,
                          s_own.ap()[tt * 128:(tt + 1) * 128, :], wait=w)
                    r.dma("sync", gv, gsc.ap()[tt * 128:(tt + 1) * 128, :])
                    r.dma("sync", xv, x_own[tt * 128:(tt + 1) * 128, :])
                    ld = r.v("SQ")
                    r.emit("vector", "tensor_tensor", tv, sv, gv, ALU.mult,
                           wait=(("SQ", ld),))
                    r.emit("vector", "tensor_add", xv, xv, tv,
                           inc=("DVE", 1))
                    pda_dve = r.v("DVE")
                    r.dma("scalar", out_t[tt * 128:(tt + 1) * 128, :], xv,
                          wait=(("DVE", pda_dve),))
                    pda_st = r.v("SC")

        r.barrier()

        sc_out = [0] * 16
        dve_pd = [0] * 16
        for tt in range(8, 16):
            par = tt % 2
            sv = PBF[:, 4096 + par * 2048:4096 + (par + 1) * 2048]
            gv = PBF[:, par * 2048:(par + 1) * 2048]
            xv = PF[:, par * 2048:(par + 1) * 2048]
            tv = PF[:, 4096 + par * 2048:4096 + (par + 1) * 2048]
            w = [("CC2", 1)] if tt == 8 else []
            if tt >= 10:
                w.append(("DVE", dve_pd[tt - 2]))
                w.append(("SC", sc_out[tt - 2]))
            r.dma("sync", sv, s_own.ap()[tt * 128:(tt + 1) * 128, :], wait=w)
            r.dma("sync", gv, gsc.ap()[tt * 128:(tt + 1) * 128, :])
            r.dma("sync", xv, x_own[tt * 128:(tt + 1) * 128, :])
            ld = r.v("SQ")
            r.emit("vector", "tensor_tensor", tv, sv, gv, ALU.mult,
                   wait=(("SQ", ld),))
            r.emit("vector", "tensor_add", xv, xv, tv, inc=("DVE", 1))
            dve_pd[tt] = r.v("DVE")
            r.dma("scalar", out_t[tt * 128:(tt + 1) * 128, :], xv,
                  wait=(("DVE", dve_pd[tt]),))
            sc_out[tt] = r.v("SC")

        if dbg_outs:
            r.barrier()
            for nm, dap in dbg_outs.items():
                if nm in dbg_src:
                    r.dma("gpsimd", dap, dbg_src[nm].ap())

        r.barrier()

        r.finalize()

        def mk(ename):
            stream = r.streams[ename]

            def play(eng):
                for (op, args, kw, waits, inc, _pre) in stream:
                    for (s, vv) in waits:
                        eng.wait_ge(sems[s], vv)
                    if op is None:
                        continue
                    inst = getattr(eng, op)(*args, **kw)
                    if inc is not None:
                        inst.then_inc(sems[inc[0]], inc[1])
            return play

        block.sync(mk("sync"))
        block.scalar(mk("scalar"))
        block.vector(mk("vector"))
        block.tensor(mk("tensor"))
        block.gpsimd(mk("gpsimd"))

    return nc


def _host_consts():
    k = np.arange(64)
    inv = ROPE_BASE ** (-2.0 * k / D)
    didx = np.concatenate([k, k])
    ang = np.arange(T, dtype=np.float64)[None, :] * inv[didx][:, None]
    cos_t = np.cos(ang).astype(np.float32)
    sin_t = np.sin(ang).astype(np.float32)
    m = np.zeros((128, 2048), dtype=np.float32)
    p = np.arange(128)[:, None]
    c = np.arange(512)[None, :]
    for j in range(4):
        m[:, j * 512:(j + 1) * 512] = (c >= p + 128 * j).astype(np.float32)
    masks = m.astype(BF)
    P = np.zeros((128, 128), dtype=np.float32)
    P[np.arange(64), np.arange(64) + 64] = -1.0
    P[np.arange(64) + 64, np.arange(64)] = 1.0
    protT = np.ascontiguousarray(P.T).astype(BF)
    ident = np.eye(128, dtype=np.float32).astype(BF)
    ones_col = np.ones((128, 1), dtype=np.float32).astype(BF)
    ones128 = np.ones((128, 128), dtype=np.float32).astype(BF)
    ones_row_f = np.ones((1, 128), dtype=np.float32)
    return cos_t, sin_t, masks, protT, ident, ones_col, ones128, ones_row_f


def make_in_maps(L0_attn_x, L0_attn_wq, L0_attn_wk, L0_attn_wv, L0_attn_wo,
                 L0_ffn_gate_w, L0_ffn_up_w, L0_ffn_down_w,
                 L0_ffn_gate_s, L0_ffn_up_s, L0_ffn_down_s):
    f32 = np.float32
    x2 = np.asarray(L0_attn_x, f32).reshape(R, E)
    wq = np.asarray(L0_attn_wq, f32)
    wk = np.asarray(L0_attn_wk, f32)
    wv = np.asarray(L0_attn_wv, f32)
    wo = np.asarray(L0_attn_wo, f32)
    gw = np.asarray(L0_ffn_gate_w, f32)
    uw = np.asarray(L0_ffn_up_w, f32)
    dw = np.asarray(L0_ffn_down_w, f32)
    gs = np.asarray(L0_ffn_gate_s, f32)
    us = np.asarray(L0_ffn_up_s, f32)
    ds = np.asarray(L0_ffn_down_s, f32)

    gate_hh = np.ascontiguousarray(
        gw.reshape(NIT, 128, 16, 128).transpose(0, 3, 2, 1).reshape(
            NIT, 128, 2048))
    up_hh = np.ascontiguousarray(
        uw.reshape(NIT, 128, 16, 128).transpose(0, 3, 2, 1).reshape(
            NIT, 128, 2048))
    down_hh = np.ascontiguousarray(
        dw.reshape(16, 128, NIT, 128).transpose(0, 3, 2, 1).reshape(
            16, 128, NIT * 128))
    gate_s2 = np.ascontiguousarray(gs.reshape(NIT, 128).T)
    up_s2 = np.ascontiguousarray(us.reshape(NIT, 128).T)
    down_s2 = np.ascontiguousarray(ds.reshape(16, 128).T)
    (cos_t, sin_t, masks, protT, ident, ones_col, ones128,
     ones_row_f) = _host_consts()

    in_maps = []
    for r_ in range(NCORES):
        f0 = 256 * r_
        wq_hc = np.ascontiguousarray(
            wq[f0:f0 + 256, :].reshape(256, 16, 128).transpose(2, 1, 0)
            .reshape(128, 4096))
        wk_hc = np.ascontiguousarray(
            wk[f0:f0 + 256, :].reshape(256, 16, 128).transpose(2, 1, 0)
            .reshape(128, 4096))
        wv_hc = np.ascontiguousarray(
            wv[f0:f0 + 256, :].reshape(256, 16, 128).transpose(2, 1, 0)
            .reshape(128, 4096))
        wo_hc = np.ascontiguousarray(
            wo[:, f0:f0 + 256].T.reshape(2, 128, 2048).transpose(1, 0, 2)
            .reshape(128, 4096))
        in_maps.append({
            "x_own": np.ascontiguousarray(x2[RPC * r_:RPC * (r_ + 1), :]),
            "wq_h": wq_hc, "wk_h": wk_hc, "wv_h": wv_hc, "wo_h": wo_hc,
            "gate_h": gate_hh, "up_h": up_hh, "down_h": down_hh,
            "gate_s2": gate_s2, "up_s2": up_s2, "down_s2": down_s2,
            "cos_t": cos_t, "sin_t": sin_t, "masks": masks,
            "protT": protT, "ident": ident, "ones_col": ones_col,
            "ones128": ones128, "ones_row_f": ones_row_f,
        })
    return in_maps


def kernel(**inputs):
    nc = build_program()
    in_maps = make_in_maps(**inputs)
    res = run_bass_kernel_spmd(nc, in_maps, core_ids=list(range(NCORES)))
    out = np.concatenate([res.results[i]["out"] for i in range(NCORES)],
                         axis=0)
    return out.reshape(B, T, E)
